# version 26
# baseline (speedup 1.0000x reference)
import numpy as np
import concourse.bass as bass
import concourse.mybir as mybir
from concourse.bass_utils import run_bass_kernel_spmd

F32 = mybir.dt.float32
BF16 = mybir.dt.bfloat16
AF = mybir.ActivationFunctionType
ALU = mybir.AluOpType
AX = mybir.AxisListType

T = 512
NT = 8
KC = 8
EPS = 1e-5
N_CORES = 8
NW = 24
UPT = 25
NSLOT = 3
INV7 = 1.0 / 7.0
INV_D = 1.0 / 1024.0

ENGS = ("tensor", "vector", "scalar", "sync", "gpsimd")
SEM_OF = {"tensor": "ST", "vector": "SV", "scalar": "SS"}


class Plan:
    def __init__(self):
        self.ops = {e: [] for e in ENGS}
        self.cnt = {"ST": 0, "SV": 0, "SS": 0, "SW": 0, "SX": 0, "SO": 0, "SG": 0, "SD": 0}
        self.last_wait = {}

    def wait(self, eng, sem, val):
        if not val or val <= 0:
            return
        key = (eng, sem)
        if self.last_wait.get(key, 0) >= val:
            return
        self.last_wait[key] = val
        self.ops[eng].append(("w", sem, val))

    def emit(self, eng, fn, inc=None, n=1):
        self.ops[eng].append(("o", fn, inc, n))
        if inc is not None:
            self.cnt[inc] += n
            return self.cnt[inc]
        return None

    def drain(self, eng):
        self.ops[eng].append(("d",))

    def barrier(self):
        for a in ("tensor", "vector", "scalar"):
            for b in ("tensor", "vector", "scalar"):
                if a != b:
                    self.wait(b, SEM_OF[a], self.cnt[SEM_OF[a]])


def build_nc(dbg=False):
    nc = bass.Bass("TRN2", target_bir_lowering=False)

    dbg_t = nc.dram_tensor("dbg", [24, 128, 4096], F32, kind="ExternalOutput").ap() if dbg else None
    dbg_names = []

    x_tl = nc.dram_tensor("x_tl", [NT, 128, KC * T], F32, kind="ExternalInput").ap()
    hstream = nc.dram_tensor("hstream", [NW, 128, 4096], F32, kind="ExternalInput").ap()
    wk_blk = nc.dram_tensor("wk_blk", [128, 8192], F32, kind="ExternalInput").ap()
    gate_b_t = nc.dram_tensor("gate_b_t", [128, 16], F32, kind="ExternalInput").ap()
    state_b_t = nc.dram_tensor("state_b_t", [128, 8], F32, kind="ExternalInput").ap()
    out_b_t = nc.dram_tensor("out_b_t", [128, 8], F32, kind="ExternalInput").ap()
    mlp_b1_t = nc.dram_tensor("mlp_b1_t", [128, 32], F32, kind="ExternalInput").ap()
    mlp_b2_t = nc.dram_tensor("mlp_b2_t", [128, 8], F32, kind="ExternalInput").ap()
    ffn_b_t = nc.dram_tensor("ffn_b_t", [128, 8], F32, kind="ExternalInput").ap()
    n1w_t = nc.dram_tensor("n1w_t", [128, 8], F32, kind="ExternalInput").ap()
    n1b_t = nc.dram_tensor("n1b_t", [128, 8], F32, kind="ExternalInput").ap()
    n2w_t = nc.dram_tensor("n2w_t", [128, 8], F32, kind="ExternalInput").ap()
    n2b_t = nc.dram_tensor("n2b_t", [128, 8], F32, kind="ExternalInput").ap()
    a_t = nc.dram_tensor("a_t", [128, 128], F32, kind="ExternalInput").ap()
    b_t = nc.dram_tensor("b_t", [16, 1024], F32, kind="ExternalInput").ap()
    wq_dram = nc.dram_tensor("wq_dram", [128, 8192], BF16, kind="Internal").ap()
    wsp_hi = nc.dram_tensor("wsp_hi", [NW, 128, 4096], BF16, kind="Internal").ap()
    wsp_lo = nc.dram_tensor("wsp_lo", [NW, 128, 4096], BF16, kind="Internal").ap()
    out_tl = nc.dram_tensor("out_tl", [NT, 128, KC * T], F32, kind="ExternalOutput").ap()

    from contextlib import ExitStack
    with ExitStack() as st:
        e = st.enter_context
        p0 = e(nc.sbuf_tensor("p0", [128, KC, T], F32))
        p1 = e(nc.sbuf_tensor("p1", [128, KC, T], F32))
        p2 = e(nc.sbuf_tensor("p2", [128, KC, T], F32))
        p3 = e(nc.sbuf_tensor("p3", [128, KC, T], F32))
        acc = e(nc.sbuf_tensor("acc", [128, KC, T], F32))
        q0h = e(nc.sbuf_tensor("q0h", [128, KC, T], BF16))
        q0l = e(nc.sbuf_tensor("q0l", [128, KC, T], BF16))
        q1h = e(nc.sbuf_tensor("q1h", [128, KC, T], BF16))
        q1l = e(nc.sbuf_tensor("q1l", [128, KC, T], BF16))
        mdh = e(nc.sbuf_tensor("mdh", [128, KC, T], BF16))
        mdl = e(nc.sbuf_tensor("mdl", [128, KC, T], BF16))
        slots = [e(nc.sbuf_tensor(f"slot{k}", [128, 8192], BF16)) for k in range(NSLOT)]
        b0 = e(nc.sbuf_tensor("b0", [128, T], F32))
        b1 = e(nc.sbuf_tensor("b1", [128, T], F32))
        r0 = e(nc.sbuf_tensor("r0", [1, T], F32))
        r1 = e(nc.sbuf_tensor("r1", [1, T], F32))
        r2 = e(nc.sbuf_tensor("r2", [1, T], F32))
        r3 = e(nc.sbuf_tensor("r3", [1, T], F32))
        srow = e(nc.sbuf_tensor("srow", [1, 128], F32))
        ones_row = e(nc.sbuf_tensor("ones_row", [1, 128], F32))
        ones_col = e(nc.sbuf_tensor("ones_col", [128, 1], F32))
        eps_s = e(nc.sbuf_tensor("eps_s", [1, 1], F32))
        sbm_s = e(nc.sbuf_tensor("sbm_s", [128, 1], F32))
        am_col = e(nc.sbuf_tensor("am_col", [128, 1], F32))
        scale_col = e(nc.sbuf_tensor("scale_col", [128, 1], F32))
        inv_col = e(nc.sbuf_tensor("inv_col", [128, 1], F32))
        carry = e(nc.sbuf_tensor("carry", [128, 8], F32))
        r_s = e(nc.sbuf_tensor("r_s", [16, T], F32))
        gate_b_s = e(nc.sbuf_tensor("gate_b_s", [128, 16], F32))
        state_b_s = e(nc.sbuf_tensor("state_b_s", [128, 8], F32))
        out_b_s = e(nc.sbuf_tensor("out_b_s", [128, 8], F32))
        mlp_b1_s = e(nc.sbuf_tensor("mlp_b1_s", [128, 32], F32))
        b2_s = e(nc.sbuf_tensor("b2_s", [128, 8], F32))
        fb_s = e(nc.sbuf_tensor("fb_s", [128, 8], F32))
        cb_s = e(nc.sbuf_tensor("cb_s", [128, 8], F32))
        n1w_s = e(nc.sbuf_tensor("n1w_s", [128, 8], F32))
        n1b_s = e(nc.sbuf_tensor("n1b_s", [128, 8], F32))
        n2w_s = e(nc.sbuf_tensor("n2w_s", [128, 8], F32))
        n2b_s = e(nc.sbuf_tensor("n2b_s", [128, 8], F32))
        a_s = e(nc.sbuf_tensor("a_s", [128, 8, 16], F32))
        bt_s = e(nc.sbuf_tensor("bt_s", [16, 8, 128], F32))
        swm_s = e(nc.sbuf_tensor("swm_s", [128, 8], F32))
        swp_s = e(nc.sbuf_tensor("swp_s", [128, 8], F32))
        swp2_s = e(nc.sbuf_tensor("swp2_s", [128, 8], F32))
        gstg = e(nc.sbuf_tensor("gstg", [128, 2048], F32))
        psA = e(nc.psum_tensor("psA", [128, T], F32))
        psB = e(nc.psum_tensor("psB", [128, T], F32))
        psR0 = e(nc.psum_tensor("psR0", [128, T], F32))
        psR1 = e(nc.psum_tensor("psR1", [128, T], F32))
        SW = e(nc.semaphore("SW"))
        SX = e(nc.semaphore("SX"))
        SO = e(nc.semaphore("SO"))
        ST = e(nc.semaphore("ST"))
        SV = e(nc.semaphore("SV"))
        SS = e(nc.semaphore("SS"))
        SG = e(nc.semaphore("SG"))
        SD = e(nc.semaphore("SD"))
        sems = {"SW": SW, "SX": SX, "SO": SO, "ST": ST, "SV": SV, "SS": SS,
                "SG": SG, "SD": SD}

        banks = {"A": psA, "B": psB, "R0": psR0, "R1": psR1}
        P = Plan()
        bank_gate = {k: [] for k in banks}
        unit_tok = {}
        unit_dma_cum = {}
        split_out_tok = {}
        split_out_so_tok = {}
        wq_out_tok_box = [0]
        gsplit_out_tok_box = [0]

        def dump(name, src_ap, gates, blockers=()):
            if not dbg:
                return
            k = len(dbg_names)
            dbg_names.append(name)
            for sem, tok in gates:
                P.wait("gpsimd", sem, tok)
            r, c = src_ap.shape[0], src_ap.shape[1]
            P.emit("gpsimd", lambda g, k=k, src_ap=src_ap, r=r, c=c:
                   g.dma_start(out=dbg_t[k][0:r, 0:c], in_=src_ap,
                               allow_slow_non_contiguous=True), inc="SO", n=16)
            tok = P.cnt["SO"]
            for eng in blockers:
                P.wait(eng, "SO", tok)

        def bank_wait(bk):
            for sem, tok in bank_gate[bk]:
                P.wait("tensor", sem, tok)
            bank_gate[bk] = []

        def mm_group(bk, mm_list, extra_gate=None, start=True, stop=True):
            if start:
                bank_wait(bk)
            if extra_gate:
                for sem, tok in extra_gate:
                    P.wait("tensor", sem, tok)
            nmm = len(mm_list)
            for idx, (lf, rf) in enumerate(mm_list):
                last = idx == nmm - 1
                P.emit("tensor",
                       lambda t, lf=lf, rf=rf, bk=bk, s=(start and idx == 0), p=(stop and last):
                       t.matmul(banks[bk][:], lf(), rf(), start=s, stop=p),
                       inc="ST" if (last and stop) else None)
            return P.cnt["ST"]

        def evac(eng, bk, fn, gate_tok, extra=()):
            P.wait(eng, "ST", gate_tok)
            for sem, tok in extra:
                P.wait(eng, sem, tok)
            t = P.emit(eng, fn, inc=SEM_OF[eng])
            bank_gate[bk].append((SEM_OF[eng], t))
            return t

        def load_unit(g):
            tile, j = divmod(g, UPT)
            slot = slots[g % NSLOT]
            P.wait("sync", "ST", unit_tok.get(g - NSLOT, 0))
            if tile == 0:
                if j == 24:
                    P.wait("sync", "SD", wq_out_tok_box[0])
                elif j in split_out_so_tok:
                    P.wait("sync", "SO", split_out_so_tok[j])
                else:
                    P.wait("sync", "SD", split_out_tok.get(j, 0))
            if j == 24:
                P.emit("sync", lambda s, slot=slot: s.dma_start(out=slot[:], in_=wq_dram[:]),
                       inc="SW", n=16)
            else:
                P.emit("sync", lambda s, j=j, slot=slot:
                       s.dma_start(out=slot[:, 0:4096], in_=wsp_hi[j]), inc="SW", n=16)
                P.emit("sync", lambda s, j=j, slot=slot:
                       s.dma_start(out=slot[:, 4096:8192], in_=wsp_lo[j]), inc="SW", n=16)
            unit_dma_cum[g] = P.cnt["SW"]

        def unit_views(tile, j):
            slot = slots[(tile * UPT + j) % NSLOT]
            if j == 24:
                return slot[:].rearrange("p (a b) -> p a b", b=1024), None
            hi = slot[:, 0:4096].rearrange("p (a b) -> p a b", b=512)
            lo = slot[:, 4096:8192].rearrange("p (a b) -> p a b", b=512)
            return hi, lo

        def use_unit(tile, j):
            P.wait("tensor", "SW", unit_dma_cum[tile * UPT + j])

        def done_unit(tile, j):
            g = tile * UPT + j
            unit_tok[g] = P.cnt["ST"]
            if g + NSLOT < NT * UPT:
                load_unit(g + NSLOT)

        def mm3(uvh, uvl, fsl, rh, rl):
            mms = []
            for kc in range(KC):
                wh = (lambda uvh=uvh, kc=kc, fsl=fsl: uvh[:, kc, fsl[0]:fsl[1]])
                wl = (lambda uvl=uvl, kc=kc, fsl=fsl: uvl[:, kc, fsl[0]:fsl[1]])
                ah = (lambda rh=rh, kc=kc: rh[:, kc, :])
                al = (lambda rl=rl, kc=kc: rl[:, kc, :])
                mms += [(wh, ah), (wh, al), (wl, ah)]
            return mms

        P.emit("vector", lambda v: v.memset(ones_row[:], 1.0))
        P.emit("vector", lambda v: v.memset(ones_col[:], 1.0))
        P.emit("vector", lambda v: v.memset(eps_s[:], EPS), inc="SV")

        const_pairs = [
            (gate_b_t, gate_b_s), (state_b_t, state_b_s), (out_b_t, out_b_s),
            (mlp_b1_t, mlp_b1_s), (mlp_b2_t, b2_s), (ffn_b_t, fb_s),
            (n1w_t, n1w_s), (n1b_t, n1b_s), (n2w_t, n2w_s), (n2b_t, n2b_s),
        ]
        for src, dst in const_pairs:
            P.emit("sync", lambda s, src=src, dst=dst:
                   s.dma_start(out=dst[:], in_=src[:]), inc="SX", n=16)
        P.emit("sync", lambda s: s.dma_start(
            out=a_s[:], in_=a_t.rearrange("p (a b) -> p a b", b=16)), inc="SX", n=16)
        P.emit("sync", lambda s: s.dma_start(
            out=bt_s[:], in_=b_t.rearrange("p (a b) -> p a b", b=128)), inc="SX", n=16)
        P.emit("sync", lambda s: s.dma_start(
            out=p3[:], in_=x_tl[0].rearrange("p (a b) -> p a b", b=T)), inc="SX", n=16)
        consts_tok = P.cnt["SX"]
        pva = p2[:].rearrange("p a b -> p (a b)")
        pvb = acc[:].rearrange("p a b -> p (a b)")
        P.emit("sync", lambda s: s.dma_start(out=pva[:, 0:4096], in_=wk_blk[:, 0:4096]), inc="SW", n=16)
        P.emit("sync", lambda s: s.dma_start(out=pvb[:, 0:4096], in_=wk_blk[:, 4096:8192]), inc="SW", n=16)
        wkin0_tok = P.cnt["SW"]

        P.wait("vector", "SX", consts_tok)
        P.emit("vector", lambda v: v.tensor_add(cb_s[:], b2_s[:], fb_s[:]), inc="SV")
        P.emit("vector", lambda v: v.reduce_sum(am_col[:, 0:1], state_b_s[:], axis=AX.X), inc="SV")
        sbsum_tok = P.cnt["SV"]
        P.wait("sync", "SV", sbsum_tok)
        P.emit("sync", lambda s: s.dma_start(out=srow[0:1, :], in_=am_col[:, 0:1]), inc="SX", n=16)
        sbrow_tok = P.cnt["SX"]
        P.wait("tensor", "SX", sbrow_tok)
        P.emit("tensor", lambda t: t.matmul(psR0[:, 0:128], ones_row[0:1, :], srow[0:1, :],
                                            start=True, stop=True), inc="ST")
        sb_mm_tok = P.cnt["ST"]
        P.wait("vector", "ST", sb_mm_tok)
        P.emit("vector", lambda v: v.reduce_sum(sbm_s[:, 0:1], psR0[:, 0:128], axis=AX.X))
        P.drain("vector")
        P.emit("vector", lambda v: v.tensor_scalar(sbm_s[:, 0:1], sbm_s[:, 0:1], -INV_D, None, ALU.mult), inc="SV")
        bank_gate["R0"].append(("SV", P.cnt["SV"]))

        P.wait("vector", "SW", wkin0_tok)
        P.emit("vector", lambda v: v.reduce_max(am_col[:, 0:1], pva[:, 0:4096], axis=AX.X,
                                                apply_absolute_value=True))
        P.drain("vector")
        P.emit("vector", lambda v: v.reduce_max(scale_col[:, 0:1], pvb[:, 0:4096], axis=AX.X,
                                                apply_absolute_value=True))
        P.drain("vector")
        P.emit("vector", lambda v: v.tensor_max(am_col[:, 0:1], am_col[:, 0:1], scale_col[:, 0:1]), inc="SV")
        amax_tok = P.cnt["SV"]
        P.wait("sync", "SV", amax_tok)
        P.wait("sync", "ST", sb_mm_tok)
        P.emit("sync", lambda s: s.dma_start(out=srow[0:1, :], in_=am_col[:, 0:1]), inc="SX", n=16)
        row_tok = P.cnt["SX"]
        bank_wait("R0")
        P.wait("tensor", "SX", row_tok)
        P.emit("tensor", lambda t: t.matmul(psR0[:, 0:128], ones_row[0:1, :], srow[0:1, :],
                                            start=True, stop=True), inc="ST")
        amb_tok = P.cnt["ST"]
        P.wait("vector", "ST", amb_tok)
        P.emit("vector", lambda v: v.reduce_max(am_col[:, 0:1], psR0[:, 0:128], axis=AX.X))
        P.drain("vector")
        P.emit("vector", lambda v: v.tensor_scalar(scale_col[:, 0:1], am_col[:, 0:1], INV7, None, ALU.mult))
        P.drain("vector")
        P.emit("vector", lambda v: v.reciprocal(inv_col[:, 0:1], scale_col[:, 0:1]), inc="SV")
        scaleinv_tok = P.cnt["SV"]
        bank_gate["R0"].append(("SV", scaleinv_tok))

        split_in_tok, split_sg_tok, split_out_tok_l = {}, {}, {}
        NV_SPLIT = 6

        def vsplit_stage_in(u):
            stg = (p2 if u % 2 == 0 else acc)[:].rearrange("p a b -> p (a b)")
            P.wait("scalar", "SV", amax_tok)
            P.emit("scalar", lambda s, u=u, stg=stg: s.dma_start(out=stg[:, 0:4096], in_=hstream[u]),
                   inc="SD", n=16)
            split_in_tok[u] = P.cnt["SD"]

        vsplit_stage_in(0)
        vsplit_stage_in(1)
        hi_v = q1h[:].rearrange("p a b -> p (a b)")
        lo_v = q1l[:].rearrange("p a b -> p (a b)")
        for u in range(NV_SPLIT):
            stg = (p2 if u % 2 == 0 else acc)[:].rearrange("p a b -> p (a b)")
            P.wait("vector", "SD", split_in_tok[u])
            if u >= 1:
                P.wait("vector", "SD", split_out_tok_l[u - 1])
            P.emit("vector", lambda v, stg=stg: v.tensor_copy(hi_v[:, 0:4096], stg[:, 0:4096]))
            P.emit("vector", lambda v, stg=stg: v.tensor_sub(lo_v[:, 0:4096], stg[:, 0:4096], hi_v[:, 0:4096]), inc="SV")
            vtok = P.cnt["SV"]
            P.wait("scalar", "SV", vtok)
            P.emit("scalar", lambda s, u=u: s.dma_start(out=wsp_hi[u], in_=hi_v[:, 0:4096]), inc="SD", n=16)
            P.emit("scalar", lambda s, u=u: s.dma_start(out=wsp_lo[u], in_=lo_v[:, 0:4096]), inc="SD", n=16)
            split_out_tok[u] = P.cnt["SD"]
            split_out_tok_l[u] = P.cnt["SD"]
            if u + 2 < NV_SPLIT:
                vsplit_stage_in(u + 2)
        vsplit_done_sv = P.cnt["SV"]
        q1_free_tok = split_out_tok[NV_SPLIT - 1]

        hi_g = mdh[:].rearrange("p a b -> p (a b)")
        lo_g = mdl[:].rearrange("p a b -> p (a b)")
        for u in range(NV_SPLIT, NW):
            for hf in range(2):
                o0, o1 = hf * 2048, (hf + 1) * 2048
                P.emit("gpsimd", lambda g, u=u, o0=o0, o1=o1: g.dma_start(
                    out=gstg[:, 0:2048], in_=hstream[u][:, o0:o1]), inc="SO", n=16)
                P.wait("gpsimd", "SO", P.cnt["SO"])
                P.emit("gpsimd", lambda g, o0=o0, o1=o1: g.tensor_copy(hi_g[:, o0:o1], gstg[:, 0:2048]))
                P.drain("gpsimd")
                P.emit("gpsimd", lambda g, o0=o0, o1=o1: g.tensor_sub(lo_g[:, o0:o1], gstg[:, 0:2048], hi_g[:, o0:o1]))
                P.drain("gpsimd")
            P.emit("gpsimd", lambda g, u=u: g.dma_start(out=wsp_hi[u], in_=hi_g[:, 0:4096]), inc="SO", n=16)
            P.emit("gpsimd", lambda g, u=u: g.dma_start(out=wsp_lo[u], in_=lo_g[:, 0:4096]), inc="SO", n=16)
            P.wait("gpsimd", "SO", P.cnt["SO"])
            split_out_so_tok[u] = P.cnt["SO"]
        gsplit_out_tok_box[0] = P.cnt["SO"]

        restage_gate_sv = [vsplit_done_sv, 0]

        load_unit(0)
        load_unit(1)
        load_unit(2)

        def ln_stats(src, s1_extra=(), sq_extra=(), scr_fn=None):
            bank_wait("R0")
            for sem, tok in s1_extra:
                P.wait("tensor", sem, tok)
            for kc in range(KC):
                P.emit("tensor", lambda t, kc=kc, src=src, s=(kc == 0), p=(kc == KC - 1):
                       t.matmul(psR0[0:1, :], ones_col[:], src[:, kc, :], start=s, stop=p),
                       inc="ST" if kc == KC - 1 else None)
            tokS1 = P.cnt["ST"]
            bank_wait("R1")
            sq_mm = []
            for kc in range(KC):
                scr = scr_fn(kc) if scr_fn else (b0 if kc % 2 == 0 else b1)[:]
                if scr_fn is None and kc >= 2:
                    P.wait("scalar", "ST", sq_mm[kc - 2])
                for sem, tok in sq_extra:
                    P.wait("scalar", sem, tok)
                stok = P.emit("scalar", lambda s, kc=kc, src=src, scr=scr:
                              s.activation(scr, src[:, kc, :], AF.Square), inc="SS")
                P.wait("tensor", "SS", stok)
                P.emit("tensor", lambda t, scr=scr, s=(kc == 0), p=(kc == KC - 1):
                       t.matmul(psR1[0:1, :], ones_col[:], scr, start=s, stop=p), inc="ST")
                sq_mm.append(P.cnt["ST"])
            return tokS1, P.cnt["ST"]

        def row_musig(tokS1, tokSQ):
            P.wait("vector", "ST", tokS1)
            P.emit("vector", lambda v: v.tensor_scalar(r0[0:1, :], psR0[0:1, :], INV_D, None, ALU.mult), inc="SV")
            bank_gate["R0"].append(("SV", P.cnt["SV"]))
            P.wait("vector", "ST", tokSQ)
            P.emit("vector", lambda v: v.tensor_scalar(r1[0:1, :], psR1[0:1, :], INV_D, None, ALU.mult), inc="SV")
            bank_gate["R1"].append(("SV", P.cnt["SV"]))
            P.emit("vector", lambda v: v.tensor_mul(r3[0:1, :], r0[0:1, :], r0[0:1, :]))
            P.emit("vector", lambda v: v.tensor_sub(r1[0:1, :], r1[0:1, :], r3[0:1, :]), inc="SV")
            var_tok = P.cnt["SV"]
            P.wait("scalar", "SV", var_tok)
            sq_tok = P.emit("scalar", lambda s: s.activation(
                r2[0:1, :], r1[0:1, :], AF.Sqrt, bias=eps_s[0:1, 0:1], scale=1.0), inc="SS")
            P.wait("vector", "SS", sq_tok)
            P.emit("vector", lambda v: v.reciprocal(r2[0:1, :], r2[0:1, :]), inc="SV")
            return P.cnt["SV"]

        def bcast(row_ap, bk, gates):
            bank_wait(bk)
            for sem, tok in gates:
                P.wait("tensor", sem, tok)
            P.emit("tensor", lambda t, row_ap=row_ap, bk=bk:
                   t.matmul(banks[bk][:], ones_row[0:1, :], row_ap, start=True, stop=True), inc="ST")
            return P.cnt["ST"]

        def split_pair(src_chunk_fn, dsth, dstl, idxs):
            for i2 in idxs:
                src = src_chunk_fn(i2)
                P.emit("vector", lambda v, src=src, dsth=dsth, i2=i2: v.tensor_copy(dsth[:, i2, :], src))
                P.emit("vector", lambda v, src=src, dsth=dsth, dstl=dstl, i2=i2:
                       v.tensor_sub(dstl[:, i2, :], src, dsth[:, i2, :]),
                       inc="SV" if i2 == idxs[-1] else None)
            return P.cnt["SV"]

        def quantize_half(half):
            stg = acc[:].rearrange("p a b -> p (a b)")
            P.wait("sync", "SV", restage_gate_sv[half])
            P.emit("sync", lambda s, half=half: s.dma_start(
                out=stg[:, 0:4096], in_=wk_blk[:, half * 4096:(half + 1) * 4096]), inc="SW", n=16)
            ld = P.cnt["SW"]
            qa2 = p0[:].rearrange("p a b -> p (a b)")
            qo = q0l[:].rearrange("p a b -> p (a b)")
            P.wait("vector", "SW", ld)
            P.emit("vector", lambda v: v.tensor_scalar(stg[:, 0:4096], stg[:, 0:4096], inv_col[:, 0:1], None, ALU.mult))
            P.emit("vector", lambda v: v.tensor_scalar(qa2[:, 0:4096], stg[:, 0:4096], -7.5, None, ALU.is_ge))
            for k in range(-6, 8):
                P.emit("vector", lambda v, thr=k - 0.5:
                       v.scalar_tensor_tensor(qa2[:, 0:4096], stg[:, 0:4096], thr, qa2[:, 0:4096], ALU.is_ge, ALU.add))
            P.emit("vector", lambda v: v.tensor_scalar(qo[:, 0:4096], qa2[:, 0:4096], -8.0, None, ALU.add), inc="SV")
            qt = P.cnt["SV"]
            restage_gate_sv[half + 1 if half == 0 else 0] = qt if half == 0 else restage_gate_sv[0]
            if half == 0:
                restage_gate_sv[1] = qt
            P.wait("scalar", "SV", qt)
            P.emit("scalar", lambda s, half=half: s.dma_start(
                out=wq_dram[:, half * 4096:(half + 1) * 4096], in_=qo[:, 0:4096]), inc="SD", n=16)
            if half == 1:
                wq_out_tok_box[0] = P.cnt["SD"]
            return P.cnt["SD"]

        x_tok = {0: consts_tok}
        prev_out_so = 0
        prev_ffn_tok = 0
        prev_mlp1_tok = 0
        prev_xns_tok = 0
        prev_tokr = 0
        prev_hp_sv = 0
        pend_sq, pend_stats, pend_xn2p = {}, {}, {}

        def pairs_of(ti):
            return ((q0h, q0l, q1h, q1l) if ti % 2 == 0
                    else (q1h, q1l, q0h, q0l))

        def emit_p0_squares(ti):
            P.wait("scalar", "SX", x_tok[ti])
            P.wait("scalar", "SV", prev_hp_sv)
            if ti == 1:
                P.wait("scalar", "SD", wq_out_tok_box[0])
            toks = []
            for kc in range(KC):
                P.emit("scalar", lambda s, kc=kc: s.activation(
                    p0[:, kc, :], p3[:, kc, :], AF.Square), inc="SS")
                toks.append(P.cnt["SS"])
            pend_sq[ti] = toks

        def emit_p0_stats(ti):
            P.wait("tensor", "SX", x_tok[ti])
            bank_wait("R0")
            for kc in range(KC):
                P.emit("tensor", lambda t, kc=kc, s=(kc == 0), p=(kc == KC - 1):
                       t.matmul(psR0[0:1, :], ones_col[:], p3[:, kc, :], start=s, stop=p),
                       inc="ST" if kc == KC - 1 else None)
            tokS1 = P.cnt["ST"]
            bank_wait("R1")
            for kc in range(KC):
                P.wait("tensor", "SS", pend_sq[ti][kc])
                P.emit("tensor", lambda t, kc=kc, s=(kc == 0), p=(kc == KC - 1):
                       t.matmul(psR1[0:1, :], ones_col[:], p0[:, kc, :], start=s, stop=p),
                       inc="ST" if kc == KC - 1 else None)
            tokSQ = P.cnt["ST"]
            rstd_tok = row_musig(tokS1, tokSQ)
            P.emit("vector", lambda v: v.tensor_mul(r1[0:1, :], r1[0:1, :], r2[0:1, :]))
            P.emit("vector", lambda v: v.tensor_mul(r1[0:1, :], r1[0:1, :], r2[0:1, :]), inc="SV")
            v1_tok = P.cnt["SV"]
            P.wait("scalar", "SV", v1_tok)
            sq1_tok = P.emit("scalar", lambda s: s.activation(
                r1[0:1, :], r1[0:1, :], AF.Sqrt, bias=eps_s[0:1, 0:1], scale=1.0), inc="SS")
            P.wait("vector", "SS", sq1_tok)
            P.emit("vector", lambda v: v.reciprocal(r1[0:1, :], r1[0:1, :]))
            P.drain("vector")
            P.emit("vector", lambda v: v.tensor_mul(r3[0:1, :], r2[0:1, :], r1[0:1, :]), inc="SV")
            pend_stats[ti] = (rstd_tok, P.cnt["SV"])

        pend_bc = {}

        def emit_p1_bcast(ti):
            rstd_tok, r2c_tok = pend_stats[ti]
            pend_bc[ti] = {"mu": bcast(r0[0:1, :], "R0", [("SV", rstd_tok)]),
                           "r2c": bcast(r3[0:1, :], "R1", [("SV", r2c_tok)]),
                           "rstd_tok": rstd_tok}

        def p1_weave(ti, fc):
            qAh, qAl, _, _ = pairs_of(ti)
            bc = pend_bc[ti]
            if fc == 0:
                P.wait("vector", "ST", bc["mu"])
                P.wait("vector", "ST", prev_tokr)
                for kc in range(KC):
                    P.emit("vector", lambda v, kc=kc: v.tensor_sub(p1[:, kc, :], p3[:, kc, :], psR0[:, :]),
                           inc="SV" if kc == KC - 1 else None)
                bc["s"] = P.cnt["SV"]
                bank_gate["R0"].append(("SV", bc["s"]))
            else:
                if fc == 1:
                    P.wait("vector", "ST", bc["r2c"])
                    P.wait("vector", "ST", prev_mlp1_tok)
                for kc in ([fc - 1] if fc < 7 else [6, 7]):
                    P.emit("vector", lambda v, kc=kc: v.tensor_mul(p0[:, kc, :], p1[:, kc, :], psR1[:, :]))
                    P.emit("vector", lambda v, kc=kc: v.tensor_scalar(
                        p0[:, kc, :], p0[:, kc, :], n1w_s[:, kc:kc + 1], n1b_s[:, kc:kc + 1], ALU.mult, ALU.add))
                    P.emit("vector", lambda v, kc=kc, qAh=qAh: v.tensor_copy(qAh[:, kc, :], p0[:, kc, :]))
                    P.emit("vector", lambda v, kc=kc, qAh=qAh, qAl=qAl:
                           v.tensor_sub(qAl[:, kc, :], p0[:, kc, :], qAh[:, kc, :]),
                           inc="SV" if (fc == 7 and kc == 7) else None)
                if fc == 7:
                    pend_xn2p[ti] = P.cnt["SV"]
                    bank_gate["R1"].append(("SV", pend_xn2p[ti]))

        def emit_p1_tail(ti):
            rstd_b = bcast(r2[0:1, :], "R0", [("SV", pend_bc[ti]["rstd_tok"])])
            P.wait("vector", "ST", rstd_b)
            for kc in range(KC):
                P.emit("vector", lambda v, kc=kc: v.tensor_mul(p1[:, kc, :], p1[:, kc, :], psR0[:, :]),
                       inc="SV" if kc == KC - 1 else None)
            bank_gate["R0"].append(("SV", P.cnt["SV"]))
            if dbg and ti == 0:
                dump("xn", p1[:].rearrange("p a b -> p (a b)"), [("SV", P.cnt["SV"])])
                dump("xn2", p0[:].rearrange("p a b -> p (a b)"), [("SV", pend_xn2p[ti])], blockers=("vector",))

        P.barrier()
        emit_p0_squares(0)
        emit_p0_stats(0)
        emit_p1_bcast(0)
        for fc0 in range(8):
            p1_weave(0, fc0)
        emit_p1_tail(0)

        for i in range(NT):
            first_in_batch = (i % 4 == 0)
            qAh, qAl, qBh, qBl = pairs_of(i)
            xn2p_tok = pend_xn2p[i]

            scr_lo_tok = {}
            gg_tok = 0
            for q in range(4):
                use_unit(i, q)
                uvh, uvl = unit_views(i, q)
                for fo in range(4):
                    fcg = q * 4 + fo
                    bk = "A" if fcg % 2 == 0 else "B"
                    tok = mm_group(bk, mm3(uvh, uvl, (fo * 128, (fo + 1) * 128), qAh, qAl),
                                   extra_gate=[("SV", xn2p_tok)] if fcg == 0 else None)
                    if fcg < 8:
                        evac("scalar", bk, lambda s, bk=bk, tc=fcg, fcg=fcg:
                             s.activation(p3[:, tc, :], banks[bk][:], AF.Silu,
                                          bias=gate_b_s[:, fcg:fcg + 1], scale=1.0), tok)
                    else:
                        tc = fcg - 8
                        scr = b0 if tc % 2 == 0 else b1
                        if i == 0 and tc == 0:
                            P.wait("scalar", "SD", wq_out_tok_box[0])
                            P.wait("vector", "SD", wq_out_tok_box[0])
                        if tc == 0:
                            P.wait("scalar", "ST", prev_ffn_tok)
                        P.wait("scalar", "ST", tok)
                        if tc >= 2:
                            P.wait("scalar", "SV", scr_lo_tok[tc - 2])
                        st1 = P.emit("scalar", lambda s, bk=bk, scr=scr, fcg=fcg:
                                     s.activation(scr[:], banks[bk][:], AF.Silu,
                                                  bias=gate_b_s[:, fcg:fcg + 1], scale=1.0), inc="SS")
                        bank_gate[bk].append(("SS", st1))
                        P.emit("scalar", lambda s, scr=scr, tc=tc, qBh=qBh: s.copy(qBh[:, tc, :], scr[:]), inc="SS")
                        st2 = P.cnt["SS"]
                        P.wait("vector", "SS", st2)
                        gg_tok = P.emit("vector", lambda v, scr=scr, tc=tc, qBh=qBh, qBl=qBl:
                                        v.tensor_sub(qBl[:, tc, :], scr[:], qBh[:, tc, :]), inc="SV")
                        scr_lo_tok[tc] = gg_tok
                done_unit(i, q)
            silu_all_s = P.cnt["SS"]

            use_unit(i, 4)
            use_unit(i, 5)
            if i == 0:
                P.wait("vector", "SW", unit_dma_cum[5])
                u4h0, u4l0 = unit_views(0, 4)
                u5h0, u5l0 = unit_views(0, 5)
                for kc in range(KC):
                    P.emit("vector", lambda v, kc=kc, u4h0=u4h0: v.reduce_sum(swp_s[:, kc:kc + 1], u4h0[:, kc, :], axis=AX.X))
                    P.emit("vector", lambda v, kc=kc, u5h0=u5h0: v.reduce_sum(swp2_s[:, kc:kc + 1], u5h0[:, kc, :], axis=AX.X))
                P.drain("vector")
                P.emit("vector", lambda v: v.tensor_add(swm_s[:], swp_s[:], swp2_s[:]))
                for kc in range(KC):
                    P.emit("vector", lambda v, kc=kc, u4l0=u4l0: v.reduce_sum(swp_s[:, kc:kc + 1], u4l0[:, kc, :], axis=AX.X))
                    P.emit("vector", lambda v, kc=kc, u5l0=u5l0: v.reduce_sum(swp2_s[:, kc:kc + 1], u5l0[:, kc, :], axis=AX.X))
                P.drain("vector")
                P.emit("vector", lambda v: v.tensor_add(swp_s[:], swp_s[:], swp2_s[:]))
                P.drain("vector")
                P.emit("vector", lambda v: v.tensor_add(swm_s[:], swm_s[:], swp_s[:]))
                P.drain("vector")
                P.emit("vector", lambda v: v.tensor_scalar(swm_s[:], swm_s[:], INV_D, None, ALU.mult), inc="SV")
                P.wait("tensor", "SV", P.cnt["SV"])
            bank_wait("R0")
            for kc in range(KC):
                P.emit("tensor", lambda t, kc=kc, s=(kc == 0), p=(kc == KC - 1):
                       t.matmul(psR0[0:1, :], swm_s[:, kc:kc + 1], p0[:, kc, :], start=s, stop=p),
                       inc="ST" if kc == KC - 1 else None)
            dmv_tok = P.cnt["ST"]
            P.wait("vector", "ST", dmv_tok)
            P.wait("vector", "SS", silu_all_s)
            for kc in range(KC):
                scr = b0 if kc % 2 == 0 else b1
                P.emit("vector", lambda v, kc=kc, scr=scr: v.tensor_mul(scr[:], p0[:, kc, :], p3[:, kc, :]))
                P.emit("vector", lambda v, kc=kc, scr=scr, qAh=qAh: v.tensor_copy(qAh[:, kc, :], scr[:]))
                P.emit("vector", lambda v, kc=kc, scr=scr, qAh=qAh, qAl=qAl: v.tensor_sub(qAl[:, kc, :], scr[:], qAh[:, kc, :]),
                       inc="SV" if kc == KC - 1 else None)
            xgp_tok = P.cnt["SV"]
            P.wait("scalar", "ST", dmv_tok)
            da_tok = P.emit("scalar", lambda s: s.activation(
                r0[0:1, :], psR0[0:1, :], AF.Sigmoid, bias=sbm_s[0:1, 0:1], scale=-1.0), inc="SS")
            bank_gate["R0"].append(("SS", da_tok))
            ln_tok = P.emit("scalar", lambda s: s.activation(r1[0:1, :], r0[0:1, :], AF.Ln), inc="SS")
            P.wait("vector", "SS", ln_tok)
            P.emit("vector", lambda v: v.tensor_scalar(r1[0:1, :], r1[0:1, :], -1.0, None, ALU.mult), inc="SV")
            dl_tok = P.cnt["SV"]

            u4h, u4l = unit_views(i, 4)
            u5h, u5l = unit_views(i, 5)
            for j in range(KC):
                uvh, uvl = (u4h, u4l) if j < 4 else (u5h, u5l)
                bk = "A" if j % 2 == 0 else "B"
                tok = mm_group(bk, mm3(uvh, uvl, ((j % 4) * 128, (j % 4 + 1) * 128), qBh, qBl),
                               extra_gate=[("SV", gg_tok), ("SS", silu_all_s)] if j == 0 else None)
                evac("vector", bk, lambda v, bk=bk, j=j: v.tensor_scalar(
                    p3[:, j, :], banks[bk][:], state_b_s[:, j:j + 1], None, ALU.add), tok,
                    extra=[("SV", xgp_tok)] if j == 0 else ())

            dlb = bcast(r1[0:1, :], "R0", [("SV", dl_tok)])
            dab = bcast(r0[0:1, :], "R1", [("SS", da_tok)])

            P.wait("vector", "ST", dlb)
            P.wait("vector", "ST", dab)

            def chain_step(j):
                if 1 <= j <= 8:
                    c = j - 1
                    P.emit("vector", lambda v, c=c: v.tensor_mul(p0[:, c, :], p0[:, c, :], p3[:, c, :]))
                    P.emit("vector", lambda v, c=c: v.tensor_mul(p0[:, c, :], p0[:, c, :], psR0[:, :]))
                if 2 <= j <= 9:
                    c = j - 2
                    init = 0.0 if first_in_batch else carry[:, c:c + 1]
                    P.emit("vector", lambda v, c=c, init=init: v.tensor_tensor_scan(
                        p3[:, c, :], psR1[:, :], p0[:, c, :], init, ALU.mult, ALU.add))
                if 3 <= j <= 10:
                    c = j - 3
                    P.emit("vector", lambda v, c=c, qBh=qBh: v.tensor_copy(qBh[:, c, :], p3[:, c, :]))
                    P.emit("vector", lambda v, c=c, qBh=qBh, qBl=qBl: v.tensor_sub(qBl[:, c, :], p3[:, c, :], qBh[:, c, :]))
                if 4 <= j <= 11:
                    c = j - 4
                    P.emit("vector", lambda v, c=c: v.tensor_copy(carry[:, c:c + 1], p3[:, c, T - 1:T]),
                           inc="SV" if j == 11 else None)

            for j in range(KC):
                uvh, uvl = (u4h, u4l) if j < 4 else (u5h, u5l)
                bk = "A" if j % 2 == 0 else "B"
                tok = mm_group(bk, mm3(uvh, uvl, ((j % 4) * 128, (j % 4 + 1) * 128), qAh, qAl),
                               extra_gate=[("SV", xgp_tok)] if j == 0 else None)
                chain_step(j)
                evac("vector", bk, lambda v, bk=bk, j=j: v.tensor_scalar(
                    p0[:, j, :], banks[bk][:], state_b_s[:, j:j + 1], None, ALU.add), tok)
                if j == 3:
                    done_unit(i, 4)
            done_unit(i, 5)
            for j in range(KC, 12):
                chain_step(j)
            hp_tok = P.cnt["SV"]
            carry_tok = hp_tok
            prev_hp_sv = hp_tok
            bank_gate["R0"].append(("SV", hp_tok))
            bank_gate["R1"].append(("SV", hp_tok))
            if dbg and i == 0:
                dump("dA_row", r0[0:1, :], [("SS", da_tok)], blockers=("vector",))
                dump("delta_row", r1[0:1, :], [("SV", dl_tok)], blockers=("vector",))
                dump("swm", swm_s[:], [("SV", dl_tok)])
            if i == 0:
                quantize_half(0)

            use_unit(i, 6)
            use_unit(i, 7)
            u6h, u6l = unit_views(i, 6)
            u7h, u7l = unit_views(i, 7)
            for fc in range(KC):
                uvh, uvl = (u6h, u6l) if fc < 4 else (u7h, u7l)
                bk = "A" if fc % 2 == 0 else "B"
                tok = mm_group(bk, mm3(uvh, uvl, ((fc % 4) * 128, (fc % 4 + 1) * 128), qBh, qBl),
                               extra_gate=[("SV", hp_tok)] if fc == 0 else None)
                evac("vector", bk, lambda v, bk=bk, fc=fc: v.scalar_tensor_tensor(
                    p2[:, fc, :], banks[bk][:], out_b_s[:, fc:fc + 1], p1[:, fc, :], ALU.add, ALU.add), tok,
                    extra=[("SO", prev_out_so)] if fc == 0 else ())
                if fc == 3:
                    done_unit(i, 6)
            done_unit(i, 7)
            ssm_tok = P.cnt["ST"]
            xmid_tok = P.cnt["SV"]

            P.wait("sync", "ST", ssm_tok)
            P.wait("sync", "SV", carry_tok)
            P.emit("sync", lambda s, i=i: s.dma_start(
                out=p3[:], in_=x_tl[i].rearrange("p (a b) -> p a b", b=T)), inc="SX", n=16)
            xrl_tok = P.cnt["SX"]

            tokS1b, tokSQb = ln_stats(p2, s1_extra=[("SV", xmid_tok)], sq_extra=[("SV", xmid_tok)])
            rstd2_tok = row_musig(tokS1b, tokSQb)
            mu2_b = bcast(r0[0:1, :], "R0", [("SV", rstd2_tok)])
            rstd2_b = bcast(r2[0:1, :], "R1", [("SV", rstd2_tok)])
            bank_wait("B")
            for kc in range(KC):
                P.emit("tensor", lambda t, kc=kc, s=(kc == 0), p=(kc == KC - 1):
                       t.matmul(psB[0:16, :], a_s[:, kc, :], p1[:, kc, :], start=s, stop=p),
                       inc="ST" if kc == KC - 1 else None)
            tokr = P.cnt["ST"]
            P.wait("vector", "ST", rstd2_b)
            for kc in range(KC):
                scr = b0 if kc % 2 == 0 else b1
                P.emit("vector", lambda v, kc=kc, scr=scr: v.tensor_sub(scr[:], p2[:, kc, :], psR0[:, :]))
                P.emit("vector", lambda v, kc=kc, scr=scr: v.tensor_mul(scr[:], scr[:], psR1[:, :]))
                P.emit("vector", lambda v, kc=kc, scr=scr: v.tensor_scalar(
                    scr[:], scr[:], n2w_s[:, kc:kc + 1], n2b_s[:, kc:kc + 1], ALU.mult, ALU.add))
                P.emit("vector", lambda v, kc=kc, scr=scr, qBh=qBh: v.tensor_copy(qBh[:, kc, :], scr[:]))
                P.emit("vector", lambda v, kc=kc, scr=scr, qBh=qBh, qBl=qBl: v.tensor_sub(qBl[:, kc, :], scr[:], qBh[:, kc, :]),
                       inc="SV" if kc == KC - 1 else None)
            x2p_tok = P.cnt["SV"]
            bank_gate["R0"].append(("SV", x2p_tok))
            bank_gate["R1"].append(("SV", x2p_tok))
            if dbg and i == 0:
                dump("x2h", qBh[:].rearrange("p a b -> p (a b)"), [("SV", x2p_tok)], blockers=("vector",))

            P.wait("vector", "ST", tokr)
            rs_tok = P.emit("vector", lambda v: v.tensor_copy(r_s[:], psB[0:16, :]), inc="SV")
            bank_gate["B"].append(("SV", rs_tok))
            prev_tokr = tokr

            if i == 0:
                wqsd = quantize_half(1)
                P.wait("vector", "SD", wqsd)
            for kc in range(KC):
                scr = b0 if kc % 2 == 0 else b1
                P.emit("vector", lambda v, kc=kc, scr=scr: v.tensor_scalar(
                    scr[:], p1[:, kc, :], scale_col[:, 0:1], None, ALU.mult))
                P.emit("vector", lambda v, kc=kc, scr=scr, qAh=qAh: v.tensor_copy(qAh[:, kc, :], scr[:]))
                P.emit("vector", lambda v, kc=kc, scr=scr, qAh=qAh, qAl=qAl: v.tensor_sub(qAl[:, kc, :], scr[:], qAh[:, kc, :]),
                       inc="SV" if kc == KC - 1 else None)
            xnp_tok = P.cnt["SV"]
            prev_xns_tok = xnp_tok

            P.wait("vector", "SX", xrl_tok)
            for kc in range(KC):
                P.emit("vector", lambda v, kc=kc: v.tensor_add(p2[:, kc, :], p3[:, kc, :], p2[:, kc, :]),
                       inc="SV" if kc == KC - 1 else None)
            xmm_tok = P.cnt["SV"]
            if dbg and i == 0:
                dump("h_hi", qBh[:].rearrange("p a b -> p (a b)"), [("SV", hp_tok)])
                dump("x_mid", p2[:].rearrange("p a b -> p (a b)"), [("SV", xmid_tok)], blockers=("vector",))
                dump("r_s", r_s[:], [("SV", rs_tok)])

            if i + 1 < NT:
                P.wait("sync", "SV", xmm_tok)
                P.emit("sync", lambda s, i=i: s.dma_start(
                    out=p3[:], in_=x_tl[i + 1].rearrange("p (a b) -> p a b", b=T)), inc="SX", n=16)
                x_tok[i + 1] = P.cnt["SX"]

            wqv, _ = unit_views(i, 24)
            mid_lo_tok = {}
            for hg in range(4):
                ua, ub = 8 + hg * 4, 9 + hg * 4
                use_unit(i, ua)
                use_unit(i, ub)
                w1ah, w1al = unit_views(i, ua)
                w1bh, w1bl = unit_views(i, ub)
                for hc in range(8):
                    hcg = hg * 8 + hc
                    uvh, uvl = (w1ah, w1al) if hc < 4 else (w1bh, w1bl)
                    bk = "A" if hc % 2 == 0 else "B"
                    tok = mm_group(bk, mm3(uvh, uvl, ((hc % 4) * 128, (hc % 4 + 1) * 128), qBh, qBl),
                                   extra_gate=[("SV", x2p_tok)] if (hg == 0 and hc == 0) else None)
                    scr = b0 if hc % 2 == 0 else b1
                    if i == 0 and hcg == 0:
                        P.wait("scalar", "SO", gsplit_out_tok_box[0])
                        P.wait("vector", "SO", gsplit_out_tok_box[0])
                    if hcg < 2:
                        P.wait("scalar", "SV", xnp_tok)
                    P.wait("scalar", "ST", tok)
                    if hcg >= 2:
                        P.wait("scalar", "SV", mid_lo_tok[hcg - 2])
                    gt = P.emit("scalar", lambda s, bk=bk, scr=scr, hcg=hcg:
                                s.activation(scr[:], banks[bk][:], AF.Gelu,
                                             bias=mlp_b1_s[:, hcg:hcg + 1], scale=1.0), inc="SS")
                    bank_gate[bk].append(("SS", gt))
                    P.emit("scalar", lambda s, scr=scr, hc=hc: s.copy(mdh[:, hc, :], scr[:]), inc="SS")
                    st2 = P.cnt["SS"]
                    P.wait("vector", "SS", st2)
                    mid_lo_tok[hcg] = P.emit("vector", lambda v, scr=scr, hc=hc:
                                             v.tensor_sub(mdl[:, hc, :], scr[:], mdh[:, hc, :]), inc="SV")
                    if hc == 3:
                        done_unit(i, ua)
                mid_tok = P.cnt["SV"]
                if dbg and i == 0 and hg == 0:
                    dump("mid0h", mdh[:].rearrange("p a b -> p (a b)"), [("SV", mid_tok)], blockers=("scalar", "vector"))
                    dump("mid0l", mdl[:].rearrange("p a b -> p (a b)"), [("SV", mid_tok)], blockers=("scalar", "vector"))
                if hg == 3:
                    prev_mlp1_tok = P.cnt["ST"]
                    if i + 1 < NT:
                        emit_p0_stats(i + 1)
                        emit_p1_bcast(i + 1)
                done_unit(i, ub)
                uc, ud = 10 + hg * 4, 11 + hg * 4
                use_unit(i, uc)
                use_unit(i, ud)
                w2ah, w2al = unit_views(i, uc)
                w2bh, w2bl = unit_views(i, ud)
                if hg == 3:
                    use_unit(i, 24)
                for fc in range(8):
                    uvh, uvl = (w2ah, w2al) if fc < 4 else (w2bh, w2bl)
                    bk = "A" if fc % 2 == 0 else "B"
                    mms = mm3(uvh, uvl, ((fc % 4) * 128, (fc % 4 + 1) * 128), mdh, mdl)
                    if hg == 3:
                        for kc in range(KC):
                            mms.append(((lambda wqv=wqv, kc=kc, fc=fc: wqv[:, kc, fc * 128:(fc + 1) * 128]),
                                        (lambda kc=kc, qAh=qAh: qAh[:, kc, :])))
                            mms.append(((lambda wqv=wqv, kc=kc, fc=fc: wqv[:, kc, fc * 128:(fc + 1) * 128]),
                                        (lambda kc=kc, qAl=qAl: qAl[:, kc, :])))
                        mms.append(((lambda fc=fc: bt_s[:, fc, :]), (lambda: r_s[:])))
                    eg = [("SV", mid_tok)] if fc == 0 else []
                    if hg == 3 and fc == 0:
                        eg += [("SV", xnp_tok), ("SV", rs_tok)]
                    tok = mm_group(bk, mms, extra_gate=eg or None)
                    if hg == 0:
                        evac("vector", bk, lambda v, bk=bk, fc=fc: v.tensor_copy(
                            acc[:, fc, :], banks[bk][:]), tok)
                    else:
                        evac("vector", bk, lambda v, bk=bk, fc=fc: v.tensor_add(
                            acc[:, fc, :], acc[:, fc, :], banks[bk][:]), tok)
                    if fc == 3:
                        done_unit(i, uc)
                    if hg == 3 and i + 1 < NT:
                        p1_weave(i + 1, fc)
                done_unit(i, ud)
                if hg == 2 and i + 1 < NT:
                    emit_p0_squares(i + 1)
                if hg == 3:
                    prev_ffn_tok = P.cnt["ST"]
            done_unit(i, 24)
            if i + 1 < NT:
                emit_p1_tail(i + 1)

            if dbg and i == 0:
                dump("acc", acc[:].rearrange("p a b -> p (a b)"), [("SV", P.cnt["SV"])], blockers=("vector",))
                dump("w10h", wsp_hi[10], [("SV", P.cnt["SV"])])
                dump("w10l", wsp_lo[10], [("SV", P.cnt["SV"])])
                dump("w11h", wsp_hi[11], [("SV", P.cnt["SV"])])
            out_gate = []
            for fc in range(KC):
                P.emit("vector", lambda v, fc=fc: v.scalar_tensor_tensor(
                    p2[:, fc, :], acc[:, fc, :], cb_s[:, fc:fc + 1], p2[:, fc, :], ALU.add, ALU.add),
                       inc="SV")
                out_gate.append(P.cnt["SV"])
            if dbg and i == 0:
                dump("final", p2[:].rearrange("p a b -> p (a b)"), [("SV", out_gate[-1])], blockers=("vector",))
            for half in range(2):
                P.wait("gpsimd", "SV", out_gate[4 * half + 3])
                P.emit("gpsimd", lambda g, i=i, half=half: g.dma_start(
                    out=out_tl[i][:, half * 2048:(half + 1) * 2048],
                    in_=p2[:, 4 * half:4 * half + 4, :].rearrange("p a b -> p (a b)")), inc="SO", n=16)
            prev_out_so = P.cnt["SO"]

        P.wait("gpsimd", "SO", P.cnt["SO"])

        def runner(eng_name):
            drain_incs = eng_name in ("tensor", "vector", "scalar", "gpsimd")

            def body(eng):
                for op in P.ops[eng_name]:
                    if op[0] == "w":
                        eng.wait_ge(sems[op[1]], op[2])
                    elif op[0] == "d":
                        eng.drain()
                    else:
                        _, fn, inc, n = op
                        instr = fn(eng)
                        if inc is not None and drain_incs and n != 16:
                            instr = eng.drain()
                        if inc is not None:
                            instr.then_inc(sems[inc], n)
            return body

        with nc.Block() as block:
            block.sync(runner("sync"))
            block.gpsimd(runner("gpsimd"))
            block.tensor(runner("tensor"))
            block.vector(runner("vector"))
            block.scalar(runner("scalar"))

    if dbg:
        return nc, dbg_names
    return nc


def _block_unit(w, k0, k1, f0, f1):
    v = w[k0:k1, f0:f1]
    return np.ascontiguousarray(v.reshape((k1 - k0) // 128, 128, f1 - f0).transpose(1, 0, 2)).reshape(128, 4096)


def _col_block(v):
    return np.ascontiguousarray(np.asarray(v, np.float32).reshape(-1, 128).T)


def make_hstream(gw, sw, ow, w1, w2):
    units = []
    for q in range(4):
        units.append(_block_unit(gw, 0, 1024, q * 512, (q + 1) * 512))
    for h in range(2):
        units.append(_block_unit(sw, 0, 1024, h * 512, (h + 1) * 512))
    for h in range(2):
        units.append(_block_unit(ow, 0, 1024, h * 512, (h + 1) * 512))
    for hg in range(4):
        units.append(_block_unit(w1, 0, 1024, hg * 1024, hg * 1024 + 512))
        units.append(_block_unit(w1, 0, 1024, hg * 1024 + 512, hg * 1024 + 1024))
        units.append(_block_unit(w2, hg * 1024, (hg + 1) * 1024, 0, 512))
        units.append(_block_unit(w2, hg * 1024, (hg + 1) * 1024, 512, 1024))
    return np.ascontiguousarray(np.stack(units))


_NC_CACHE = [None]


def kernel(L0_attn_x, L0_attn_norm1_w, L0_attn_norm1_b, L0_attn_gate_w, L0_attn_gate_b,
           L0_attn_state_w, L0_attn_state_b, L0_attn_out_w, L0_attn_out_b,
           L0_attn_norm2_w, L0_attn_norm2_b, L0_attn_mlp_w1, L0_attn_mlp_b1,
           L0_attn_mlp_w2, L0_attn_mlp_b2, L0_ffn_kernel, L0_ffn_bias, L0_ffn_a, L0_ffn_b):
    f = np.float32
    x = np.ascontiguousarray(L0_attn_x, dtype=f)
    hstream = make_hstream(np.asarray(L0_attn_gate_w, f), np.asarray(L0_attn_state_w, f),
                           np.asarray(L0_attn_out_w, f), np.asarray(L0_attn_mlp_w1, f),
                           np.asarray(L0_attn_mlp_w2, f))
    wk = np.asarray(L0_ffn_kernel, f)
    consts = {
        "hstream": hstream,
        "wk_blk": np.ascontiguousarray(
            wk.reshape(8, 128, 1024).transpose(1, 0, 2).reshape(128, 8192)),
        "gate_b_t": _col_block(L0_attn_gate_b),
        "state_b_t": _col_block(L0_attn_state_b),
        "out_b_t": _col_block(L0_attn_out_b),
        "mlp_b1_t": _col_block(L0_attn_mlp_b1),
        "mlp_b2_t": _col_block(L0_attn_mlp_b2),
        "ffn_b_t": _col_block(L0_ffn_bias),
        "n1w_t": _col_block(L0_attn_norm1_w),
        "n1b_t": _col_block(L0_attn_norm1_b),
        "n2w_t": _col_block(L0_attn_norm2_w),
        "n2b_t": _col_block(L0_attn_norm2_b),
        "a_t": np.ascontiguousarray(
            np.asarray(L0_ffn_a, f).reshape(8, 128, 16).transpose(1, 0, 2).reshape(128, 128)),
        "b_t": np.ascontiguousarray(np.asarray(L0_ffn_b, f)),
    }

    in_maps = []
    for c in range(N_CORES):
        xc = x[2 * c:2 * c + 2].reshape(4096, 1024)
        x_tl = np.ascontiguousarray(
            xc.reshape(8, 512, 8, 128).transpose(0, 3, 2, 1).reshape(NT, 128, KC * T))
        m = dict(consts)
        m["x_tl"] = x_tl
        in_maps.append(m)

    if _NC_CACHE[0] is None:
        _NC_CACHE[0] = build_nc()
    res = run_bass_kernel_spmd(_NC_CACHE[0], in_maps, core_ids=list(range(N_CORES)))

    outs = []
    for c in range(N_CORES):
        ot = np.asarray(res.results[c]["out_tl"])
        oc = ot.reshape(8, 128, 8, 512).transpose(0, 3, 2, 1).reshape(4096, 1024)
        outs.append(oc.reshape(2, 2048, 1024))
    return np.ascontiguousarray(np.concatenate(outs, axis=0))


# version 30 (final)
# speedup vs baseline: 1.0244x; 1.0244x over previous
import numpy as np
import concourse.bass as bass
import concourse.mybir as mybir
from concourse.bass_utils import run_bass_kernel_spmd

F32 = mybir.dt.float32
BF16 = mybir.dt.bfloat16
AF = mybir.ActivationFunctionType
ALU = mybir.AluOpType
AX = mybir.AxisListType

T = 512
NT = 8
KC = 8
EPS = 1e-5
N_CORES = 8
NW = 24
UPT = 25
NSLOT = 3
INV7 = 1.0 / 7.0
INV_D = 1.0 / 1024.0

ENGS = ("tensor", "vector", "scalar", "sync", "gpsimd")
SEM_OF = {"tensor": "ST", "vector": "SV", "scalar": "SS"}


class Plan:
    def __init__(self):
        self.ops = {e: [] for e in ENGS}
        self.cnt = {"ST": 0, "SV": 0, "SS": 0, "SW": 0, "SX": 0, "SO": 0, "SG": 0, "SD": 0}
        self.last_wait = {}

    def wait(self, eng, sem, val):
        if not val or val <= 0:
            return
        key = (eng, sem)
        if self.last_wait.get(key, 0) >= val:
            return
        self.last_wait[key] = val
        self.ops[eng].append(("w", sem, val))

    def emit(self, eng, fn, inc=None, n=1):
        self.ops[eng].append(("o", fn, inc, n))
        if inc is not None:
            self.cnt[inc] += n
            return self.cnt[inc]
        return None

    def drain(self, eng):
        self.ops[eng].append(("d",))

    def barrier(self):
        for a in ("tensor", "vector", "scalar"):
            for b in ("tensor", "vector", "scalar"):
                if a != b:
                    self.wait(b, SEM_OF[a], self.cnt[SEM_OF[a]])


def build_nc(dbg=False):
    nc = bass.Bass("TRN2", target_bir_lowering=False)

    dbg_t = nc.dram_tensor("dbg", [24, 128, 4096], F32, kind="ExternalOutput").ap() if dbg else None
    dbg_names = []

    x_tl = nc.dram_tensor("x_tl", [NT, 128, KC * T], F32, kind="ExternalInput").ap()
    hstream = nc.dram_tensor("hstream", [NW, 128, 4096], F32, kind="ExternalInput").ap()
    wk_blk = nc.dram_tensor("wk_blk", [128, 8192], F32, kind="ExternalInput").ap()
    gate_b_t = nc.dram_tensor("gate_b_t", [128, 16], F32, kind="ExternalInput").ap()
    state_b_t = nc.dram_tensor("state_b_t", [128, 8], F32, kind="ExternalInput").ap()
    out_b_t = nc.dram_tensor("out_b_t", [128, 8], F32, kind="ExternalInput").ap()
    mlp_b1_t = nc.dram_tensor("mlp_b1_t", [128, 32], F32, kind="ExternalInput").ap()
    mlp_b2_t = nc.dram_tensor("mlp_b2_t", [128, 8], F32, kind="ExternalInput").ap()
    ffn_b_t = nc.dram_tensor("ffn_b_t", [128, 8], F32, kind="ExternalInput").ap()
    n1w_t = nc.dram_tensor("n1w_t", [128, 8], F32, kind="ExternalInput").ap()
    n1b_t = nc.dram_tensor("n1b_t", [128, 8], F32, kind="ExternalInput").ap()
    n2w_t = nc.dram_tensor("n2w_t", [128, 8], F32, kind="ExternalInput").ap()
    n2b_t = nc.dram_tensor("n2b_t", [128, 8], F32, kind="ExternalInput").ap()
    a_t = nc.dram_tensor("a_t", [128, 128], F32, kind="ExternalInput").ap()
    b_t = nc.dram_tensor("b_t", [16, 1024], F32, kind="ExternalInput").ap()
    wq_dram = nc.dram_tensor("wq_dram", [128, 8192], BF16, kind="Internal").ap()
    wsp_hi = nc.dram_tensor("wsp_hi", [NW, 128, 4096], BF16, kind="Internal").ap()
    wsp_lo = nc.dram_tensor("wsp_lo", [NW, 128, 4096], BF16, kind="Internal").ap()
    out_tl = nc.dram_tensor("out_tl", [NT, 128, KC * T], F32, kind="ExternalOutput").ap()

    from contextlib import ExitStack
    with ExitStack() as st:
        e = st.enter_context
        p0 = e(nc.sbuf_tensor("p0", [128, KC, T], F32))
        p1 = e(nc.sbuf_tensor("p1", [128, KC, T], F32))
        p2 = e(nc.sbuf_tensor("p2", [128, KC, T], F32))
        p3 = e(nc.sbuf_tensor("p3", [128, KC, T], F32))
        acc = e(nc.sbuf_tensor("acc", [128, KC, T], F32))
        q0h = e(nc.sbuf_tensor("q0h", [128, KC, T], BF16))
        q0l = e(nc.sbuf_tensor("q0l", [128, KC, T], BF16))
        q1h = e(nc.sbuf_tensor("q1h", [128, KC, T], BF16))
        q1l = e(nc.sbuf_tensor("q1l", [128, KC, T], BF16))
        mdh = e(nc.sbuf_tensor("mdh", [128, KC, T], BF16))
        mdl = e(nc.sbuf_tensor("mdl", [128, KC, T], BF16))
        slots = [e(nc.sbuf_tensor(f"slot{k}", [128, 8192], BF16)) for k in range(NSLOT)]
        b0 = e(nc.sbuf_tensor("b0", [128, T], F32))
        b1 = e(nc.sbuf_tensor("b1", [128, T], F32))
        r0 = e(nc.sbuf_tensor("r0", [1, T], F32))
        r1 = e(nc.sbuf_tensor("r1", [1, T], F32))
        r2 = e(nc.sbuf_tensor("r2", [1, T], F32))
        r3 = e(nc.sbuf_tensor("r3", [1, T], F32))
        srow = e(nc.sbuf_tensor("srow", [1, 128], F32))
        ones_row = e(nc.sbuf_tensor("ones_row", [1, 128], F32))
        ones_col = e(nc.sbuf_tensor("ones_col", [128, 1], F32))
        eps_s = e(nc.sbuf_tensor("eps_s", [1, 1], F32))
        sbm_s = e(nc.sbuf_tensor("sbm_s", [128, 1], F32))
        am_col = e(nc.sbuf_tensor("am_col", [128, 1], F32))
        scale_col = e(nc.sbuf_tensor("scale_col", [128, 1], F32))
        inv_col = e(nc.sbuf_tensor("inv_col", [128, 1], F32))
        carry = e(nc.sbuf_tensor("carry", [128, 8], F32))
        r_s = e(nc.sbuf_tensor("r_s", [16, T], F32))
        gate_b_s = e(nc.sbuf_tensor("gate_b_s", [128, 16], F32))
        state_b_s = e(nc.sbuf_tensor("state_b_s", [128, 8], F32))
        out_b_s = e(nc.sbuf_tensor("out_b_s", [128, 8], F32))
        mlp_b1_s = e(nc.sbuf_tensor("mlp_b1_s", [128, 32], F32))
        b2_s = e(nc.sbuf_tensor("b2_s", [128, 8], F32))
        fb_s = e(nc.sbuf_tensor("fb_s", [128, 8], F32))
        cb_s = e(nc.sbuf_tensor("cb_s", [128, 8], F32))
        n1w_s = e(nc.sbuf_tensor("n1w_s", [128, 8], F32))
        n1b_s = e(nc.sbuf_tensor("n1b_s", [128, 8], F32))
        n2w_s = e(nc.sbuf_tensor("n2w_s", [128, 8], F32))
        n2b_s = e(nc.sbuf_tensor("n2b_s", [128, 8], F32))
        a_s = e(nc.sbuf_tensor("a_s", [128, 8, 16], F32))
        bt_s = e(nc.sbuf_tensor("bt_s", [16, 8, 128], F32))
        swm_s = e(nc.sbuf_tensor("swm_s", [128, 8], F32))
        swp_s = e(nc.sbuf_tensor("swp_s", [128, 8], F32))
        swp2_s = e(nc.sbuf_tensor("swp2_s", [128, 8], F32))
        gstg = e(nc.sbuf_tensor("gstg", [128, 2048], F32))
        psA = e(nc.psum_tensor("psA", [128, T], F32))
        psB = e(nc.psum_tensor("psB", [128, T], F32))
        psR0 = e(nc.psum_tensor("psR0", [128, T], F32))
        psR1 = e(nc.psum_tensor("psR1", [128, T], F32))
        SW = e(nc.semaphore("SW"))
        SX = e(nc.semaphore("SX"))
        SO = e(nc.semaphore("SO"))
        ST = e(nc.semaphore("ST"))
        SV = e(nc.semaphore("SV"))
        SS = e(nc.semaphore("SS"))
        SG = e(nc.semaphore("SG"))
        SD = e(nc.semaphore("SD"))
        sems = {"SW": SW, "SX": SX, "SO": SO, "ST": ST, "SV": SV, "SS": SS,
                "SG": SG, "SD": SD}

        banks = {"A": psA, "B": psB, "R0": psR0, "R1": psR1}
        P = Plan()
        bank_gate = {k: [] for k in banks}
        unit_tok = {}
        unit_dma_cum = {}
        split_out_tok = {}
        split_out_so_tok = {}
        wq_out_tok_box = [0]
        gsplit_out_tok_box = [0]

        def dump(name, src_ap, gates, blockers=()):
            if not dbg:
                return
            k = len(dbg_names)
            dbg_names.append(name)
            for sem, tok in gates:
                P.wait("gpsimd", sem, tok)
            r, c = src_ap.shape[0], src_ap.shape[1]
            P.emit("gpsimd", lambda g, k=k, src_ap=src_ap, r=r, c=c:
                   g.dma_start(out=dbg_t[k][0:r, 0:c], in_=src_ap,
                               allow_slow_non_contiguous=True), inc="SO", n=16)
            tok = P.cnt["SO"]
            for eng in blockers:
                P.wait(eng, "SO", tok)

        def bank_wait(bk):
            for sem, tok in bank_gate[bk]:
                P.wait("tensor", sem, tok)
            bank_gate[bk] = []

        def mm_group(bk, mm_list, extra_gate=None, start=True, stop=True):
            if start:
                bank_wait(bk)
            if extra_gate:
                for sem, tok in extra_gate:
                    P.wait("tensor", sem, tok)
            nmm = len(mm_list)
            for idx, (lf, rf) in enumerate(mm_list):
                last = idx == nmm - 1
                P.emit("tensor",
                       lambda t, lf=lf, rf=rf, bk=bk, s=(start and idx == 0), p=(stop and last):
                       t.matmul(banks[bk][:], lf(), rf(), start=s, stop=p),
                       inc="ST" if (last and stop) else None)
            return P.cnt["ST"]

        def evac(eng, bk, fn, gate_tok, extra=()):
            P.wait(eng, "ST", gate_tok)
            for sem, tok in extra:
                P.wait(eng, sem, tok)
            t = P.emit(eng, fn, inc=SEM_OF[eng])
            bank_gate[bk].append((SEM_OF[eng], t))
            return t

        def load_unit(g):
            tile, j = divmod(g, UPT)
            slot = slots[g % NSLOT]
            P.wait("sync", "ST", unit_tok.get(g - NSLOT, 0))
            if tile == 0:
                if j == 24:
                    P.wait("sync", "SD", wq_out_tok_box[0])
                elif j in split_out_so_tok:
                    P.wait("sync", "SO", split_out_so_tok[j])
                else:
                    P.wait("sync", "SD", split_out_tok.get(j, 0))
            if j == 24:
                P.emit("sync", lambda s, slot=slot: s.dma_start(out=slot[:], in_=wq_dram[:]),
                       inc="SW", n=16)
            else:
                P.emit("sync", lambda s, j=j, slot=slot:
                       s.dma_start(out=slot[:, 0:4096], in_=wsp_hi[j]), inc="SW", n=16)
                P.emit("sync", lambda s, j=j, slot=slot:
                       s.dma_start(out=slot[:, 4096:8192], in_=wsp_lo[j]), inc="SW", n=16)
            unit_dma_cum[g] = P.cnt["SW"]

        def unit_views(tile, j):
            slot = slots[(tile * UPT + j) % NSLOT]
            if j == 24:
                return slot[:].rearrange("p (a b) -> p a b", b=1024), None
            hi = slot[:, 0:4096].rearrange("p (a b) -> p a b", b=512)
            lo = slot[:, 4096:8192].rearrange("p (a b) -> p a b", b=512)
            return hi, lo

        def use_unit(tile, j):
            P.wait("tensor", "SW", unit_dma_cum[tile * UPT + j])

        def done_unit(tile, j):
            g = tile * UPT + j
            unit_tok[g] = P.cnt["ST"]
            if g + NSLOT < NT * UPT:
                load_unit(g + NSLOT)

        def mm3(uvh, uvl, fsl, rh, rl):
            mms = []
            for kc in range(KC):
                wh = (lambda uvh=uvh, kc=kc, fsl=fsl: uvh[:, kc, fsl[0]:fsl[1]])
                wl = (lambda uvl=uvl, kc=kc, fsl=fsl: uvl[:, kc, fsl[0]:fsl[1]])
                ah = (lambda rh=rh, kc=kc: rh[:, kc, :])
                al = (lambda rl=rl, kc=kc: rl[:, kc, :])
                mms += [(wh, ah), (wh, al), (wl, ah)]
            return mms

        P.emit("vector", lambda v: v.memset(ones_row[:], 1.0))
        P.emit("vector", lambda v: v.memset(ones_col[:], 1.0))
        P.emit("vector", lambda v: v.memset(eps_s[:], EPS), inc="SV")

        const_pairs = [
            (gate_b_t, gate_b_s), (state_b_t, state_b_s), (out_b_t, out_b_s),
            (mlp_b1_t, mlp_b1_s), (mlp_b2_t, b2_s), (ffn_b_t, fb_s),
            (n1w_t, n1w_s), (n1b_t, n1b_s), (n2w_t, n2w_s), (n2b_t, n2b_s),
        ]
        for src, dst in const_pairs:
            P.emit("gpsimd", lambda g, src=src, dst=dst:
                   g.dma_start(out=dst[:], in_=src[:]), inc="SX", n=16)
        P.emit("gpsimd", lambda g: g.dma_start(
            out=a_s[:], in_=a_t.rearrange("p (a b) -> p a b", b=16)), inc="SX", n=16)
        P.emit("gpsimd", lambda g: g.dma_start(
            out=bt_s[:], in_=b_t.rearrange("p (a b) -> p a b", b=128)), inc="SX", n=16)
        P.emit("gpsimd", lambda g: g.dma_start(
            out=p3[:], in_=x_tl[0].rearrange("p (a b) -> p a b", b=T)), inc="SX", n=16)
        consts_tok = P.cnt["SX"]

        P.wait("vector", "SX", consts_tok)
        P.emit("vector", lambda v: v.tensor_add(cb_s[:], b2_s[:], fb_s[:]), inc="SV")
        P.emit("vector", lambda v: v.reduce_sum(am_col[:, 0:1], state_b_s[:], axis=AX.X), inc="SV")
        sbsum_tok = P.cnt["SV"]
        P.wait("sync", "SV", sbsum_tok)
        P.emit("sync", lambda s: s.dma_start(out=srow[0:1, :], in_=am_col[:, 0:1]), inc="SX", n=16)
        sbrow_tok = P.cnt["SX"]
        P.wait("tensor", "SX", sbrow_tok)
        P.emit("tensor", lambda t: t.matmul(psR0[:, 0:128], ones_row[0:1, :], srow[0:1, :],
                                            start=True, stop=True), inc="ST")
        sb_mm_tok = P.cnt["ST"]
        P.wait("vector", "ST", sb_mm_tok)
        P.emit("vector", lambda v: v.reduce_sum(sbm_s[:, 0:1], psR0[:, 0:128], axis=AX.X))
        P.drain("vector")
        P.emit("vector", lambda v: v.tensor_scalar(sbm_s[:, 0:1], sbm_s[:, 0:1], -INV_D, None, ALU.mult), inc="SV")
        bank_gate["R0"].append(("SV", P.cnt["SV"]))

        pva = p2[:].rearrange("p a b -> p (a b)")
        pvb = acc[:].rearrange("p a b -> p (a b)")
        P.emit("sync", lambda s: s.dma_start(out=pva[:, 0:4096], in_=wk_blk[:, 0:4096]), inc="SW", n=16)
        P.emit("sync", lambda s: s.dma_start(out=pvb[:, 0:4096], in_=wk_blk[:, 4096:8192]), inc="SW", n=16)
        wkin0_tok = P.cnt["SW"]
        P.wait("vector", "SW", wkin0_tok)
        P.emit("vector", lambda v: v.reduce_max(am_col[:, 0:1], pva[:, 0:4096], axis=AX.X,
                                                apply_absolute_value=True))
        P.drain("vector")
        P.emit("vector", lambda v: v.reduce_max(scale_col[:, 0:1], pvb[:, 0:4096], axis=AX.X,
                                                apply_absolute_value=True))
        P.drain("vector")
        P.emit("vector", lambda v: v.tensor_max(am_col[:, 0:1], am_col[:, 0:1], scale_col[:, 0:1]), inc="SV")
        amax_tok = P.cnt["SV"]
        P.wait("sync", "SV", amax_tok)
        P.wait("sync", "ST", sb_mm_tok)
        P.emit("sync", lambda s: s.dma_start(out=srow[0:1, :], in_=am_col[:, 0:1]), inc="SX", n=16)
        row_tok = P.cnt["SX"]
        bank_wait("R0")
        P.wait("tensor", "SX", row_tok)
        P.emit("tensor", lambda t: t.matmul(psR0[:, 0:128], ones_row[0:1, :], srow[0:1, :],
                                            start=True, stop=True), inc="ST")
        amb_tok = P.cnt["ST"]
        P.wait("vector", "ST", amb_tok)
        P.emit("vector", lambda v: v.reduce_max(am_col[:, 0:1], psR0[:, 0:128], axis=AX.X))
        P.drain("vector")
        P.emit("vector", lambda v: v.tensor_scalar(scale_col[:, 0:1], am_col[:, 0:1], INV7, None, ALU.mult))
        P.drain("vector")
        P.emit("vector", lambda v: v.reciprocal(inv_col[:, 0:1], scale_col[:, 0:1]), inc="SV")
        scaleinv_tok = P.cnt["SV"]
        bank_gate["R0"].append(("SV", scaleinv_tok))

        split_in_tok, split_sg_tok, split_out_tok_l = {}, {}, {}
        NV_SPLIT = 10

        def vsplit_stage_in(u):
            stg = (p2 if u % 2 == 0 else acc)[:].rearrange("p a b -> p (a b)")
            P.wait("scalar", "SV", amax_tok)
            P.emit("scalar", lambda s, u=u, stg=stg: s.dma_start(out=stg[:, 0:4096], in_=hstream[u]),
                   inc="SD", n=16)
            split_in_tok[u] = P.cnt["SD"]

        vsplit_stage_in(0)
        vsplit_stage_in(1)
        hi_v = q1h[:].rearrange("p a b -> p (a b)")
        lo_v = q1l[:].rearrange("p a b -> p (a b)")
        for u in range(NV_SPLIT):
            stg = (p2 if u % 2 == 0 else acc)[:].rearrange("p a b -> p (a b)")
            P.wait("vector", "SD", split_in_tok[u])
            if u >= 1:
                P.wait("vector", "SD", split_out_tok_l[u - 1])
            P.emit("vector", lambda v, stg=stg: v.tensor_copy(hi_v[:, 0:4096], stg[:, 0:4096]))
            P.emit("vector", lambda v, stg=stg: v.tensor_sub(lo_v[:, 0:4096], stg[:, 0:4096], hi_v[:, 0:4096]), inc="SV")
            vtok = P.cnt["SV"]
            P.wait("scalar", "SV", vtok)
            P.emit("scalar", lambda s, u=u: s.dma_start(out=wsp_hi[u], in_=hi_v[:, 0:4096]), inc="SD", n=16)
            P.emit("scalar", lambda s, u=u: s.dma_start(out=wsp_lo[u], in_=lo_v[:, 0:4096]), inc="SD", n=16)
            split_out_tok[u] = P.cnt["SD"]
            split_out_tok_l[u] = P.cnt["SD"]
            if u + 2 < NV_SPLIT:
                vsplit_stage_in(u + 2)
        vsplit_done_sv = P.cnt["SV"]
        q1_free_tok = split_out_tok[NV_SPLIT - 1]

        hi_g = mdh[:].rearrange("p a b -> p (a b)")
        lo_g = mdl[:].rearrange("p a b -> p (a b)")
        for u in range(NV_SPLIT, NW):
            for hf in range(2):
                o0, o1 = hf * 2048, (hf + 1) * 2048
                P.emit("gpsimd", lambda g, u=u, o0=o0, o1=o1: g.dma_start(
                    out=gstg[:, 0:2048], in_=hstream[u][:, o0:o1]), inc="SO", n=16)
                P.wait("gpsimd", "SO", P.cnt["SO"])
                P.emit("gpsimd", lambda g, o0=o0, o1=o1: g.tensor_copy(hi_g[:, o0:o1], gstg[:, 0:2048]))
                P.drain("gpsimd")
                P.emit("gpsimd", lambda g, o0=o0, o1=o1: g.tensor_sub(lo_g[:, o0:o1], gstg[:, 0:2048], hi_g[:, o0:o1]))
                P.drain("gpsimd")
            P.emit("gpsimd", lambda g, u=u: g.dma_start(out=wsp_hi[u], in_=hi_g[:, 0:4096]), inc="SO", n=16)
            P.emit("gpsimd", lambda g, u=u: g.dma_start(out=wsp_lo[u], in_=lo_g[:, 0:4096]), inc="SO", n=16)
            P.wait("gpsimd", "SO", P.cnt["SO"])
            split_out_so_tok[u] = P.cnt["SO"]
        gsplit_out_tok_box[0] = P.cnt["SO"]


        load_unit(0)
        load_unit(1)
        load_unit(2)

        def ln_stats(src, s1_extra=(), sq_extra=(), scr_fn=None):
            bank_wait("R0")
            for sem, tok in s1_extra:
                P.wait("tensor", sem, tok)
            for kc in range(KC):
                P.emit("tensor", lambda t, kc=kc, src=src, s=(kc == 0), p=(kc == KC - 1):
                       t.matmul(psR0[0:1, :], ones_col[:], src[:, kc, :], start=s, stop=p),
                       inc="ST" if kc == KC - 1 else None)
            tokS1 = P.cnt["ST"]
            bank_wait("R1")
            sq_mm = []
            for kc in range(KC):
                scr = scr_fn(kc) if scr_fn else (b0 if kc % 2 == 0 else b1)[:]
                if scr_fn is None and kc >= 2:
                    P.wait("scalar", "ST", sq_mm[kc - 2])
                for sem, tok in sq_extra:
                    P.wait("scalar", sem, tok)
                stok = P.emit("scalar", lambda s, kc=kc, src=src, scr=scr:
                              s.activation(scr, src[:, kc, :], AF.Square), inc="SS")
                P.wait("tensor", "SS", stok)
                P.emit("tensor", lambda t, scr=scr, s=(kc == 0), p=(kc == KC - 1):
                       t.matmul(psR1[0:1, :], ones_col[:], scr, start=s, stop=p), inc="ST")
                sq_mm.append(P.cnt["ST"])
            return tokS1, P.cnt["ST"]

        def row_musig(tokS1, tokSQ):
            P.wait("vector", "ST", tokS1)
            P.emit("vector", lambda v: v.tensor_scalar(r0[0:1, :], psR0[0:1, :], INV_D, None, ALU.mult), inc="SV")
            bank_gate["R0"].append(("SV", P.cnt["SV"]))
            P.wait("vector", "ST", tokSQ)
            P.emit("vector", lambda v: v.tensor_scalar(r1[0:1, :], psR1[0:1, :], INV_D, None, ALU.mult), inc="SV")
            bank_gate["R1"].append(("SV", P.cnt["SV"]))
            P.emit("vector", lambda v: v.tensor_mul(r3[0:1, :], r0[0:1, :], r0[0:1, :]))
            P.emit("vector", lambda v: v.tensor_sub(r1[0:1, :], r1[0:1, :], r3[0:1, :]), inc="SV")
            var_tok = P.cnt["SV"]
            P.wait("scalar", "SV", var_tok)
            sq_tok = P.emit("scalar", lambda s: s.activation(
                r2[0:1, :], r1[0:1, :], AF.Sqrt, bias=eps_s[0:1, 0:1], scale=1.0), inc="SS")
            P.wait("vector", "SS", sq_tok)
            P.emit("vector", lambda v: v.reciprocal(r2[0:1, :], r2[0:1, :]), inc="SV")
            return P.cnt["SV"]

        def bcast(row_ap, bk, gates):
            bank_wait(bk)
            for sem, tok in gates:
                P.wait("tensor", sem, tok)
            P.emit("tensor", lambda t, row_ap=row_ap, bk=bk:
                   t.matmul(banks[bk][:], ones_row[0:1, :], row_ap, start=True, stop=True), inc="ST")
            return P.cnt["ST"]

        def split_pair(src_chunk_fn, dsth, dstl, idxs):
            for i2 in idxs:
                src = src_chunk_fn(i2)
                P.emit("vector", lambda v, src=src, dsth=dsth, i2=i2: v.tensor_copy(dsth[:, i2, :], src))
                P.emit("vector", lambda v, src=src, dsth=dsth, dstl=dstl, i2=i2:
                       v.tensor_sub(dstl[:, i2, :], src, dsth[:, i2, :]),
                       inc="SV" if i2 == idxs[-1] else None)
            return P.cnt["SV"]

        qz = {}
        pv0 = p0[:].rearrange("p a b -> p (a b)")
        qo_flat = q0l[:].rearrange("p a b -> p (a b)")

        def q_stage(pc):
            half, sub = divmod(pc, 4)
            c0 = half * 4096 + sub * 1024
            P.wait("sync", "SV", qz["gate"])
            P.emit("sync", lambda s, c0=c0: s.dma_start(
                out=pv0[:, 0:1024], in_=wk_blk[:, c0:c0 + 1024]), inc="SW", n=16)
            qz["ld"] = P.cnt["SW"]

        def q_piece(pc):
            half, sub = divmod(pc, 4)
            c0 = sub * 1024
            if pc == 4:
                P.wait("vector", "SD", qz["out0"])
            P.wait("vector", "SW", qz["ld"])
            P.emit("vector", lambda v: v.tensor_scalar(
                pv0[:, 0:1024], pv0[:, 0:1024], inv_col[:, 0:1], None, ALU.mult))
            P.emit("vector", lambda v: v.tensor_scalar(
                pv0[:, 1024:2048], pv0[:, 0:1024], -7.5, None, ALU.is_ge))
            for k in range(-6, 8):
                P.emit("vector", lambda v, thr=k - 0.5: v.scalar_tensor_tensor(
                    pv0[:, 1024:2048], pv0[:, 0:1024], thr, pv0[:, 1024:2048], ALU.is_ge, ALU.add))
            P.emit("vector", lambda v, c0=c0: v.tensor_scalar(
                qo_flat[:, c0:c0 + 1024], pv0[:, 1024:2048], -8.0, None, ALU.add), inc="SV")
            qz["gate"] = P.cnt["SV"]
            if pc == 3:
                P.wait("scalar", "SV", qz["gate"])
                P.emit("scalar", lambda s: s.dma_start(
                    out=wq_dram[:, 0:4096], in_=qo_flat[:, 0:4096]), inc="SD", n=16)
                qz["out0"] = P.cnt["SD"]
            if pc + 1 < 8:
                q_stage(pc + 1)

        def q_out_and_xns():
            P.wait("scalar", "SV", qz["gate"])
            P.emit("scalar", lambda s: s.dma_start(
                out=wq_dram[:, 4096:8192], in_=qo_flat[:, 0:4096]), inc="SD", n=16)
            wq_out_tok_box[0] = P.cnt["SD"]
            P.wait("vector", "SD", wq_out_tok_box[0])
            for kc in range(KC):
                P.emit("vector", lambda v, kc=kc: v.tensor_scalar(
                    q0h[:, kc, :], p1[:, kc, :], scale_col[:, 0:1], None, ALU.mult))
                P.emit("vector", lambda v, kc=kc: v.scalar_tensor_tensor(
                    q0l[:, kc, :], p1[:, kc, :], scale_col[:, 0:1], q0h[:, kc, :],
                    ALU.mult, ALU.subtract), inc="SV" if kc == KC - 1 else None)
            qz["xns"] = P.cnt["SV"]
            return qz["xns"]

        x_tok = {0: consts_tok}
        prev_out_so = 0
        prev_ffn_tok = 0
        prev_mlp1_tok = 0
        prev_xns_tok = 0
        prev_tokr = 0
        prev_hp_sv = 0
        pend_sq, pend_stats, pend_xn2p = {}, {}, {}

        def pairs_of(ti):
            return ((q0h, q0l, q1h, q1l) if ti % 2 == 0
                    else (q1h, q1l, q0h, q0l))

        def emit_p0_squares(ti):
            P.wait("scalar", "SX", x_tok[ti])
            P.wait("scalar", "SV", prev_hp_sv)
            if ti == 1:
                P.wait("scalar", "SD", wq_out_tok_box[0])
            toks = []
            for kc in range(KC):
                P.emit("scalar", lambda s, kc=kc: s.activation(
                    p0[:, kc, :], p3[:, kc, :], AF.Square), inc="SS")
                toks.append(P.cnt["SS"])
            pend_sq[ti] = toks

        def emit_p0_stats(ti):
            P.wait("tensor", "SX", x_tok[ti])
            bank_wait("R0")
            for kc in range(KC):
                P.emit("tensor", lambda t, kc=kc, s=(kc == 0), p=(kc == KC - 1):
                       t.matmul(psR0[0:1, :], ones_col[:], p3[:, kc, :], start=s, stop=p),
                       inc="ST" if kc == KC - 1 else None)
            tokS1 = P.cnt["ST"]
            bank_wait("R1")
            for kc in range(KC):
                P.wait("tensor", "SS", pend_sq[ti][kc])
                P.emit("tensor", lambda t, kc=kc, s=(kc == 0), p=(kc == KC - 1):
                       t.matmul(psR1[0:1, :], ones_col[:], p0[:, kc, :], start=s, stop=p),
                       inc="ST" if kc == KC - 1 else None)
            tokSQ = P.cnt["ST"]
            rstd_tok = row_musig(tokS1, tokSQ)
            P.emit("vector", lambda v: v.tensor_mul(r1[0:1, :], r1[0:1, :], r2[0:1, :]))
            P.emit("vector", lambda v: v.tensor_mul(r1[0:1, :], r1[0:1, :], r2[0:1, :]), inc="SV")
            v1_tok = P.cnt["SV"]
            P.wait("scalar", "SV", v1_tok)
            sq1_tok = P.emit("scalar", lambda s: s.activation(
                r1[0:1, :], r1[0:1, :], AF.Sqrt, bias=eps_s[0:1, 0:1], scale=1.0), inc="SS")
            P.wait("vector", "SS", sq1_tok)
            P.emit("vector", lambda v: v.reciprocal(r1[0:1, :], r1[0:1, :]))
            P.drain("vector")
            P.emit("vector", lambda v: v.tensor_mul(r3[0:1, :], r2[0:1, :], r1[0:1, :]), inc="SV")
            pend_stats[ti] = (rstd_tok, P.cnt["SV"])

        pend_bc = {}

        def emit_p1_bcast(ti):
            rstd_tok, r2c_tok = pend_stats[ti]
            pend_bc[ti] = {"mu": bcast(r0[0:1, :], "R0", [("SV", rstd_tok)]),
                           "r2c": bcast(r3[0:1, :], "R1", [("SV", r2c_tok)]),
                           "rstd_tok": rstd_tok}

        def p1_weave(ti, fc):
            qAh, qAl, _, _ = pairs_of(ti)
            bc = pend_bc[ti]
            if fc == 0:
                P.wait("vector", "ST", bc["mu"])
                P.wait("vector", "ST", prev_tokr)
                for kc in range(KC):
                    P.emit("vector", lambda v, kc=kc: v.tensor_sub(p1[:, kc, :], p3[:, kc, :], psR0[:, :]),
                           inc="SV" if kc == KC - 1 else None)
                bc["s"] = P.cnt["SV"]
                bank_gate["R0"].append(("SV", bc["s"]))
            else:
                if fc == 1:
                    P.wait("vector", "ST", bc["r2c"])
                    P.wait("vector", "ST", prev_mlp1_tok)
                for kc in ([fc - 1] if fc < 7 else [6, 7]):
                    P.emit("vector", lambda v, kc=kc: v.tensor_mul(p0[:, kc, :], p1[:, kc, :], psR1[:, :]))
                    P.emit("vector", lambda v, kc=kc: v.tensor_scalar(
                        p0[:, kc, :], p0[:, kc, :], n1w_s[:, kc:kc + 1], n1b_s[:, kc:kc + 1], ALU.mult, ALU.add))
                    P.emit("vector", lambda v, kc=kc, qAh=qAh: v.tensor_copy(qAh[:, kc, :], p0[:, kc, :]))
                    P.emit("vector", lambda v, kc=kc, qAh=qAh, qAl=qAl:
                           v.tensor_sub(qAl[:, kc, :], p0[:, kc, :], qAh[:, kc, :]),
                           inc="SV" if (fc == 7 and kc == 7) else None)
                if fc == 7:
                    pend_xn2p[ti] = P.cnt["SV"]
                    bank_gate["R1"].append(("SV", pend_xn2p[ti]))

        def emit_p1_tail(ti):
            rstd_b = bcast(r2[0:1, :], "R0", [("SV", pend_bc[ti]["rstd_tok"])])
            P.wait("vector", "ST", rstd_b)
            for kc in range(KC):
                P.emit("vector", lambda v, kc=kc: v.tensor_mul(p1[:, kc, :], p1[:, kc, :], psR0[:, :]),
                       inc="SV" if kc == KC - 1 else None)
            bank_gate["R0"].append(("SV", P.cnt["SV"]))
            if dbg and ti == 0:
                dump("xn", p1[:].rearrange("p a b -> p (a b)"), [("SV", P.cnt["SV"])])
                dump("xn2", p0[:].rearrange("p a b -> p (a b)"), [("SV", pend_xn2p[ti])], blockers=("vector",))

        P.barrier()
        emit_p0_squares(0)
        emit_p0_stats(0)
        emit_p1_bcast(0)
        for fc0 in range(8):
            p1_weave(0, fc0)
        emit_p1_tail(0)

        for i in range(NT):
            first_in_batch = (i % 4 == 0)
            qAh, qAl, qBh, qBl = pairs_of(i)
            xn2p_tok = pend_xn2p[i]

            scr_lo_tok = {}
            gg_tok = 0
            for q in range(4):
                use_unit(i, q)
                uvh, uvl = unit_views(i, q)
                for fo in range(4):
                    fcg = q * 4 + fo
                    bk = "A" if fcg % 2 == 0 else "B"
                    tok = mm_group(bk, mm3(uvh, uvl, (fo * 128, (fo + 1) * 128), qAh, qAl),
                                   extra_gate=[("SV", xn2p_tok)] if fcg == 0 else None)
                    if fcg < 8:
                        evac("scalar", bk, lambda s, bk=bk, tc=fcg, fcg=fcg:
                             s.activation(p3[:, tc, :], banks[bk][:], AF.Silu,
                                          bias=gate_b_s[:, fcg:fcg + 1], scale=1.0), tok)
                    else:
                        tc = fcg - 8
                        scr = b0 if tc % 2 == 0 else b1
                        if i == 0 and tc == 0:
                            P.wait("scalar", "SD", wq_out_tok_box[0])
                            P.wait("vector", "SD", wq_out_tok_box[0])
                        if tc == 0:
                            P.wait("scalar", "ST", prev_ffn_tok)
                        P.wait("scalar", "ST", tok)
                        if tc >= 2:
                            P.wait("scalar", "SV", scr_lo_tok[tc - 2])
                        st1 = P.emit("scalar", lambda s, bk=bk, scr=scr, fcg=fcg:
                                     s.activation(scr[:], banks[bk][:], AF.Silu,
                                                  bias=gate_b_s[:, fcg:fcg + 1], scale=1.0), inc="SS")
                        bank_gate[bk].append(("SS", st1))
                        P.emit("scalar", lambda s, scr=scr, tc=tc, qBh=qBh: s.copy(qBh[:, tc, :], scr[:]), inc="SS")
                        st2 = P.cnt["SS"]
                        P.wait("vector", "SS", st2)
                        gg_tok = P.emit("vector", lambda v, scr=scr, tc=tc, qBh=qBh, qBl=qBl:
                                        v.tensor_sub(qBl[:, tc, :], scr[:], qBh[:, tc, :]), inc="SV")
                        scr_lo_tok[tc] = gg_tok
                done_unit(i, q)
            silu_all_s = P.cnt["SS"]

            use_unit(i, 4)
            use_unit(i, 5)
            if i == 0:
                P.wait("vector", "SW", unit_dma_cum[5])
                u4h0, u4l0 = unit_views(0, 4)
                u5h0, u5l0 = unit_views(0, 5)
                for kc in range(KC):
                    P.emit("vector", lambda v, kc=kc, u4h0=u4h0: v.reduce_sum(swp_s[:, kc:kc + 1], u4h0[:, kc, :], axis=AX.X))
                    P.emit("vector", lambda v, kc=kc, u5h0=u5h0: v.reduce_sum(swp2_s[:, kc:kc + 1], u5h0[:, kc, :], axis=AX.X))
                P.drain("vector")
                P.emit("vector", lambda v: v.tensor_add(swm_s[:], swp_s[:], swp2_s[:]))
                for kc in range(KC):
                    P.emit("vector", lambda v, kc=kc, u4l0=u4l0: v.reduce_sum(swp_s[:, kc:kc + 1], u4l0[:, kc, :], axis=AX.X))
                    P.emit("vector", lambda v, kc=kc, u5l0=u5l0: v.reduce_sum(swp2_s[:, kc:kc + 1], u5l0[:, kc, :], axis=AX.X))
                P.drain("vector")
                P.emit("vector", lambda v: v.tensor_add(swp_s[:], swp_s[:], swp2_s[:]))
                P.drain("vector")
                P.emit("vector", lambda v: v.tensor_add(swm_s[:], swm_s[:], swp_s[:]))
                P.drain("vector")
                P.emit("vector", lambda v: v.tensor_scalar(swm_s[:], swm_s[:], INV_D, None, ALU.mult), inc="SV")
                P.wait("tensor", "SV", P.cnt["SV"])
            bank_wait("R0")
            for kc in range(KC):
                P.emit("tensor", lambda t, kc=kc, s=(kc == 0), p=(kc == KC - 1):
                       t.matmul(psR0[0:1, :], swm_s[:, kc:kc + 1], p0[:, kc, :], start=s, stop=p),
                       inc="ST" if kc == KC - 1 else None)
            dmv_tok = P.cnt["ST"]
            P.wait("vector", "ST", dmv_tok)
            P.wait("vector", "SS", silu_all_s)
            for kc in range(KC):
                scr = b0 if kc % 2 == 0 else b1
                P.emit("vector", lambda v, kc=kc, scr=scr: v.tensor_mul(scr[:], p0[:, kc, :], p3[:, kc, :]))
                P.emit("vector", lambda v, kc=kc, scr=scr, qAh=qAh: v.tensor_copy(qAh[:, kc, :], scr[:]))
                P.emit("vector", lambda v, kc=kc, scr=scr, qAh=qAh, qAl=qAl: v.tensor_sub(qAl[:, kc, :], scr[:], qAh[:, kc, :]),
                       inc="SV" if kc == KC - 1 else None)
            xgp_tok = P.cnt["SV"]
            P.wait("scalar", "ST", dmv_tok)
            da_tok = P.emit("scalar", lambda s: s.activation(
                r0[0:1, :], psR0[0:1, :], AF.Sigmoid, bias=sbm_s[0:1, 0:1], scale=-1.0), inc="SS")
            bank_gate["R0"].append(("SS", da_tok))
            ln_tok = P.emit("scalar", lambda s: s.activation(r1[0:1, :], r0[0:1, :], AF.Ln), inc="SS")
            P.wait("vector", "SS", ln_tok)
            P.emit("vector", lambda v: v.tensor_scalar(r1[0:1, :], r1[0:1, :], -1.0, None, ALU.mult), inc="SV")
            dl_tok = P.cnt["SV"]

            u4h, u4l = unit_views(i, 4)
            u5h, u5l = unit_views(i, 5)
            for j in range(KC):
                uvh, uvl = (u4h, u4l) if j < 4 else (u5h, u5l)
                bk = "A" if j % 2 == 0 else "B"
                tok = mm_group(bk, mm3(uvh, uvl, ((j % 4) * 128, (j % 4 + 1) * 128), qBh, qBl),
                               extra_gate=[("SV", gg_tok), ("SS", silu_all_s)] if j == 0 else None)
                evac("vector", bk, lambda v, bk=bk, j=j: v.tensor_scalar(
                    p3[:, j, :], banks[bk][:], state_b_s[:, j:j + 1], None, ALU.add), tok,
                    extra=[("SV", xgp_tok)] if j == 0 else ())

            dlb = bcast(r1[0:1, :], "R0", [("SV", dl_tok)])
            dab = bcast(r0[0:1, :], "R1", [("SS", da_tok)])

            P.wait("vector", "ST", dlb)
            P.wait("vector", "ST", dab)

            def chain_step(j):
                if 1 <= j <= 8:
                    c = j - 1
                    P.emit("vector", lambda v, c=c: v.tensor_mul(p0[:, c, :], p0[:, c, :], p3[:, c, :]))
                    P.emit("vector", lambda v, c=c: v.tensor_mul(p0[:, c, :], p0[:, c, :], psR0[:, :]))
                if 2 <= j <= 9:
                    c = j - 2
                    init = 0.0 if first_in_batch else carry[:, c:c + 1]
                    P.emit("vector", lambda v, c=c, init=init: v.tensor_tensor_scan(
                        p3[:, c, :], psR1[:, :], p0[:, c, :], init, ALU.mult, ALU.add))
                if 3 <= j <= 10:
                    c = j - 3
                    P.emit("vector", lambda v, c=c, qBh=qBh: v.tensor_copy(qBh[:, c, :], p3[:, c, :]))
                    P.emit("vector", lambda v, c=c, qBh=qBh, qBl=qBl: v.tensor_sub(qBl[:, c, :], p3[:, c, :], qBh[:, c, :]))
                if 4 <= j <= 11:
                    c = j - 4
                    P.emit("vector", lambda v, c=c: v.tensor_copy(carry[:, c:c + 1], p3[:, c, T - 1:T]),
                           inc="SV" if j == 11 else None)

            for j in range(KC):
                uvh, uvl = (u4h, u4l) if j < 4 else (u5h, u5l)
                bk = "A" if j % 2 == 0 else "B"
                tok = mm_group(bk, mm3(uvh, uvl, ((j % 4) * 128, (j % 4 + 1) * 128), qAh, qAl),
                               extra_gate=[("SV", xgp_tok)] if j == 0 else None)
                chain_step(j)
                evac("vector", bk, lambda v, bk=bk, j=j: v.tensor_scalar(
                    p0[:, j, :], banks[bk][:], state_b_s[:, j:j + 1], None, ALU.add), tok)
                if j == 3:
                    done_unit(i, 4)
            done_unit(i, 5)
            for j in range(KC, 12):
                chain_step(j)
            hp_tok = P.cnt["SV"]
            carry_tok = hp_tok
            prev_hp_sv = hp_tok
            bank_gate["R0"].append(("SV", hp_tok))
            bank_gate["R1"].append(("SV", hp_tok))
            if dbg and i == 0:
                dump("dA_row", r0[0:1, :], [("SS", da_tok)], blockers=("vector",))
                dump("delta_row", r1[0:1, :], [("SV", dl_tok)], blockers=("vector",))
                dump("swm", swm_s[:], [("SV", dl_tok)])
            if i == 0:
                qz["gate"] = hp_tok
                q_stage(0)

            use_unit(i, 6)
            use_unit(i, 7)
            u6h, u6l = unit_views(i, 6)
            u7h, u7l = unit_views(i, 7)
            for fc in range(KC):
                uvh, uvl = (u6h, u6l) if fc < 4 else (u7h, u7l)
                bk = "A" if fc % 2 == 0 else "B"
                tok = mm_group(bk, mm3(uvh, uvl, ((fc % 4) * 128, (fc % 4 + 1) * 128), qBh, qBl),
                               extra_gate=[("SV", hp_tok)] if fc == 0 else None)
                evac("vector", bk, lambda v, bk=bk, fc=fc: v.scalar_tensor_tensor(
                    p2[:, fc, :], banks[bk][:], out_b_s[:, fc:fc + 1], p1[:, fc, :], ALU.add, ALU.add), tok,
                    extra=[("SO", prev_out_so)] if fc == 0 else ())
                if fc == 3:
                    done_unit(i, 6)
            done_unit(i, 7)
            ssm_tok = P.cnt["ST"]
            xmid_tok = P.cnt["SV"]

            P.wait("gpsimd", "ST", ssm_tok)
            P.wait("gpsimd", "SV", carry_tok)
            P.emit("gpsimd", lambda g, i=i: g.dma_start(
                out=p3[:], in_=x_tl[i].rearrange("p (a b) -> p a b", b=T)), inc="SX", n=16)
            xrl_tok = P.cnt["SX"]

            tokS1b, tokSQb = ln_stats(p2, s1_extra=[("SV", xmid_tok)], sq_extra=[("SV", xmid_tok)])
            rstd2_tok = row_musig(tokS1b, tokSQb)
            mu2_b = bcast(r0[0:1, :], "R0", [("SV", rstd2_tok)])
            rstd2_b = bcast(r2[0:1, :], "R1", [("SV", rstd2_tok)])
            bank_wait("B")
            for kc in range(KC):
                P.emit("tensor", lambda t, kc=kc, s=(kc == 0), p=(kc == KC - 1):
                       t.matmul(psB[0:16, :], a_s[:, kc, :], p1[:, kc, :], start=s, stop=p),
                       inc="ST" if kc == KC - 1 else None)
            tokr = P.cnt["ST"]
            P.wait("vector", "ST", rstd2_b)
            for kc in range(KC):
                scr = b0 if kc % 2 == 0 else b1
                P.emit("vector", lambda v, kc=kc, scr=scr: v.tensor_sub(scr[:], p2[:, kc, :], psR0[:, :]))
                P.emit("vector", lambda v, kc=kc, scr=scr: v.tensor_mul(scr[:], scr[:], psR1[:, :]))
                P.emit("vector", lambda v, kc=kc, scr=scr: v.tensor_scalar(
                    scr[:], scr[:], n2w_s[:, kc:kc + 1], n2b_s[:, kc:kc + 1], ALU.mult, ALU.add))
                P.emit("vector", lambda v, kc=kc, scr=scr, qBh=qBh: v.tensor_copy(qBh[:, kc, :], scr[:]))
                P.emit("vector", lambda v, kc=kc, scr=scr, qBh=qBh, qBl=qBl: v.tensor_sub(qBl[:, kc, :], scr[:], qBh[:, kc, :]),
                       inc="SV" if kc == KC - 1 else None)
            x2p_tok = P.cnt["SV"]
            bank_gate["R0"].append(("SV", x2p_tok))
            bank_gate["R1"].append(("SV", x2p_tok))
            if dbg and i == 0:
                dump("x2h", qBh[:].rearrange("p a b -> p (a b)"), [("SV", x2p_tok)], blockers=("vector",))

            P.wait("vector", "ST", tokr)
            rs_tok = P.emit("vector", lambda v: v.tensor_copy(r_s[:], psB[0:16, :]), inc="SV")
            bank_gate["B"].append(("SV", rs_tok))
            prev_tokr = tokr

            if i > 0:
                for kc in range(KC):
                    P.emit("vector", lambda v, kc=kc, qAh=qAh: v.tensor_scalar(
                        qAh[:, kc, :], p1[:, kc, :], scale_col[:, 0:1], None, ALU.mult))
                    P.emit("vector", lambda v, kc=kc, qAh=qAh, qAl=qAl: v.scalar_tensor_tensor(
                        qAl[:, kc, :], p1[:, kc, :], scale_col[:, 0:1], qAh[:, kc, :],
                        ALU.mult, ALU.subtract), inc="SV" if kc == KC - 1 else None)
                xnp_tok = P.cnt["SV"]
            else:
                xnp_tok = None

            P.wait("vector", "SX", xrl_tok)
            for kc in range(KC):
                P.emit("vector", lambda v, kc=kc: v.tensor_add(p2[:, kc, :], p3[:, kc, :], p2[:, kc, :]),
                       inc="SV" if kc == KC - 1 else None)
            xmm_tok = P.cnt["SV"]
            if i == 0:
                q_piece(0)
            if dbg and i == 0:
                dump("h_hi", qBh[:].rearrange("p a b -> p (a b)"), [("SV", hp_tok)])
                dump("x_mid", p2[:].rearrange("p a b -> p (a b)"), [("SV", xmid_tok)], blockers=("vector",))
                dump("r_s", r_s[:], [("SV", rs_tok)])

            if i + 1 < NT:
                P.wait("gpsimd", "SV", xmm_tok)
                P.emit("gpsimd", lambda g, i=i: g.dma_start(
                    out=p3[:], in_=x_tl[i + 1].rearrange("p (a b) -> p a b", b=T)), inc="SX", n=16)
                x_tok[i + 1] = P.cnt["SX"]

            wqv, _ = unit_views(i, 24)
            mid_lo_tok = {}
            for hg in range(4):
                ua, ub = 8 + hg * 4, 9 + hg * 4
                use_unit(i, ua)
                use_unit(i, ub)
                w1ah, w1al = unit_views(i, ua)
                w1bh, w1bl = unit_views(i, ub)
                for hc in range(8):
                    hcg = hg * 8 + hc
                    uvh, uvl = (w1ah, w1al) if hc < 4 else (w1bh, w1bl)
                    bk = "A" if hc % 2 == 0 else "B"
                    tok = mm_group(bk, mm3(uvh, uvl, ((hc % 4) * 128, (hc % 4 + 1) * 128), qBh, qBl),
                                   extra_gate=[("SV", x2p_tok)] if (hg == 0 and hc == 0) else None)
                    scr = b0 if hc % 2 == 0 else b1
                    if i == 0 and hcg == 0:
                        P.wait("scalar", "SO", gsplit_out_tok_box[0])
                        P.wait("vector", "SO", gsplit_out_tok_box[0])
                    if hcg < 2:
                        P.wait("scalar", "SV", xnp_tok)
                    P.wait("scalar", "ST", tok)
                    if hcg >= 2:
                        P.wait("scalar", "SV", mid_lo_tok[hcg - 2])
                    gt = P.emit("scalar", lambda s, bk=bk, scr=scr, hcg=hcg:
                                s.activation(scr[:], banks[bk][:], AF.Gelu,
                                             bias=mlp_b1_s[:, hcg:hcg + 1], scale=1.0), inc="SS")
                    bank_gate[bk].append(("SS", gt))
                    P.emit("scalar", lambda s, scr=scr, hc=hc: s.copy(mdh[:, hc, :], scr[:]), inc="SS")
                    st2 = P.cnt["SS"]
                    P.wait("vector", "SS", st2)
                    mid_lo_tok[hcg] = P.emit("vector", lambda v, scr=scr, hc=hc:
                                             v.tensor_sub(mdl[:, hc, :], scr[:], mdh[:, hc, :]), inc="SV")
                    if hc == 3:
                        done_unit(i, ua)
                mid_tok = P.cnt["SV"]
                if i == 0 and hg < 3:
                    q_piece(1 + hg * 2)
                if dbg and i == 0 and hg == 0:
                    dump("mid0h", mdh[:].rearrange("p a b -> p (a b)"), [("SV", mid_tok)], blockers=("scalar", "vector"))
                    dump("mid0l", mdl[:].rearrange("p a b -> p (a b)"), [("SV", mid_tok)], blockers=("scalar", "vector"))
                if hg == 3:
                    prev_mlp1_tok = P.cnt["ST"]
                    if i + 1 < NT:
                        emit_p0_stats(i + 1)
                        emit_p1_bcast(i + 1)
                done_unit(i, ub)
                uc, ud = 10 + hg * 4, 11 + hg * 4
                use_unit(i, uc)
                use_unit(i, ud)
                w2ah, w2al = unit_views(i, uc)
                w2bh, w2bl = unit_views(i, ud)
                if hg == 3:
                    use_unit(i, 24)
                for fc in range(8):
                    uvh, uvl = (w2ah, w2al) if fc < 4 else (w2bh, w2bl)
                    bk = "A" if fc % 2 == 0 else "B"
                    mms = mm3(uvh, uvl, ((fc % 4) * 128, (fc % 4 + 1) * 128), mdh, mdl)
                    if hg == 3:
                        for kc in range(KC):
                            mms.append(((lambda wqv=wqv, kc=kc, fc=fc: wqv[:, kc, fc * 128:(fc + 1) * 128]),
                                        (lambda kc=kc, qAh=qAh: qAh[:, kc, :])))
                            mms.append(((lambda wqv=wqv, kc=kc, fc=fc: wqv[:, kc, fc * 128:(fc + 1) * 128]),
                                        (lambda kc=kc, qAl=qAl: qAl[:, kc, :])))
                        mms.append(((lambda fc=fc: bt_s[:, fc, :]), (lambda: r_s[:])))
                    eg = [("SV", mid_tok)] if fc == 0 else []
                    if hg == 3 and fc == 0:
                        eg += [("SV", xnp_tok), ("SV", rs_tok)]
                    tok = mm_group(bk, mms, extra_gate=eg or None)
                    if hg == 0:
                        evac("vector", bk, lambda v, bk=bk, fc=fc: v.tensor_copy(
                            acc[:, fc, :], banks[bk][:]), tok)
                    else:
                        evac("vector", bk, lambda v, bk=bk, fc=fc: v.tensor_add(
                            acc[:, fc, :], acc[:, fc, :], banks[bk][:]), tok)
                    if fc == 3:
                        done_unit(i, uc)
                    if hg == 3 and i + 1 < NT:
                        p1_weave(i + 1, fc)
                done_unit(i, ud)
                if i == 0:
                    if hg < 2:
                        q_piece(2 + hg * 2)
                    elif hg == 2:
                        q_piece(6)
                        q_piece(7)
                        xnp_tok = q_out_and_xns()
                if hg == 2 and i + 1 < NT:
                    emit_p0_squares(i + 1)
                if hg == 3:
                    prev_ffn_tok = P.cnt["ST"]
            done_unit(i, 24)
            if i + 1 < NT:
                emit_p1_tail(i + 1)

            if dbg and i == 0:
                dump("acc", acc[:].rearrange("p a b -> p (a b)"), [("SV", P.cnt["SV"])], blockers=("vector",))
                dump("w10h", wsp_hi[10], [("SV", P.cnt["SV"])])
                dump("w10l", wsp_lo[10], [("SV", P.cnt["SV"])])
                dump("w11h", wsp_hi[11], [("SV", P.cnt["SV"])])
            out_gate = []
            for fc in range(KC):
                P.emit("vector", lambda v, fc=fc: v.scalar_tensor_tensor(
                    p2[:, fc, :], acc[:, fc, :], cb_s[:, fc:fc + 1], p2[:, fc, :], ALU.add, ALU.add),
                       inc="SV")
                out_gate.append(P.cnt["SV"])
            if dbg and i == 0:
                dump("final", p2[:].rearrange("p a b -> p (a b)"), [("SV", out_gate[-1])], blockers=("vector",))
            for half in range(2):
                P.wait("gpsimd", "SV", out_gate[4 * half + 3])
                P.emit("gpsimd", lambda g, i=i, half=half: g.dma_start(
                    out=out_tl[i][:, half * 2048:(half + 1) * 2048],
                    in_=p2[:, 4 * half:4 * half + 4, :].rearrange("p a b -> p (a b)")), inc="SO", n=16)
            prev_out_so = P.cnt["SO"]

        P.wait("gpsimd", "SO", P.cnt["SO"])

        def runner(eng_name):
            drain_incs = eng_name in ("tensor", "vector", "scalar", "gpsimd")

            def body(eng):
                for op in P.ops[eng_name]:
                    if op[0] == "w":
                        eng.wait_ge(sems[op[1]], op[2])
                    elif op[0] == "d":
                        eng.drain()
                    else:
                        _, fn, inc, n = op
                        instr = fn(eng)
                        if inc is not None and drain_incs and n != 16:
                            instr = eng.drain()
                        if inc is not None:
                            instr.then_inc(sems[inc], n)
            return body

        with nc.Block() as block:
            block.sync(runner("sync"))
            block.gpsimd(runner("gpsimd"))
            block.tensor(runner("tensor"))
            block.vector(runner("vector"))
            block.scalar(runner("scalar"))

    if dbg:
        return nc, dbg_names
    return nc


def _block_unit(w, k0, k1, f0, f1):
    v = w[k0:k1, f0:f1]
    return np.ascontiguousarray(v.reshape((k1 - k0) // 128, 128, f1 - f0).transpose(1, 0, 2)).reshape(128, 4096)


def _col_block(v):
    return np.ascontiguousarray(np.asarray(v, np.float32).reshape(-1, 128).T)


def make_hstream(gw, sw, ow, w1, w2):
    units = []
    for q in range(4):
        units.append(_block_unit(gw, 0, 1024, q * 512, (q + 1) * 512))
    for h in range(2):
        units.append(_block_unit(sw, 0, 1024, h * 512, (h + 1) * 512))
    for h in range(2):
        units.append(_block_unit(ow, 0, 1024, h * 512, (h + 1) * 512))
    for hg in range(4):
        units.append(_block_unit(w1, 0, 1024, hg * 1024, hg * 1024 + 512))
        units.append(_block_unit(w1, 0, 1024, hg * 1024 + 512, hg * 1024 + 1024))
        units.append(_block_unit(w2, hg * 1024, (hg + 1) * 1024, 0, 512))
        units.append(_block_unit(w2, hg * 1024, (hg + 1) * 1024, 512, 1024))
    return np.ascontiguousarray(np.stack(units))


_NC_CACHE = [None]


def kernel(L0_attn_x, L0_attn_norm1_w, L0_attn_norm1_b, L0_attn_gate_w, L0_attn_gate_b,
           L0_attn_state_w, L0_attn_state_b, L0_attn_out_w, L0_attn_out_b,
           L0_attn_norm2_w, L0_attn_norm2_b, L0_attn_mlp_w1, L0_attn_mlp_b1,
           L0_attn_mlp_w2, L0_attn_mlp_b2, L0_ffn_kernel, L0_ffn_bias, L0_ffn_a, L0_ffn_b):
    f = np.float32
    x = np.ascontiguousarray(L0_attn_x, dtype=f)
    hstream = make_hstream(np.asarray(L0_attn_gate_w, f), np.asarray(L0_attn_state_w, f),
                           np.asarray(L0_attn_out_w, f), np.asarray(L0_attn_mlp_w1, f),
                           np.asarray(L0_attn_mlp_w2, f))
    wk = np.asarray(L0_ffn_kernel, f)
    consts = {
        "hstream": hstream,
        "wk_blk": np.ascontiguousarray(
            wk.reshape(8, 128, 1024).transpose(1, 0, 2).reshape(128, 8192)),
        "gate_b_t": _col_block(L0_attn_gate_b),
        "state_b_t": _col_block(L0_attn_state_b),
        "out_b_t": _col_block(L0_attn_out_b),
        "mlp_b1_t": _col_block(L0_attn_mlp_b1),
        "mlp_b2_t": _col_block(L0_attn_mlp_b2),
        "ffn_b_t": _col_block(L0_ffn_bias),
        "n1w_t": _col_block(L0_attn_norm1_w),
        "n1b_t": _col_block(L0_attn_norm1_b),
        "n2w_t": _col_block(L0_attn_norm2_w),
        "n2b_t": _col_block(L0_attn_norm2_b),
        "a_t": np.ascontiguousarray(
            np.asarray(L0_ffn_a, f).reshape(8, 128, 16).transpose(1, 0, 2).reshape(128, 128)),
        "b_t": np.ascontiguousarray(np.asarray(L0_ffn_b, f)),
    }

    in_maps = []
    for c in range(N_CORES):
        xc = x[2 * c:2 * c + 2].reshape(4096, 1024)
        x_tl = np.ascontiguousarray(
            xc.reshape(8, 512, 8, 128).transpose(0, 3, 2, 1).reshape(NT, 128, KC * T))
        m = dict(consts)
        m["x_tl"] = x_tl
        in_maps.append(m)

    if _NC_CACHE[0] is None:
        _NC_CACHE[0] = build_nc()
    res = run_bass_kernel_spmd(_NC_CACHE[0], in_maps, core_ids=list(range(N_CORES)))

    outs = []
    for c in range(N_CORES):
        ot = np.asarray(res.results[c]["out_tl"])
        oc = ot.reshape(8, 128, 8, 512).transpose(0, 3, 2, 1).reshape(4096, 1024)
        outs.append(oc.reshape(2, 2048, 1024))
    return np.ascontiguousarray(np.concatenate(outs, axis=0))


# version 31
# speedup vs baseline: 1.0357x; 1.0110x over previous
import numpy as np
import concourse.bass as bass
import concourse.mybir as mybir
from concourse.bass_utils import run_bass_kernel_spmd

F32 = mybir.dt.float32
BF16 = mybir.dt.bfloat16
AF = mybir.ActivationFunctionType
ALU = mybir.AluOpType
AX = mybir.AxisListType

T = 512
NT = 8
KC = 8
EPS = 1e-5
N_CORES = 8
NW = 24
UPT = 25
NSLOT = 3
INV7 = 1.0 / 7.0
INV_D = 1.0 / 1024.0

ENGS = ("tensor", "vector", "scalar", "sync", "gpsimd")
SEM_OF = {"tensor": "ST", "vector": "SV", "scalar": "SS"}


class Plan:
    def __init__(self):
        self.ops = {e: [] for e in ENGS}
        self.cnt = {"ST": 0, "SV": 0, "SS": 0, "SW": 0, "SX": 0, "SO": 0, "SG": 0, "SD": 0}
        self.last_wait = {}

    def wait(self, eng, sem, val):
        if not val or val <= 0:
            return
        key = (eng, sem)
        if self.last_wait.get(key, 0) >= val:
            return
        self.last_wait[key] = val
        self.ops[eng].append(("w", sem, val))

    def emit(self, eng, fn, inc=None, n=1):
        self.ops[eng].append(("o", fn, inc, n))
        if inc is not None:
            self.cnt[inc] += n
            return self.cnt[inc]
        return None

    def drain(self, eng):
        self.ops[eng].append(("d",))

    def barrier(self):
        for a in ("tensor", "vector", "scalar"):
            for b in ("tensor", "vector", "scalar"):
                if a != b:
                    self.wait(b, SEM_OF[a], self.cnt[SEM_OF[a]])


def build_nc(dbg=False):
    nc = bass.Bass("TRN2", target_bir_lowering=False)

    dbg_t = nc.dram_tensor("dbg", [24, 128, 4096], F32, kind="ExternalOutput").ap() if dbg else None
    dbg_names = []

    x_tl = nc.dram_tensor("x_tl", [NT, 128, KC * T], F32, kind="ExternalInput").ap()
    hstream = nc.dram_tensor("hstream", [NW, 128, 4096], F32, kind="ExternalInput").ap()
    wk_blk = nc.dram_tensor("wk_blk", [128, 8192], F32, kind="ExternalInput").ap()
    gate_b_t = nc.dram_tensor("gate_b_t", [128, 16], F32, kind="ExternalInput").ap()
    state_b_t = nc.dram_tensor("state_b_t", [128, 8], F32, kind="ExternalInput").ap()
    out_b_t = nc.dram_tensor("out_b_t", [128, 8], F32, kind="ExternalInput").ap()
    mlp_b1_t = nc.dram_tensor("mlp_b1_t", [128, 32], F32, kind="ExternalInput").ap()
    mlp_b2_t = nc.dram_tensor("mlp_b2_t", [128, 8], F32, kind="ExternalInput").ap()
    ffn_b_t = nc.dram_tensor("ffn_b_t", [128, 8], F32, kind="ExternalInput").ap()
    n1w_t = nc.dram_tensor("n1w_t", [128, 8], F32, kind="ExternalInput").ap()
    n1b_t = nc.dram_tensor("n1b_t", [128, 8], F32, kind="ExternalInput").ap()
    n2w_t = nc.dram_tensor("n2w_t", [128, 8], F32, kind="ExternalInput").ap()
    n2b_t = nc.dram_tensor("n2b_t", [128, 8], F32, kind="ExternalInput").ap()
    a_t = nc.dram_tensor("a_t", [128, 128], F32, kind="ExternalInput").ap()
    b_t = nc.dram_tensor("b_t", [16, 1024], F32, kind="ExternalInput").ap()
    wq_dram = nc.dram_tensor("wq_dram", [128, 8192], BF16, kind="Internal").ap()
    wsp_hi = nc.dram_tensor("wsp_hi", [NW, 128, 4096], BF16, kind="Internal").ap()
    wsp_lo = nc.dram_tensor("wsp_lo", [NW, 128, 4096], BF16, kind="Internal").ap()
    out_tl = nc.dram_tensor("out_tl", [NT, 128, KC * T], F32, kind="ExternalOutput").ap()

    from contextlib import ExitStack
    with ExitStack() as st:
        e = st.enter_context
        p0 = e(nc.sbuf_tensor("p0", [128, KC, T], F32))
        p1 = e(nc.sbuf_tensor("p1", [128, KC, T], F32))
        p2 = e(nc.sbuf_tensor("p2", [128, KC, T], F32))
        p3 = e(nc.sbuf_tensor("p3", [128, KC, T], F32))
        acc = e(nc.sbuf_tensor("acc", [128, KC, T], F32))
        q0h = e(nc.sbuf_tensor("q0h", [128, KC, T], BF16))
        q0l = e(nc.sbuf_tensor("q0l", [128, KC, T], BF16))
        q1h = e(nc.sbuf_tensor("q1h", [128, KC, T], BF16))
        q1l = e(nc.sbuf_tensor("q1l", [128, KC, T], BF16))
        mdh = e(nc.sbuf_tensor("mdh", [128, KC, T], BF16))
        mdl = e(nc.sbuf_tensor("mdl", [128, KC, T], BF16))
        slots = [e(nc.sbuf_tensor(f"slot{k}", [128, 8192], BF16)) for k in range(NSLOT)]
        b0 = e(nc.sbuf_tensor("b0", [128, T], F32))
        b1 = e(nc.sbuf_tensor("b1", [128, T], F32))
        r0 = e(nc.sbuf_tensor("r0", [1, T], F32))
        r1 = e(nc.sbuf_tensor("r1", [1, T], F32))
        r2 = e(nc.sbuf_tensor("r2", [1, T], F32))
        r3 = e(nc.sbuf_tensor("r3", [1, T], F32))
        srow = e(nc.sbuf_tensor("srow", [1, 128], F32))
        ones_row = e(nc.sbuf_tensor("ones_row", [1, 128], F32))
        ones_col = e(nc.sbuf_tensor("ones_col", [128, 1], F32))
        eps_s = e(nc.sbuf_tensor("eps_s", [1, 1], F32))
        sbm_s = e(nc.sbuf_tensor("sbm_s", [128, 1], F32))
        am_col = e(nc.sbuf_tensor("am_col", [128, 1], F32))
        scale_col = e(nc.sbuf_tensor("scale_col", [128, 1], F32))
        inv_col = e(nc.sbuf_tensor("inv_col", [128, 1], F32))
        carry = e(nc.sbuf_tensor("carry", [128, 8], F32))
        r_s = e(nc.sbuf_tensor("r_s", [16, T], F32))
        gate_b_s = e(nc.sbuf_tensor("gate_b_s", [128, 16], F32))
        state_b_s = e(nc.sbuf_tensor("state_b_s", [128, 8], F32))
        out_b_s = e(nc.sbuf_tensor("out_b_s", [128, 8], F32))
        mlp_b1_s = e(nc.sbuf_tensor("mlp_b1_s", [128, 32], F32))
        b2_s = e(nc.sbuf_tensor("b2_s", [128, 8], F32))
        fb_s = e(nc.sbuf_tensor("fb_s", [128, 8], F32))
        cb_s = e(nc.sbuf_tensor("cb_s", [128, 8], F32))
        n1w_s = e(nc.sbuf_tensor("n1w_s", [128, 8], F32))
        n1b_s = e(nc.sbuf_tensor("n1b_s", [128, 8], F32))
        n2w_s = e(nc.sbuf_tensor("n2w_s", [128, 8], F32))
        n2b_s = e(nc.sbuf_tensor("n2b_s", [128, 8], F32))
        a_s = e(nc.sbuf_tensor("a_s", [128, 8, 16], F32))
        bt_s = e(nc.sbuf_tensor("bt_s", [16, 8, 128], F32))
        swm_s = e(nc.sbuf_tensor("swm_s", [128, 8], F32))
        swp_s = e(nc.sbuf_tensor("swp_s", [128, 8], F32))
        swp2_s = e(nc.sbuf_tensor("swp2_s", [128, 8], F32))
        gstg = e(nc.sbuf_tensor("gstg", [128, 2048], F32))
        psA = e(nc.psum_tensor("psA", [128, T], F32))
        psB = e(nc.psum_tensor("psB", [128, T], F32))
        psR0 = e(nc.psum_tensor("psR0", [128, T], F32))
        psR1 = e(nc.psum_tensor("psR1", [128, T], F32))
        SW = e(nc.semaphore("SW"))
        SX = e(nc.semaphore("SX"))
        SO = e(nc.semaphore("SO"))
        ST = e(nc.semaphore("ST"))
        SV = e(nc.semaphore("SV"))
        SS = e(nc.semaphore("SS"))
        SG = e(nc.semaphore("SG"))
        SD = e(nc.semaphore("SD"))
        sems = {"SW": SW, "SX": SX, "SO": SO, "ST": ST, "SV": SV, "SS": SS,
                "SG": SG, "SD": SD}

        banks = {"A": psA, "B": psB, "R0": psR0, "R1": psR1}
        P = Plan()
        bank_gate = {k: [] for k in banks}
        unit_tok = {}
        unit_dma_cum = {}
        split_out_tok = {}
        split_out_so_tok = {}
        wq_out_tok_box = [0]
        gsplit_out_tok_box = [0]

        def dump(name, src_ap, gates, blockers=()):
            if not dbg:
                return
            k = len(dbg_names)
            dbg_names.append(name)
            for sem, tok in gates:
                P.wait("gpsimd", sem, tok)
            r, c = src_ap.shape[0], src_ap.shape[1]
            P.emit("gpsimd", lambda g, k=k, src_ap=src_ap, r=r, c=c:
                   g.dma_start(out=dbg_t[k][0:r, 0:c], in_=src_ap,
                               allow_slow_non_contiguous=True), inc="SO", n=16)
            tok = P.cnt["SO"]
            for eng in blockers:
                P.wait(eng, "SO", tok)

        def bank_wait(bk):
            for sem, tok in bank_gate[bk]:
                P.wait("tensor", sem, tok)
            bank_gate[bk] = []

        def mm_group(bk, mm_list, extra_gate=None, start=True, stop=True):
            if start:
                bank_wait(bk)
            if extra_gate:
                for sem, tok in extra_gate:
                    P.wait("tensor", sem, tok)
            nmm = len(mm_list)
            for idx, (lf, rf) in enumerate(mm_list):
                last = idx == nmm - 1
                P.emit("tensor",
                       lambda t, lf=lf, rf=rf, bk=bk, s=(start and idx == 0), p=(stop and last):
                       t.matmul(banks[bk][:], lf(), rf(), start=s, stop=p),
                       inc="ST" if (last and stop) else None)
            return P.cnt["ST"]

        def evac(eng, bk, fn, gate_tok, extra=()):
            P.wait(eng, "ST", gate_tok)
            for sem, tok in extra:
                P.wait(eng, sem, tok)
            t = P.emit(eng, fn, inc=SEM_OF[eng])
            bank_gate[bk].append((SEM_OF[eng], t))
            return t

        def load_unit(g):
            tile, j = divmod(g, UPT)
            slot = slots[g % NSLOT]
            P.wait("sync", "ST", unit_tok.get(g - NSLOT, 0))
            if tile == 0:
                if j == 24:
                    P.wait("sync", "SD", wq_out_tok_box[0])
                elif j in split_out_so_tok:
                    P.wait("sync", "SO", split_out_so_tok[j])
                else:
                    P.wait("sync", "SD", split_out_tok.get(j, 0))
            if j == 24:
                P.emit("sync", lambda s, slot=slot: s.dma_start(out=slot[:], in_=wq_dram[:]),
                       inc="SW", n=16)
            else:
                P.emit("sync", lambda s, j=j, slot=slot:
                       s.dma_start(out=slot[:, 0:4096], in_=wsp_hi[j]), inc="SW", n=16)
                P.emit("sync", lambda s, j=j, slot=slot:
                       s.dma_start(out=slot[:, 4096:8192], in_=wsp_lo[j]), inc="SW", n=16)
            unit_dma_cum[g] = P.cnt["SW"]

        def unit_views(tile, j):
            slot = slots[(tile * UPT + j) % NSLOT]
            if j == 24:
                return slot[:].rearrange("p (a b) -> p a b", b=1024), None
            hi = slot[:, 0:4096].rearrange("p (a b) -> p a b", b=512)
            lo = slot[:, 4096:8192].rearrange("p (a b) -> p a b", b=512)
            return hi, lo

        def use_unit(tile, j):
            P.wait("tensor", "SW", unit_dma_cum[tile * UPT + j])

        def done_unit(tile, j):
            g = tile * UPT + j
            unit_tok[g] = P.cnt["ST"]
            if g + NSLOT < NT * UPT:
                load_unit(g + NSLOT)

        def mm3(uvh, uvl, fsl, rh, rl):
            mms = []
            for kc in range(KC):
                wh = (lambda uvh=uvh, kc=kc, fsl=fsl: uvh[:, kc, fsl[0]:fsl[1]])
                wl = (lambda uvl=uvl, kc=kc, fsl=fsl: uvl[:, kc, fsl[0]:fsl[1]])
                ah = (lambda rh=rh, kc=kc: rh[:, kc, :])
                al = (lambda rl=rl, kc=kc: rl[:, kc, :])
                mms += [(wh, ah), (wh, al), (wl, ah)]
            return mms

        P.emit("vector", lambda v: v.memset(ones_row[:], 1.0))
        P.emit("vector", lambda v: v.memset(ones_col[:], 1.0))
        P.emit("vector", lambda v: v.memset(eps_s[:], EPS), inc="SV")

        const_pairs = [
            (gate_b_t, gate_b_s), (state_b_t, state_b_s), (out_b_t, out_b_s),
            (mlp_b1_t, mlp_b1_s), (mlp_b2_t, b2_s), (ffn_b_t, fb_s),
            (n1w_t, n1w_s), (n1b_t, n1b_s), (n2w_t, n2w_s), (n2b_t, n2b_s),
        ]
        for src, dst in const_pairs:
            P.emit("gpsimd", lambda g, src=src, dst=dst:
                   g.dma_start(out=dst[:], in_=src[:]), inc="SX", n=16)
        P.emit("gpsimd", lambda g: g.dma_start(
            out=a_s[:], in_=a_t.rearrange("p (a b) -> p a b", b=16)), inc="SX", n=16)
        P.emit("gpsimd", lambda g: g.dma_start(
            out=bt_s[:], in_=b_t.rearrange("p (a b) -> p a b", b=128)), inc="SX", n=16)
        P.emit("gpsimd", lambda g: g.dma_start(
            out=p3[:], in_=x_tl[0].rearrange("p (a b) -> p a b", b=T)), inc="SX", n=16)
        consts_tok = P.cnt["SX"]

        P.wait("vector", "SX", consts_tok)
        P.emit("vector", lambda v: v.tensor_add(cb_s[:], b2_s[:], fb_s[:]), inc="SV")
        P.emit("vector", lambda v: v.reduce_sum(am_col[:, 0:1], state_b_s[:], axis=AX.X), inc="SV")
        sbsum_tok = P.cnt["SV"]
        P.wait("sync", "SV", sbsum_tok)
        P.emit("sync", lambda s: s.dma_start(out=srow[0:1, :], in_=am_col[:, 0:1]), inc="SX", n=16)
        sbrow_tok = P.cnt["SX"]
        P.wait("tensor", "SX", sbrow_tok)
        P.emit("tensor", lambda t: t.matmul(psR0[:, 0:128], ones_row[0:1, :], srow[0:1, :],
                                            start=True, stop=True), inc="ST")
        sb_mm_tok = P.cnt["ST"]
        P.wait("vector", "ST", sb_mm_tok)
        P.emit("vector", lambda v: v.reduce_sum(sbm_s[:, 0:1], psR0[:, 0:128], axis=AX.X))
        P.drain("vector")
        P.emit("vector", lambda v: v.tensor_scalar(sbm_s[:, 0:1], sbm_s[:, 0:1], -INV_D, None, ALU.mult), inc="SV")
        bank_gate["R0"].append(("SV", P.cnt["SV"]))

        pva = p2[:].rearrange("p a b -> p (a b)")
        pvb = acc[:].rearrange("p a b -> p (a b)")
        P.emit("sync", lambda s: s.dma_start(out=pva[:, 0:4096], in_=wk_blk[:, 0:4096]), inc="SW", n=16)
        P.emit("sync", lambda s: s.dma_start(out=pvb[:, 0:4096], in_=wk_blk[:, 4096:8192]), inc="SW", n=16)
        wkin0_tok = P.cnt["SW"]
        P.wait("vector", "SW", wkin0_tok)
        P.emit("vector", lambda v: v.reduce_max(am_col[:, 0:1], pva[:, 0:4096], axis=AX.X,
                                                apply_absolute_value=True))
        P.drain("vector")
        P.emit("vector", lambda v: v.reduce_max(scale_col[:, 0:1], pvb[:, 0:4096], axis=AX.X,
                                                apply_absolute_value=True))
        P.drain("vector")
        P.emit("vector", lambda v: v.tensor_max(am_col[:, 0:1], am_col[:, 0:1], scale_col[:, 0:1]), inc="SV")
        amax_tok = P.cnt["SV"]
        P.wait("sync", "SV", amax_tok)
        P.wait("sync", "ST", sb_mm_tok)
        P.emit("sync", lambda s: s.dma_start(out=srow[0:1, :], in_=am_col[:, 0:1]), inc="SX", n=16)
        row_tok = P.cnt["SX"]
        bank_wait("R0")
        P.wait("tensor", "SX", row_tok)
        P.emit("tensor", lambda t: t.matmul(psR0[:, 0:128], ones_row[0:1, :], srow[0:1, :],
                                            start=True, stop=True), inc="ST")
        amb_tok = P.cnt["ST"]
        P.wait("vector", "ST", amb_tok)
        P.emit("vector", lambda v: v.reduce_max(am_col[:, 0:1], psR0[:, 0:128], axis=AX.X))
        P.drain("vector")
        P.emit("vector", lambda v: v.tensor_scalar(scale_col[:, 0:1], am_col[:, 0:1], INV7, None, ALU.mult))
        P.drain("vector")
        P.emit("vector", lambda v: v.reciprocal(inv_col[:, 0:1], scale_col[:, 0:1]), inc="SV")
        scaleinv_tok = P.cnt["SV"]
        bank_gate["R0"].append(("SV", scaleinv_tok))

        split_in_tok, split_sg_tok, split_out_tok_l = {}, {}, {}
        NV_SPLIT = 10

        def vsplit_stage_in(u):
            stg = (p2 if u % 2 == 0 else acc)[:].rearrange("p a b -> p (a b)")
            P.wait("scalar", "SV", amax_tok)
            P.emit("scalar", lambda s, u=u, stg=stg: s.dma_start(out=stg[:, 0:4096], in_=hstream[u]),
                   inc="SD", n=16)
            split_in_tok[u] = P.cnt["SD"]

        vsplit_stage_in(0)
        vsplit_stage_in(1)
        hi_v = q1h[:].rearrange("p a b -> p (a b)")
        lo_v = q1l[:].rearrange("p a b -> p (a b)")
        for u in range(NV_SPLIT):
            stg = (p2 if u % 2 == 0 else acc)[:].rearrange("p a b -> p (a b)")
            P.wait("vector", "SD", split_in_tok[u])
            if u >= 1:
                P.wait("vector", "SD", split_out_tok_l[u - 1])
            P.emit("vector", lambda v, stg=stg: v.tensor_copy(hi_v[:, 0:4096], stg[:, 0:4096]))
            P.emit("vector", lambda v, stg=stg: v.tensor_sub(lo_v[:, 0:4096], stg[:, 0:4096], hi_v[:, 0:4096]), inc="SV")
            vtok = P.cnt["SV"]
            P.wait("scalar", "SV", vtok)
            P.emit("scalar", lambda s, u=u: s.dma_start(out=wsp_hi[u], in_=hi_v[:, 0:4096]), inc="SD", n=16)
            P.emit("scalar", lambda s, u=u: s.dma_start(out=wsp_lo[u], in_=lo_v[:, 0:4096]), inc="SD", n=16)
            split_out_tok[u] = P.cnt["SD"]
            split_out_tok_l[u] = P.cnt["SD"]
            if u + 2 < NV_SPLIT:
                vsplit_stage_in(u + 2)
        vsplit_done_sv = P.cnt["SV"]
        q1_free_tok = split_out_tok[NV_SPLIT - 1]

        hi_g = mdh[:].rearrange("p a b -> p (a b)")
        lo_g = mdl[:].rearrange("p a b -> p (a b)")
        gprev_out = 0
        for u in range(NV_SPLIT, NW):
            stg_tok = {}

            def g_stage(q, u=u, stg_tok=stg_tok):
                P.emit("gpsimd", lambda g, u=u, q=q: g.dma_start(
                    out=gstg[:, (q % 2) * 1024:(q % 2) * 1024 + 1024],
                    in_=hstream[u][:, q * 1024:(q + 1) * 1024]), inc="SO", n=16)
                stg_tok[q] = P.cnt["SO"]

            def g_copy(q, u=u):
                P.emit("gpsimd", lambda g, q=q: g.tensor_copy(
                    hi_g[:, q * 1024:(q + 1) * 1024], gstg[:, (q % 2) * 1024:(q % 2) * 1024 + 1024]))

            def g_sub(q, u=u):
                P.emit("gpsimd", lambda g, q=q: g.tensor_sub(
                    lo_g[:, q * 1024:(q + 1) * 1024], gstg[:, (q % 2) * 1024:(q % 2) * 1024 + 1024],
                    hi_g[:, q * 1024:(q + 1) * 1024]))

            g_stage(0)
            g_stage(1)
            P.wait("gpsimd", "SO", gprev_out)
            P.wait("gpsimd", "SO", stg_tok[0])
            g_copy(0)
            P.wait("gpsimd", "SO", stg_tok[1])
            g_copy(1)
            g_sub(0)
            g_stage(2)
            g_sub(1)
            g_stage(3)
            P.wait("gpsimd", "SO", stg_tok[2])
            g_copy(2)
            P.wait("gpsimd", "SO", stg_tok[3])
            g_copy(3)
            g_sub(2)
            g_sub(3)
            P.drain("gpsimd")
            P.emit("gpsimd", lambda g, u=u: g.dma_start(out=wsp_hi[u], in_=hi_g[:, 0:4096]), inc="SO", n=16)
            P.emit("gpsimd", lambda g, u=u: g.dma_start(out=wsp_lo[u], in_=lo_g[:, 0:4096]), inc="SO", n=16)
            gprev_out = P.cnt["SO"]
            split_out_so_tok[u] = P.cnt["SO"]
        P.wait("gpsimd", "SO", gprev_out)
        gsplit_out_tok_box[0] = P.cnt["SO"]


        load_unit(0)
        load_unit(1)
        load_unit(2)

        def ln_stats(src, s1_extra=(), sq_extra=(), scr_fn=None):
            bank_wait("R0")
            for sem, tok in s1_extra:
                P.wait("tensor", sem, tok)
            for kc in range(KC):
                P.emit("tensor", lambda t, kc=kc, src=src, s=(kc == 0), p=(kc == KC - 1):
                       t.matmul(psR0[0:1, :], ones_col[:], src[:, kc, :], start=s, stop=p),
                       inc="ST" if kc == KC - 1 else None)
            tokS1 = P.cnt["ST"]
            bank_wait("R1")
            sq_mm = []
            for kc in range(KC):
                scr = scr_fn(kc) if scr_fn else (b0 if kc % 2 == 0 else b1)[:]
                if scr_fn is None and kc >= 2:
                    P.wait("scalar", "ST", sq_mm[kc - 2])
                for sem, tok in sq_extra:
                    P.wait("scalar", sem, tok)
                stok = P.emit("scalar", lambda s, kc=kc, src=src, scr=scr:
                              s.activation(scr, src[:, kc, :], AF.Square), inc="SS")
                P.wait("tensor", "SS", stok)
                P.emit("tensor", lambda t, scr=scr, s=(kc == 0), p=(kc == KC - 1):
                       t.matmul(psR1[0:1, :], ones_col[:], scr, start=s, stop=p), inc="ST")
                sq_mm.append(P.cnt["ST"])
            return tokS1, P.cnt["ST"]

        def row_musig(tokS1, tokSQ):
            P.wait("vector", "ST", tokS1)
            P.emit("vector", lambda v: v.tensor_scalar(r0[0:1, :], psR0[0:1, :], INV_D, None, ALU.mult), inc="SV")
            bank_gate["R0"].append(("SV", P.cnt["SV"]))
            P.wait("vector", "ST", tokSQ)
            P.emit("vector", lambda v: v.tensor_scalar(r1[0:1, :], psR1[0:1, :], INV_D, None, ALU.mult), inc="SV")
            bank_gate["R1"].append(("SV", P.cnt["SV"]))
            P.emit("vector", lambda v: v.tensor_mul(r3[0:1, :], r0[0:1, :], r0[0:1, :]))
            P.emit("vector", lambda v: v.tensor_sub(r1[0:1, :], r1[0:1, :], r3[0:1, :]), inc="SV")
            var_tok = P.cnt["SV"]
            P.wait("scalar", "SV", var_tok)
            sq_tok = P.emit("scalar", lambda s: s.activation(
                r2[0:1, :], r1[0:1, :], AF.Sqrt, bias=eps_s[0:1, 0:1], scale=1.0), inc="SS")
            P.wait("vector", "SS", sq_tok)
            P.emit("vector", lambda v: v.reciprocal(r2[0:1, :], r2[0:1, :]), inc="SV")
            return P.cnt["SV"]

        def bcast(row_ap, bk, gates):
            bank_wait(bk)
            for sem, tok in gates:
                P.wait("tensor", sem, tok)
            P.emit("tensor", lambda t, row_ap=row_ap, bk=bk:
                   t.matmul(banks[bk][:], ones_row[0:1, :], row_ap, start=True, stop=True), inc="ST")
            return P.cnt["ST"]

        def split_pair(src_chunk_fn, dsth, dstl, idxs):
            for i2 in idxs:
                src = src_chunk_fn(i2)
                P.emit("vector", lambda v, src=src, dsth=dsth, i2=i2: v.tensor_copy(dsth[:, i2, :], src))
                P.emit("vector", lambda v, src=src, dsth=dsth, dstl=dstl, i2=i2:
                       v.tensor_sub(dstl[:, i2, :], src, dsth[:, i2, :]),
                       inc="SV" if i2 == idxs[-1] else None)
            return P.cnt["SV"]

        qz = {}
        pv0 = p0[:].rearrange("p a b -> p (a b)")
        qo_flat = q0l[:].rearrange("p a b -> p (a b)")

        def q_stage(pc):
            half, sub = divmod(pc, 4)
            c0 = half * 4096 + sub * 1024
            P.wait("sync", "SV", qz["gate"])
            P.emit("sync", lambda s, c0=c0: s.dma_start(
                out=pv0[:, 0:1024], in_=wk_blk[:, c0:c0 + 1024]), inc="SW", n=16)
            qz["ld"] = P.cnt["SW"]

        def q_piece(pc):
            half, sub = divmod(pc, 4)
            c0 = sub * 1024
            if pc == 4:
                P.wait("vector", "SD", qz["out0"])
            P.wait("vector", "SW", qz["ld"])
            P.emit("vector", lambda v: v.tensor_scalar(
                pv0[:, 0:1024], pv0[:, 0:1024], inv_col[:, 0:1], None, ALU.mult))
            P.emit("vector", lambda v: v.tensor_scalar(
                pv0[:, 1024:2048], pv0[:, 0:1024], -7.5, None, ALU.is_ge))
            for k in range(-6, 8):
                P.emit("vector", lambda v, thr=k - 0.5: v.scalar_tensor_tensor(
                    pv0[:, 1024:2048], pv0[:, 0:1024], thr, pv0[:, 1024:2048], ALU.is_ge, ALU.add))
            P.emit("vector", lambda v, c0=c0: v.tensor_scalar(
                qo_flat[:, c0:c0 + 1024], pv0[:, 1024:2048], -8.0, None, ALU.add), inc="SV")
            qz["gate"] = P.cnt["SV"]
            if pc == 3:
                P.wait("scalar", "SV", qz["gate"])
                P.emit("scalar", lambda s: s.dma_start(
                    out=wq_dram[:, 0:4096], in_=qo_flat[:, 0:4096]), inc="SD", n=16)
                qz["out0"] = P.cnt["SD"]
            if pc + 1 < 8:
                q_stage(pc + 1)

        def q_out_and_xns():
            P.wait("scalar", "SV", qz["gate"])
            P.emit("scalar", lambda s: s.dma_start(
                out=wq_dram[:, 4096:8192], in_=qo_flat[:, 0:4096]), inc="SD", n=16)
            wq_out_tok_box[0] = P.cnt["SD"]
            P.wait("vector", "SD", wq_out_tok_box[0])
            for kc in range(KC):
                P.emit("vector", lambda v, kc=kc: v.tensor_scalar(
                    q0h[:, kc, :], p1[:, kc, :], scale_col[:, 0:1], None, ALU.mult))
                P.emit("vector", lambda v, kc=kc: v.scalar_tensor_tensor(
                    q0l[:, kc, :], p1[:, kc, :], scale_col[:, 0:1], q0h[:, kc, :],
                    ALU.mult, ALU.subtract), inc="SV" if kc == KC - 1 else None)
            qz["xns"] = P.cnt["SV"]
            return qz["xns"]

        x_tok = {0: consts_tok}
        prev_out_so = 0
        prev_ffn_tok = 0
        prev_mlp1_tok = 0
        prev_xns_tok = 0
        prev_tokr = 0
        prev_hp_sv = 0
        pend_sq, pend_stats, pend_xn2p = {}, {}, {}

        def pairs_of(ti):
            return ((q0h, q0l, q1h, q1l) if ti % 2 == 0
                    else (q1h, q1l, q0h, q0l))

        def emit_p0_squares(ti):
            P.wait("scalar", "SX", x_tok[ti])
            P.wait("scalar", "SV", prev_hp_sv)
            if ti == 1:
                P.wait("scalar", "SD", wq_out_tok_box[0])
            toks = []
            for kc in range(KC):
                P.emit("scalar", lambda s, kc=kc: s.activation(
                    p0[:, kc, :], p3[:, kc, :], AF.Square), inc="SS")
                toks.append(P.cnt["SS"])
            pend_sq[ti] = toks

        def emit_p0_stats(ti):
            P.wait("tensor", "SX", x_tok[ti])
            bank_wait("R0")
            for kc in range(KC):
                P.emit("tensor", lambda t, kc=kc, s=(kc == 0), p=(kc == KC - 1):
                       t.matmul(psR0[0:1, :], ones_col[:], p3[:, kc, :], start=s, stop=p),
                       inc="ST" if kc == KC - 1 else None)
            tokS1 = P.cnt["ST"]
            bank_wait("R1")
            for kc in range(KC):
                P.wait("tensor", "SS", pend_sq[ti][kc])
                P.emit("tensor", lambda t, kc=kc, s=(kc == 0), p=(kc == KC - 1):
                       t.matmul(psR1[0:1, :], ones_col[:], p0[:, kc, :], start=s, stop=p),
                       inc="ST" if kc == KC - 1 else None)
            tokSQ = P.cnt["ST"]
            rstd_tok = row_musig(tokS1, tokSQ)
            P.emit("vector", lambda v: v.tensor_mul(r1[0:1, :], r1[0:1, :], r2[0:1, :]))
            P.emit("vector", lambda v: v.tensor_mul(r1[0:1, :], r1[0:1, :], r2[0:1, :]), inc="SV")
            v1_tok = P.cnt["SV"]
            P.wait("scalar", "SV", v1_tok)
            sq1_tok = P.emit("scalar", lambda s: s.activation(
                r1[0:1, :], r1[0:1, :], AF.Sqrt, bias=eps_s[0:1, 0:1], scale=1.0), inc="SS")
            P.wait("vector", "SS", sq1_tok)
            P.emit("vector", lambda v: v.reciprocal(r1[0:1, :], r1[0:1, :]))
            P.drain("vector")
            P.emit("vector", lambda v: v.tensor_mul(r3[0:1, :], r2[0:1, :], r1[0:1, :]), inc="SV")
            pend_stats[ti] = (rstd_tok, P.cnt["SV"])

        pend_bc = {}

        def emit_p1_bcast(ti):
            rstd_tok, r2c_tok = pend_stats[ti]
            pend_bc[ti] = {"mu": bcast(r0[0:1, :], "R0", [("SV", rstd_tok)]),
                           "r2c": bcast(r3[0:1, :], "R1", [("SV", r2c_tok)]),
                           "rstd_tok": rstd_tok}

        def p1_weave(ti, fc):
            qAh, qAl, _, _ = pairs_of(ti)
            bc = pend_bc[ti]
            if fc == 0:
                P.wait("vector", "ST", bc["mu"])
                P.wait("vector", "ST", prev_tokr)
                for kc in range(KC):
                    P.emit("vector", lambda v, kc=kc: v.tensor_sub(p1[:, kc, :], p3[:, kc, :], psR0[:, :]),
                           inc="SV" if kc == KC - 1 else None)
                bc["s"] = P.cnt["SV"]
                bank_gate["R0"].append(("SV", bc["s"]))
            else:
                if fc == 1:
                    P.wait("vector", "ST", bc["r2c"])
                    P.wait("vector", "ST", prev_mlp1_tok)
                for kc in ([fc - 1] if fc < 7 else [6, 7]):
                    P.emit("vector", lambda v, kc=kc: v.tensor_mul(p0[:, kc, :], p1[:, kc, :], psR1[:, :]))
                    P.emit("vector", lambda v, kc=kc: v.tensor_scalar(
                        p0[:, kc, :], p0[:, kc, :], n1w_s[:, kc:kc + 1], n1b_s[:, kc:kc + 1], ALU.mult, ALU.add))
                    P.emit("vector", lambda v, kc=kc, qAh=qAh: v.tensor_copy(qAh[:, kc, :], p0[:, kc, :]))
                    P.emit("vector", lambda v, kc=kc, qAh=qAh, qAl=qAl:
                           v.tensor_sub(qAl[:, kc, :], p0[:, kc, :], qAh[:, kc, :]),
                           inc="SV" if (fc == 7 and kc == 7) else None)
                if fc == 7:
                    pend_xn2p[ti] = P.cnt["SV"]
                    bank_gate["R1"].append(("SV", pend_xn2p[ti]))

        def emit_p1_tail(ti):
            rstd_b = bcast(r2[0:1, :], "R0", [("SV", pend_bc[ti]["rstd_tok"])])
            P.wait("vector", "ST", rstd_b)
            for kc in range(KC):
                P.emit("vector", lambda v, kc=kc: v.tensor_mul(p1[:, kc, :], p1[:, kc, :], psR0[:, :]),
                       inc="SV" if kc == KC - 1 else None)
            bank_gate["R0"].append(("SV", P.cnt["SV"]))
            if dbg and ti == 0:
                dump("xn", p1[:].rearrange("p a b -> p (a b)"), [("SV", P.cnt["SV"])])
                dump("xn2", p0[:].rearrange("p a b -> p (a b)"), [("SV", pend_xn2p[ti])], blockers=("vector",))

        P.barrier()
        emit_p0_squares(0)
        emit_p0_stats(0)
        emit_p1_bcast(0)
        for fc0 in range(8):
            p1_weave(0, fc0)
        emit_p1_tail(0)

        for i in range(NT):
            first_in_batch = (i % 4 == 0)
            qAh, qAl, qBh, qBl = pairs_of(i)
            xn2p_tok = pend_xn2p[i]

            scr_lo_tok = {}
            gg_tok = 0
            for q in range(4):
                use_unit(i, q)
                uvh, uvl = unit_views(i, q)
                for fo in range(4):
                    fcg = q * 4 + fo
                    bk = "A" if fcg % 2 == 0 else "B"
                    tok = mm_group(bk, mm3(uvh, uvl, (fo * 128, (fo + 1) * 128), qAh, qAl),
                                   extra_gate=[("SV", xn2p_tok)] if fcg == 0 else None)
                    if fcg < 8:
                        evac("scalar", bk, lambda s, bk=bk, tc=fcg, fcg=fcg:
                             s.activation(p3[:, tc, :], banks[bk][:], AF.Silu,
                                          bias=gate_b_s[:, fcg:fcg + 1], scale=1.0), tok)
                    else:
                        tc = fcg - 8
                        scr = b0 if tc % 2 == 0 else b1
                        if i == 0 and tc == 0:
                            P.wait("scalar", "SD", wq_out_tok_box[0])
                            P.wait("vector", "SD", wq_out_tok_box[0])
                        if tc == 0:
                            P.wait("scalar", "ST", prev_ffn_tok)
                        P.wait("scalar", "ST", tok)
                        if tc >= 2:
                            P.wait("scalar", "SV", scr_lo_tok[tc - 2])
                        st1 = P.emit("scalar", lambda s, bk=bk, scr=scr, fcg=fcg:
                                     s.activation(scr[:], banks[bk][:], AF.Silu,
                                                  bias=gate_b_s[:, fcg:fcg + 1], scale=1.0), inc="SS")
                        bank_gate[bk].append(("SS", st1))
                        P.emit("scalar", lambda s, scr=scr, tc=tc, qBh=qBh: s.copy(qBh[:, tc, :], scr[:]), inc="SS")
                        st2 = P.cnt["SS"]
                        P.wait("vector", "SS", st2)
                        gg_tok = P.emit("vector", lambda v, scr=scr, tc=tc, qBh=qBh, qBl=qBl:
                                        v.tensor_sub(qBl[:, tc, :], scr[:], qBh[:, tc, :]), inc="SV")
                        scr_lo_tok[tc] = gg_tok
                done_unit(i, q)
            silu_all_s = P.cnt["SS"]

            use_unit(i, 4)
            use_unit(i, 5)
            if i == 0:
                P.wait("vector", "SW", unit_dma_cum[5])
                u4h0, u4l0 = unit_views(0, 4)
                u5h0, u5l0 = unit_views(0, 5)
                for kc in range(KC):
                    P.emit("vector", lambda v, kc=kc, u4h0=u4h0: v.reduce_sum(swp_s[:, kc:kc + 1], u4h0[:, kc, :], axis=AX.X))
                    P.emit("vector", lambda v, kc=kc, u5h0=u5h0: v.reduce_sum(swp2_s[:, kc:kc + 1], u5h0[:, kc, :], axis=AX.X))
                P.drain("vector")
                P.emit("vector", lambda v: v.tensor_add(swm_s[:], swp_s[:], swp2_s[:]))
                for kc in range(KC):
                    P.emit("vector", lambda v, kc=kc, u4l0=u4l0: v.reduce_sum(swp_s[:, kc:kc + 1], u4l0[:, kc, :], axis=AX.X))
                    P.emit("vector", lambda v, kc=kc, u5l0=u5l0: v.reduce_sum(swp2_s[:, kc:kc + 1], u5l0[:, kc, :], axis=AX.X))
                P.drain("vector")
                P.emit("vector", lambda v: v.tensor_add(swp_s[:], swp_s[:], swp2_s[:]))
                P.drain("vector")
                P.emit("vector", lambda v: v.tensor_add(swm_s[:], swm_s[:], swp_s[:]))
                P.drain("vector")
                P.emit("vector", lambda v: v.tensor_scalar(swm_s[:], swm_s[:], INV_D, None, ALU.mult), inc="SV")
                P.wait("tensor", "SV", P.cnt["SV"])
            bank_wait("R0")
            for kc in range(KC):
                P.emit("tensor", lambda t, kc=kc, s=(kc == 0), p=(kc == KC - 1):
                       t.matmul(psR0[0:1, :], swm_s[:, kc:kc + 1], p0[:, kc, :], start=s, stop=p),
                       inc="ST" if kc == KC - 1 else None)
            dmv_tok = P.cnt["ST"]
            P.wait("vector", "ST", dmv_tok)
            P.wait("vector", "SS", silu_all_s)
            for kc in range(KC):
                scr = b0 if kc % 2 == 0 else b1
                P.emit("vector", lambda v, kc=kc, scr=scr: v.tensor_mul(scr[:], p0[:, kc, :], p3[:, kc, :]))
                P.emit("vector", lambda v, kc=kc, scr=scr, qAh=qAh: v.tensor_copy(qAh[:, kc, :], scr[:]))
                P.emit("vector", lambda v, kc=kc, scr=scr, qAh=qAh, qAl=qAl: v.tensor_sub(qAl[:, kc, :], scr[:], qAh[:, kc, :]),
                       inc="SV" if kc == KC - 1 else None)
            xgp_tok = P.cnt["SV"]
            P.wait("scalar", "ST", dmv_tok)
            da_tok = P.emit("scalar", lambda s: s.activation(
                r0[0:1, :], psR0[0:1, :], AF.Sigmoid, bias=sbm_s[0:1, 0:1], scale=-1.0), inc="SS")
            bank_gate["R0"].append(("SS", da_tok))
            ln_tok = P.emit("scalar", lambda s: s.activation(r1[0:1, :], r0[0:1, :], AF.Ln), inc="SS")
            P.wait("vector", "SS", ln_tok)
            P.emit("vector", lambda v: v.tensor_scalar(r1[0:1, :], r1[0:1, :], -1.0, None, ALU.mult), inc="SV")
            dl_tok = P.cnt["SV"]

            u4h, u4l = unit_views(i, 4)
            u5h, u5l = unit_views(i, 5)
            for j in range(KC):
                uvh, uvl = (u4h, u4l) if j < 4 else (u5h, u5l)
                bk = "A" if j % 2 == 0 else "B"
                tok = mm_group(bk, mm3(uvh, uvl, ((j % 4) * 128, (j % 4 + 1) * 128), qBh, qBl),
                               extra_gate=[("SV", gg_tok), ("SS", silu_all_s)] if j == 0 else None)
                evac("vector", bk, lambda v, bk=bk, j=j: v.tensor_scalar(
                    p3[:, j, :], banks[bk][:], state_b_s[:, j:j + 1], None, ALU.add), tok,
                    extra=[("SV", xgp_tok)] if j == 0 else ())

            dlb = bcast(r1[0:1, :], "R0", [("SV", dl_tok)])
            dab = bcast(r0[0:1, :], "R1", [("SS", da_tok)])

            P.wait("vector", "ST", dlb)
            P.wait("vector", "ST", dab)

            def chain_step(j):
                if 1 <= j <= 8:
                    c = j - 1
                    P.emit("vector", lambda v, c=c: v.tensor_mul(p0[:, c, :], p0[:, c, :], p3[:, c, :]))
                    P.emit("vector", lambda v, c=c: v.tensor_mul(p0[:, c, :], p0[:, c, :], psR0[:, :]))
                if 2 <= j <= 9:
                    c = j - 2
                    init = 0.0 if first_in_batch else carry[:, c:c + 1]
                    P.emit("vector", lambda v, c=c, init=init: v.tensor_tensor_scan(
                        p3[:, c, :], psR1[:, :], p0[:, c, :], init, ALU.mult, ALU.add))
                if 3 <= j <= 10:
                    c = j - 3
                    P.emit("vector", lambda v, c=c, qBh=qBh: v.tensor_copy(qBh[:, c, :], p3[:, c, :]))
                    P.emit("vector", lambda v, c=c, qBh=qBh, qBl=qBl: v.tensor_sub(qBl[:, c, :], p3[:, c, :], qBh[:, c, :]))
                if 4 <= j <= 11:
                    c = j - 4
                    P.emit("vector", lambda v, c=c: v.tensor_copy(carry[:, c:c + 1], p3[:, c, T - 1:T]),
                           inc="SV" if j == 11 else None)

            for j in range(KC):
                uvh, uvl = (u4h, u4l) if j < 4 else (u5h, u5l)
                bk = "A" if j % 2 == 0 else "B"
                tok = mm_group(bk, mm3(uvh, uvl, ((j % 4) * 128, (j % 4 + 1) * 128), qAh, qAl),
                               extra_gate=[("SV", xgp_tok)] if j == 0 else None)
                chain_step(j)
                evac("vector", bk, lambda v, bk=bk, j=j: v.tensor_scalar(
                    p0[:, j, :], banks[bk][:], state_b_s[:, j:j + 1], None, ALU.add), tok)
                if j == 3:
                    done_unit(i, 4)
            done_unit(i, 5)
            for j in range(KC, 12):
                chain_step(j)
            hp_tok = P.cnt["SV"]
            carry_tok = hp_tok
            prev_hp_sv = hp_tok
            bank_gate["R0"].append(("SV", hp_tok))
            bank_gate["R1"].append(("SV", hp_tok))
            if dbg and i == 0:
                dump("dA_row", r0[0:1, :], [("SS", da_tok)], blockers=("vector",))
                dump("delta_row", r1[0:1, :], [("SV", dl_tok)], blockers=("vector",))
                dump("swm", swm_s[:], [("SV", dl_tok)])
            if i == 0:
                qz["gate"] = hp_tok
                q_stage(0)

            use_unit(i, 6)
            use_unit(i, 7)
            u6h, u6l = unit_views(i, 6)
            u7h, u7l = unit_views(i, 7)
            for fc in range(KC):
                uvh, uvl = (u6h, u6l) if fc < 4 else (u7h, u7l)
                bk = "A" if fc % 2 == 0 else "B"
                tok = mm_group(bk, mm3(uvh, uvl, ((fc % 4) * 128, (fc % 4 + 1) * 128), qBh, qBl),
                               extra_gate=[("SV", hp_tok)] if fc == 0 else None)
                evac("vector", bk, lambda v, bk=bk, fc=fc: v.scalar_tensor_tensor(
                    p2[:, fc, :], banks[bk][:], out_b_s[:, fc:fc + 1], p1[:, fc, :], ALU.add, ALU.add), tok,
                    extra=[("SO", prev_out_so)] if fc == 0 else ())
                if fc == 3:
                    done_unit(i, 6)
            done_unit(i, 7)
            ssm_tok = P.cnt["ST"]
            xmid_tok = P.cnt["SV"]

            P.wait("gpsimd", "ST", ssm_tok)
            P.wait("gpsimd", "SV", carry_tok)
            P.emit("gpsimd", lambda g, i=i: g.dma_start(
                out=p3[:], in_=x_tl[i].rearrange("p (a b) -> p a b", b=T)), inc="SX", n=16)
            xrl_tok = P.cnt["SX"]

            tokS1b, tokSQb = ln_stats(p2, s1_extra=[("SV", xmid_tok)], sq_extra=[("SV", xmid_tok)])
            rstd2_tok = row_musig(tokS1b, tokSQb)
            mu2_b = bcast(r0[0:1, :], "R0", [("SV", rstd2_tok)])
            rstd2_b = bcast(r2[0:1, :], "R1", [("SV", rstd2_tok)])
            bank_wait("B")
            for kc in range(KC):
                P.emit("tensor", lambda t, kc=kc, s=(kc == 0), p=(kc == KC - 1):
                       t.matmul(psB[0:16, :], a_s[:, kc, :], p1[:, kc, :], start=s, stop=p),
                       inc="ST" if kc == KC - 1 else None)
            tokr = P.cnt["ST"]
            P.wait("vector", "ST", rstd2_b)
            for kc in range(KC):
                scr = b0 if kc % 2 == 0 else b1
                P.emit("vector", lambda v, kc=kc, scr=scr: v.tensor_sub(scr[:], p2[:, kc, :], psR0[:, :]))
                P.emit("vector", lambda v, kc=kc, scr=scr: v.tensor_mul(scr[:], scr[:], psR1[:, :]))
                P.emit("vector", lambda v, kc=kc, scr=scr: v.tensor_scalar(
                    scr[:], scr[:], n2w_s[:, kc:kc + 1], n2b_s[:, kc:kc + 1], ALU.mult, ALU.add))
                P.emit("vector", lambda v, kc=kc, scr=scr, qBh=qBh: v.tensor_copy(qBh[:, kc, :], scr[:]))
                P.emit("vector", lambda v, kc=kc, scr=scr, qBh=qBh, qBl=qBl: v.tensor_sub(qBl[:, kc, :], scr[:], qBh[:, kc, :]),
                       inc="SV" if kc == KC - 1 else None)
            x2p_tok = P.cnt["SV"]
            bank_gate["R0"].append(("SV", x2p_tok))
            bank_gate["R1"].append(("SV", x2p_tok))
            if dbg and i == 0:
                dump("x2h", qBh[:].rearrange("p a b -> p (a b)"), [("SV", x2p_tok)], blockers=("vector",))

            P.wait("vector", "ST", tokr)
            rs_tok = P.emit("vector", lambda v: v.tensor_copy(r_s[:], psB[0:16, :]), inc="SV")
            bank_gate["B"].append(("SV", rs_tok))
            prev_tokr = tokr

            if i > 0:
                for kc in range(KC):
                    P.emit("vector", lambda v, kc=kc, qAh=qAh: v.tensor_scalar(
                        qAh[:, kc, :], p1[:, kc, :], scale_col[:, 0:1], None, ALU.mult))
                    P.emit("vector", lambda v, kc=kc, qAh=qAh, qAl=qAl: v.scalar_tensor_tensor(
                        qAl[:, kc, :], p1[:, kc, :], scale_col[:, 0:1], qAh[:, kc, :],
                        ALU.mult, ALU.subtract), inc="SV" if kc == KC - 1 else None)
                xnp_tok = P.cnt["SV"]
            else:
                xnp_tok = None

            P.wait("vector", "SX", xrl_tok)
            for kc in range(KC):
                P.emit("vector", lambda v, kc=kc: v.tensor_add(p2[:, kc, :], p3[:, kc, :], p2[:, kc, :]),
                       inc="SV" if kc == KC - 1 else None)
            xmm_tok = P.cnt["SV"]
            if i == 0:
                q_piece(0)
            if dbg and i == 0:
                dump("h_hi", qBh[:].rearrange("p a b -> p (a b)"), [("SV", hp_tok)])
                dump("x_mid", p2[:].rearrange("p a b -> p (a b)"), [("SV", xmid_tok)], blockers=("vector",))
                dump("r_s", r_s[:], [("SV", rs_tok)])

            if i + 1 < NT:
                P.wait("gpsimd", "SV", xmm_tok)
                P.emit("gpsimd", lambda g, i=i: g.dma_start(
                    out=p3[:], in_=x_tl[i + 1].rearrange("p (a b) -> p a b", b=T)), inc="SX", n=16)
                x_tok[i + 1] = P.cnt["SX"]

            wqv, _ = unit_views(i, 24)
            mid_lo_tok = {}
            for hg in range(4):
                ua, ub = 8 + hg * 4, 9 + hg * 4
                use_unit(i, ua)
                use_unit(i, ub)
                w1ah, w1al = unit_views(i, ua)
                w1bh, w1bl = unit_views(i, ub)
                for hc in range(8):
                    hcg = hg * 8 + hc
                    uvh, uvl = (w1ah, w1al) if hc < 4 else (w1bh, w1bl)
                    bk = "A" if hc % 2 == 0 else "B"
                    tok = mm_group(bk, mm3(uvh, uvl, ((hc % 4) * 128, (hc % 4 + 1) * 128), qBh, qBl),
                                   extra_gate=[("SV", x2p_tok)] if (hg == 0 and hc == 0) else None)
                    scr = b0 if hc % 2 == 0 else b1
                    if i == 0 and hcg == 0:
                        P.wait("scalar", "SO", gsplit_out_tok_box[0])
                        P.wait("vector", "SO", gsplit_out_tok_box[0])
                    if hcg < 2:
                        P.wait("scalar", "SV", xnp_tok)
                    P.wait("scalar", "ST", tok)
                    if hcg >= 2:
                        P.wait("scalar", "SV", mid_lo_tok[hcg - 2])
                    gt = P.emit("scalar", lambda s, bk=bk, scr=scr, hcg=hcg:
                                s.activation(scr[:], banks[bk][:], AF.Gelu,
                                             bias=mlp_b1_s[:, hcg:hcg + 1], scale=1.0), inc="SS")
                    bank_gate[bk].append(("SS", gt))
                    P.emit("scalar", lambda s, scr=scr, hc=hc: s.copy(mdh[:, hc, :], scr[:]), inc="SS")
                    st2 = P.cnt["SS"]
                    P.wait("vector", "SS", st2)
                    mid_lo_tok[hcg] = P.emit("vector", lambda v, scr=scr, hc=hc:
                                             v.tensor_sub(mdl[:, hc, :], scr[:], mdh[:, hc, :]), inc="SV")
                    if hc == 3:
                        done_unit(i, ua)
                mid_tok = P.cnt["SV"]
                if i == 0 and hg < 3:
                    q_piece(1 + hg * 2)
                if dbg and i == 0 and hg == 0:
                    dump("mid0h", mdh[:].rearrange("p a b -> p (a b)"), [("SV", mid_tok)], blockers=("scalar", "vector"))
                    dump("mid0l", mdl[:].rearrange("p a b -> p (a b)"), [("SV", mid_tok)], blockers=("scalar", "vector"))
                if hg == 3:
                    prev_mlp1_tok = P.cnt["ST"]
                    if i + 1 < NT:
                        emit_p0_stats(i + 1)
                        emit_p1_bcast(i + 1)
                done_unit(i, ub)
                uc, ud = 10 + hg * 4, 11 + hg * 4
                use_unit(i, uc)
                use_unit(i, ud)
                w2ah, w2al = unit_views(i, uc)
                w2bh, w2bl = unit_views(i, ud)
                if hg == 3:
                    use_unit(i, 24)
                for fc in range(8):
                    uvh, uvl = (w2ah, w2al) if fc < 4 else (w2bh, w2bl)
                    bk = "A" if fc % 2 == 0 else "B"
                    mms = mm3(uvh, uvl, ((fc % 4) * 128, (fc % 4 + 1) * 128), mdh, mdl)
                    if hg == 3:
                        for kc in range(KC):
                            mms.append(((lambda wqv=wqv, kc=kc, fc=fc: wqv[:, kc, fc * 128:(fc + 1) * 128]),
                                        (lambda kc=kc, qAh=qAh: qAh[:, kc, :])))
                            mms.append(((lambda wqv=wqv, kc=kc, fc=fc: wqv[:, kc, fc * 128:(fc + 1) * 128]),
                                        (lambda kc=kc, qAl=qAl: qAl[:, kc, :])))
                        mms.append(((lambda fc=fc: bt_s[:, fc, :]), (lambda: r_s[:])))
                    eg = [("SV", mid_tok)] if fc == 0 else []
                    if hg == 3 and fc == 0:
                        eg += [("SV", xnp_tok), ("SV", rs_tok)]
                    tok = mm_group(bk, mms, extra_gate=eg or None)
                    if hg == 0:
                        evac("vector", bk, lambda v, bk=bk, fc=fc: v.tensor_copy(
                            acc[:, fc, :], banks[bk][:]), tok)
                    else:
                        evac("vector", bk, lambda v, bk=bk, fc=fc: v.tensor_add(
                            acc[:, fc, :], acc[:, fc, :], banks[bk][:]), tok)
                    if fc == 3:
                        done_unit(i, uc)
                    if hg == 3 and i + 1 < NT:
                        p1_weave(i + 1, fc)
                done_unit(i, ud)
                if i == 0:
                    if hg < 2:
                        q_piece(2 + hg * 2)
                    elif hg == 2:
                        q_piece(6)
                        q_piece(7)
                        xnp_tok = q_out_and_xns()
                if hg == 2 and i + 1 < NT:
                    emit_p0_squares(i + 1)
                if hg == 3:
                    prev_ffn_tok = P.cnt["ST"]
            done_unit(i, 24)
            if i + 1 < NT:
                emit_p1_tail(i + 1)

            if dbg and i == 0:
                dump("acc", acc[:].rearrange("p a b -> p (a b)"), [("SV", P.cnt["SV"])], blockers=("vector",))
                dump("w10h", wsp_hi[10], [("SV", P.cnt["SV"])])
                dump("w10l", wsp_lo[10], [("SV", P.cnt["SV"])])
                dump("w11h", wsp_hi[11], [("SV", P.cnt["SV"])])
            out_gate = []
            for fc in range(KC):
                P.emit("vector", lambda v, fc=fc: v.scalar_tensor_tensor(
                    p2[:, fc, :], acc[:, fc, :], cb_s[:, fc:fc + 1], p2[:, fc, :], ALU.add, ALU.add),
                       inc="SV")
                out_gate.append(P.cnt["SV"])
            if dbg and i == 0:
                dump("final", p2[:].rearrange("p a b -> p (a b)"), [("SV", out_gate[-1])], blockers=("vector",))
            for half in range(2):
                P.wait("gpsimd", "SV", out_gate[4 * half + 3])
                P.emit("gpsimd", lambda g, i=i, half=half: g.dma_start(
                    out=out_tl[i][:, half * 2048:(half + 1) * 2048],
                    in_=p2[:, 4 * half:4 * half + 4, :].rearrange("p a b -> p (a b)")), inc="SO", n=16)
            prev_out_so = P.cnt["SO"]

        P.wait("gpsimd", "SO", P.cnt["SO"])

        def runner(eng_name):
            drain_incs = eng_name in ("tensor", "vector", "scalar", "gpsimd")

            def body(eng):
                for op in P.ops[eng_name]:
                    if op[0] == "w":
                        eng.wait_ge(sems[op[1]], op[2])
                    elif op[0] == "d":
                        eng.drain()
                    else:
                        _, fn, inc, n = op
                        instr = fn(eng)
                        if inc is not None and drain_incs and n != 16:
                            instr = eng.drain()
                        if inc is not None:
                            instr.then_inc(sems[inc], n)
            return body

        with nc.Block() as block:
            block.sync(runner("sync"))
            block.gpsimd(runner("gpsimd"))
            block.tensor(runner("tensor"))
            block.vector(runner("vector"))
            block.scalar(runner("scalar"))

    if dbg:
        return nc, dbg_names
    return nc


def _block_unit(w, k0, k1, f0, f1):
    v = w[k0:k1, f0:f1]
    return np.ascontiguousarray(v.reshape((k1 - k0) // 128, 128, f1 - f0).transpose(1, 0, 2)).reshape(128, 4096)


def _col_block(v):
    return np.ascontiguousarray(np.asarray(v, np.float32).reshape(-1, 128).T)


def make_hstream(gw, sw, ow, w1, w2):
    units = []
    for q in range(4):
        units.append(_block_unit(gw, 0, 1024, q * 512, (q + 1) * 512))
    for h in range(2):
        units.append(_block_unit(sw, 0, 1024, h * 512, (h + 1) * 512))
    for h in range(2):
        units.append(_block_unit(ow, 0, 1024, h * 512, (h + 1) * 512))
    for hg in range(4):
        units.append(_block_unit(w1, 0, 1024, hg * 1024, hg * 1024 + 512))
        units.append(_block_unit(w1, 0, 1024, hg * 1024 + 512, hg * 1024 + 1024))
        units.append(_block_unit(w2, hg * 1024, (hg + 1) * 1024, 0, 512))
        units.append(_block_unit(w2, hg * 1024, (hg + 1) * 1024, 512, 1024))
    return np.ascontiguousarray(np.stack(units))


_NC_CACHE = [None]


def kernel(L0_attn_x, L0_attn_norm1_w, L0_attn_norm1_b, L0_attn_gate_w, L0_attn_gate_b,
           L0_attn_state_w, L0_attn_state_b, L0_attn_out_w, L0_attn_out_b,
           L0_attn_norm2_w, L0_attn_norm2_b, L0_attn_mlp_w1, L0_attn_mlp_b1,
           L0_attn_mlp_w2, L0_attn_mlp_b2, L0_ffn_kernel, L0_ffn_bias, L0_ffn_a, L0_ffn_b):
    f = np.float32
    x = np.ascontiguousarray(L0_attn_x, dtype=f)
    hstream = make_hstream(np.asarray(L0_attn_gate_w, f), np.asarray(L0_attn_state_w, f),
                           np.asarray(L0_attn_out_w, f), np.asarray(L0_attn_mlp_w1, f),
                           np.asarray(L0_attn_mlp_w2, f))
    wk = np.asarray(L0_ffn_kernel, f)
    consts = {
        "hstream": hstream,
        "wk_blk": np.ascontiguousarray(
            wk.reshape(8, 128, 1024).transpose(1, 0, 2).reshape(128, 8192)),
        "gate_b_t": _col_block(L0_attn_gate_b),
        "state_b_t": _col_block(L0_attn_state_b),
        "out_b_t": _col_block(L0_attn_out_b),
        "mlp_b1_t": _col_block(L0_attn_mlp_b1),
        "mlp_b2_t": _col_block(L0_attn_mlp_b2),
        "ffn_b_t": _col_block(L0_ffn_bias),
        "n1w_t": _col_block(L0_attn_norm1_w),
        "n1b_t": _col_block(L0_attn_norm1_b),
        "n2w_t": _col_block(L0_attn_norm2_w),
        "n2b_t": _col_block(L0_attn_norm2_b),
        "a_t": np.ascontiguousarray(
            np.asarray(L0_ffn_a, f).reshape(8, 128, 16).transpose(1, 0, 2).reshape(128, 128)),
        "b_t": np.ascontiguousarray(np.asarray(L0_ffn_b, f)),
    }

    in_maps = []
    for c in range(N_CORES):
        xc = x[2 * c:2 * c + 2].reshape(4096, 1024)
        x_tl = np.ascontiguousarray(
            xc.reshape(8, 512, 8, 128).transpose(0, 3, 2, 1).reshape(NT, 128, KC * T))
        m = dict(consts)
        m["x_tl"] = x_tl
        in_maps.append(m)

    if _NC_CACHE[0] is None:
        _NC_CACHE[0] = build_nc()
    res = run_bass_kernel_spmd(_NC_CACHE[0], in_maps, core_ids=list(range(N_CORES)))

    outs = []
    for c in range(N_CORES):
        ot = np.asarray(res.results[c]["out_tl"])
        oc = ot.reshape(8, 128, 8, 512).transpose(0, 3, 2, 1).reshape(4096, 1024)
        outs.append(oc.reshape(2, 2048, 1024))
    return np.ascontiguousarray(np.concatenate(outs, axis=0))


# version 32
# speedup vs baseline: 1.0405x; 1.0047x over previous
import numpy as np
import concourse.bass as bass
import concourse.mybir as mybir
from concourse.bass_utils import run_bass_kernel_spmd

F32 = mybir.dt.float32
BF16 = mybir.dt.bfloat16
AF = mybir.ActivationFunctionType
ALU = mybir.AluOpType
AX = mybir.AxisListType

T = 512
NT = 8
KC = 8
EPS = 1e-5
N_CORES = 8
NW = 24
UPT = 25
NSLOT = 3
INV7 = 1.0 / 7.0
INV_D = 1.0 / 1024.0

ENGS = ("tensor", "vector", "scalar", "sync", "gpsimd")
SEM_OF = {"tensor": "ST", "vector": "SV", "scalar": "SS"}


class Plan:
    def __init__(self):
        self.ops = {e: [] for e in ENGS}
        self.cnt = {"ST": 0, "SV": 0, "SS": 0, "SW": 0, "SX": 0, "SO": 0, "SG": 0, "SD": 0}
        self.last_wait = {}

    def wait(self, eng, sem, val):
        if not val or val <= 0:
            return
        key = (eng, sem)
        if self.last_wait.get(key, 0) >= val:
            return
        self.last_wait[key] = val
        self.ops[eng].append(("w", sem, val))

    def emit(self, eng, fn, inc=None, n=1):
        self.ops[eng].append(("o", fn, inc, n))
        if inc is not None:
            self.cnt[inc] += n
            return self.cnt[inc]
        return None

    def drain(self, eng):
        self.ops[eng].append(("d",))

    def barrier(self):
        for a in ("tensor", "vector", "scalar"):
            for b in ("tensor", "vector", "scalar"):
                if a != b:
                    self.wait(b, SEM_OF[a], self.cnt[SEM_OF[a]])


def build_nc(dbg=False):
    nc = bass.Bass("TRN2", target_bir_lowering=False)

    dbg_t = nc.dram_tensor("dbg", [24, 128, 4096], F32, kind="ExternalOutput").ap() if dbg else None
    dbg_names = []

    x_tl = nc.dram_tensor("x_tl", [NT, 128, KC * T], F32, kind="ExternalInput").ap()
    hstream = nc.dram_tensor("hstream", [NW, 128, 4096], F32, kind="ExternalInput").ap()
    wk_blk = nc.dram_tensor("wk_blk", [128, 8192], F32, kind="ExternalInput").ap()
    gate_b_t = nc.dram_tensor("gate_b_t", [128, 16], F32, kind="ExternalInput").ap()
    state_b_t = nc.dram_tensor("state_b_t", [128, 8], F32, kind="ExternalInput").ap()
    out_b_t = nc.dram_tensor("out_b_t", [128, 8], F32, kind="ExternalInput").ap()
    mlp_b1_t = nc.dram_tensor("mlp_b1_t", [128, 32], F32, kind="ExternalInput").ap()
    mlp_b2_t = nc.dram_tensor("mlp_b2_t", [128, 8], F32, kind="ExternalInput").ap()
    ffn_b_t = nc.dram_tensor("ffn_b_t", [128, 8], F32, kind="ExternalInput").ap()
    n1w_t = nc.dram_tensor("n1w_t", [128, 8], F32, kind="ExternalInput").ap()
    n1b_t = nc.dram_tensor("n1b_t", [128, 8], F32, kind="ExternalInput").ap()
    n2w_t = nc.dram_tensor("n2w_t", [128, 8], F32, kind="ExternalInput").ap()
    n2b_t = nc.dram_tensor("n2b_t", [128, 8], F32, kind="ExternalInput").ap()
    a_t = nc.dram_tensor("a_t", [128, 128], F32, kind="ExternalInput").ap()
    b_t = nc.dram_tensor("b_t", [16, 1024], F32, kind="ExternalInput").ap()
    wq_dram = nc.dram_tensor("wq_dram", [128, 8192], BF16, kind="Internal").ap()
    wsp_hi = nc.dram_tensor("wsp_hi", [NW, 128, 4096], BF16, kind="Internal").ap()
    wsp_lo = nc.dram_tensor("wsp_lo", [NW, 128, 4096], BF16, kind="Internal").ap()
    out_tl = nc.dram_tensor("out_tl", [NT, 128, KC * T], F32, kind="ExternalOutput").ap()

    from contextlib import ExitStack
    with ExitStack() as st:
        e = st.enter_context
        p0 = e(nc.sbuf_tensor("p0", [128, KC, T], F32))
        p1 = e(nc.sbuf_tensor("p1", [128, KC, T], F32))
        p2 = e(nc.sbuf_tensor("p2", [128, KC, T], F32))
        p3 = e(nc.sbuf_tensor("p3", [128, KC, T], F32))
        acc = e(nc.sbuf_tensor("acc", [128, KC, T], F32))
        q0h = e(nc.sbuf_tensor("q0h", [128, KC, T], BF16))
        q0l = e(nc.sbuf_tensor("q0l", [128, KC, T], BF16))
        q1h = e(nc.sbuf_tensor("q1h", [128, KC, T], BF16))
        q1l = e(nc.sbuf_tensor("q1l", [128, KC, T], BF16))
        mdh = e(nc.sbuf_tensor("mdh", [128, KC, T], BF16))
        mdl = e(nc.sbuf_tensor("mdl", [128, KC, T], BF16))
        slots = [e(nc.sbuf_tensor(f"slot{k}", [128, 8192], BF16)) for k in range(NSLOT)]
        b0 = e(nc.sbuf_tensor("b0", [128, T], F32))
        b1 = e(nc.sbuf_tensor("b1", [128, T], F32))
        r0 = e(nc.sbuf_tensor("r0", [1, T], F32))
        r1 = e(nc.sbuf_tensor("r1", [1, T], F32))
        r2 = e(nc.sbuf_tensor("r2", [1, T], F32))
        r3 = e(nc.sbuf_tensor("r3", [1, T], F32))
        srow = e(nc.sbuf_tensor("srow", [1, 128], F32))
        ones_row = e(nc.sbuf_tensor("ones_row", [1, 128], F32))
        ones_col = e(nc.sbuf_tensor("ones_col", [128, 1], F32))
        eps_s = e(nc.sbuf_tensor("eps_s", [1, 1], F32))
        sbm_s = e(nc.sbuf_tensor("sbm_s", [128, 1], F32))
        am_col = e(nc.sbuf_tensor("am_col", [128, 1], F32))
        scale_col = e(nc.sbuf_tensor("scale_col", [128, 1], F32))
        inv_col = e(nc.sbuf_tensor("inv_col", [128, 1], F32))
        carry = e(nc.sbuf_tensor("carry", [128, 8], F32))
        r_s = e(nc.sbuf_tensor("r_s", [16, T], F32))
        gate_b_s = e(nc.sbuf_tensor("gate_b_s", [128, 16], F32))
        state_b_s = e(nc.sbuf_tensor("state_b_s", [128, 8], F32))
        out_b_s = e(nc.sbuf_tensor("out_b_s", [128, 8], F32))
        mlp_b1_s = e(nc.sbuf_tensor("mlp_b1_s", [128, 32], F32))
        b2_s = e(nc.sbuf_tensor("b2_s", [128, 8], F32))
        fb_s = e(nc.sbuf_tensor("fb_s", [128, 8], F32))
        cb_s = e(nc.sbuf_tensor("cb_s", [128, 8], F32))
        n1w_s = e(nc.sbuf_tensor("n1w_s", [128, 8], F32))
        n1b_s = e(nc.sbuf_tensor("n1b_s", [128, 8], F32))
        n2w_s = e(nc.sbuf_tensor("n2w_s", [128, 8], F32))
        n2b_s = e(nc.sbuf_tensor("n2b_s", [128, 8], F32))
        a_s = e(nc.sbuf_tensor("a_s", [128, 8, 16], F32))
        bt_s = e(nc.sbuf_tensor("bt_s", [16, 8, 128], F32))
        swm_s = e(nc.sbuf_tensor("swm_s", [128, 8], F32))
        swp_s = e(nc.sbuf_tensor("swp_s", [128, 8], F32))
        swp2_s = e(nc.sbuf_tensor("swp2_s", [128, 8], F32))
        gstg = e(nc.sbuf_tensor("gstg", [128, 2048], F32))
        psA = e(nc.psum_tensor("psA", [128, T], F32))
        psB = e(nc.psum_tensor("psB", [128, T], F32))
        psR0 = e(nc.psum_tensor("psR0", [128, T], F32))
        psR1 = e(nc.psum_tensor("psR1", [128, T], F32))
        SW = e(nc.semaphore("SW"))
        SX = e(nc.semaphore("SX"))
        SO = e(nc.semaphore("SO"))
        ST = e(nc.semaphore("ST"))
        SV = e(nc.semaphore("SV"))
        SS = e(nc.semaphore("SS"))
        SG = e(nc.semaphore("SG"))
        SD = e(nc.semaphore("SD"))
        sems = {"SW": SW, "SX": SX, "SO": SO, "ST": ST, "SV": SV, "SS": SS,
                "SG": SG, "SD": SD}

        banks = {"A": psA, "B": psB, "R0": psR0, "R1": psR1}
        P = Plan()
        bank_gate = {k: [] for k in banks}
        unit_tok = {}
        unit_dma_cum = {}
        split_out_tok = {}
        split_out_so_tok = {}
        wq_out_tok_box = [0]
        gsplit_out_tok_box = [0]

        def dump(name, src_ap, gates, blockers=()):
            if not dbg:
                return
            k = len(dbg_names)
            dbg_names.append(name)
            for sem, tok in gates:
                P.wait("gpsimd", sem, tok)
            r, c = src_ap.shape[0], src_ap.shape[1]
            P.emit("gpsimd", lambda g, k=k, src_ap=src_ap, r=r, c=c:
                   g.dma_start(out=dbg_t[k][0:r, 0:c], in_=src_ap,
                               allow_slow_non_contiguous=True), inc="SO", n=16)
            tok = P.cnt["SO"]
            for eng in blockers:
                P.wait(eng, "SO", tok)

        def bank_wait(bk):
            for sem, tok in bank_gate[bk]:
                P.wait("tensor", sem, tok)
            bank_gate[bk] = []

        def mm_group(bk, mm_list, extra_gate=None, start=True, stop=True):
            if start:
                bank_wait(bk)
            if extra_gate:
                for sem, tok in extra_gate:
                    P.wait("tensor", sem, tok)
            nmm = len(mm_list)
            for idx, (lf, rf) in enumerate(mm_list):
                last = idx == nmm - 1
                P.emit("tensor",
                       lambda t, lf=lf, rf=rf, bk=bk, s=(start and idx == 0), p=(stop and last):
                       t.matmul(banks[bk][:], lf(), rf(), start=s, stop=p),
                       inc="ST" if (last and stop) else None)
            return P.cnt["ST"]

        def evac(eng, bk, fn, gate_tok, extra=()):
            P.wait(eng, "ST", gate_tok)
            for sem, tok in extra:
                P.wait(eng, sem, tok)
            t = P.emit(eng, fn, inc=SEM_OF[eng])
            bank_gate[bk].append((SEM_OF[eng], t))
            return t

        def load_unit(g):
            tile, j = divmod(g, UPT)
            slot = slots[g % NSLOT]
            P.wait("sync", "ST", unit_tok.get(g - NSLOT, 0))
            if tile == 0:
                if j == 24:
                    P.wait("sync", "SD", wq_out_tok_box[0])
                elif j in split_out_so_tok:
                    P.wait("sync", "SO", split_out_so_tok[j])
                else:
                    P.wait("sync", "SD", split_out_tok.get(j, 0))
            if j == 24:
                P.emit("sync", lambda s, slot=slot: s.dma_start(out=slot[:], in_=wq_dram[:]),
                       inc="SW", n=16)
            else:
                P.emit("sync", lambda s, j=j, slot=slot:
                       s.dma_start(out=slot[:, 0:4096], in_=wsp_hi[j]), inc="SW", n=16)
                P.emit("sync", lambda s, j=j, slot=slot:
                       s.dma_start(out=slot[:, 4096:8192], in_=wsp_lo[j]), inc="SW", n=16)
            unit_dma_cum[g] = P.cnt["SW"]

        def unit_views(tile, j):
            slot = slots[(tile * UPT + j) % NSLOT]
            if j == 24:
                return slot[:].rearrange("p (a b) -> p a b", b=1024), None
            hi = slot[:, 0:4096].rearrange("p (a b) -> p a b", b=512)
            lo = slot[:, 4096:8192].rearrange("p (a b) -> p a b", b=512)
            return hi, lo

        def use_unit(tile, j):
            P.wait("tensor", "SW", unit_dma_cum[tile * UPT + j])

        def done_unit(tile, j):
            g = tile * UPT + j
            unit_tok[g] = P.cnt["ST"]
            if g + NSLOT < NT * UPT:
                load_unit(g + NSLOT)

        def mm3(uvh, uvl, fsl, rh, rl):
            mms = []
            for kc in range(KC):
                wh = (lambda uvh=uvh, kc=kc, fsl=fsl: uvh[:, kc, fsl[0]:fsl[1]])
                wl = (lambda uvl=uvl, kc=kc, fsl=fsl: uvl[:, kc, fsl[0]:fsl[1]])
                ah = (lambda rh=rh, kc=kc: rh[:, kc, :])
                al = (lambda rl=rl, kc=kc: rl[:, kc, :])
                mms += [(wh, ah), (wh, al), (wl, ah)]
            return mms

        P.emit("vector", lambda v: v.memset(ones_row[:], 1.0))
        P.emit("vector", lambda v: v.memset(ones_col[:], 1.0))
        P.emit("vector", lambda v: v.memset(eps_s[:], EPS), inc="SV")

        const_pairs = [
            (gate_b_t, gate_b_s), (state_b_t, state_b_s), (out_b_t, out_b_s),
            (mlp_b1_t, mlp_b1_s), (mlp_b2_t, b2_s), (ffn_b_t, fb_s),
            (n1w_t, n1w_s), (n1b_t, n1b_s), (n2w_t, n2w_s), (n2b_t, n2b_s),
        ]
        for src, dst in const_pairs:
            P.emit("gpsimd", lambda g, src=src, dst=dst:
                   g.dma_start(out=dst[:], in_=src[:]), inc="SX", n=16)
        P.emit("gpsimd", lambda g: g.dma_start(
            out=a_s[:], in_=a_t.rearrange("p (a b) -> p a b", b=16)), inc="SX", n=16)
        P.emit("gpsimd", lambda g: g.dma_start(
            out=bt_s[:], in_=b_t.rearrange("p (a b) -> p a b", b=128)), inc="SX", n=16)
        P.emit("gpsimd", lambda g: g.dma_start(
            out=p3[:], in_=x_tl[0].rearrange("p (a b) -> p a b", b=T)), inc="SX", n=16)
        consts_tok = P.cnt["SX"]

        P.wait("vector", "SX", consts_tok)
        P.emit("vector", lambda v: v.tensor_add(cb_s[:], b2_s[:], fb_s[:]), inc="SV")
        P.emit("vector", lambda v: v.reduce_sum(am_col[:, 0:1], state_b_s[:], axis=AX.X), inc="SV")
        sbsum_tok = P.cnt["SV"]
        P.wait("sync", "SV", sbsum_tok)
        P.emit("sync", lambda s: s.dma_start(out=srow[0:1, :], in_=am_col[:, 0:1]), inc="SX", n=16)
        sbrow_tok = P.cnt["SX"]
        P.wait("tensor", "SX", sbrow_tok)
        P.emit("tensor", lambda t: t.matmul(psR0[:, 0:128], ones_row[0:1, :], srow[0:1, :],
                                            start=True, stop=True), inc="ST")
        sb_mm_tok = P.cnt["ST"]
        P.wait("vector", "ST", sb_mm_tok)
        P.emit("vector", lambda v: v.reduce_sum(sbm_s[:, 0:1], psR0[:, 0:128], axis=AX.X))
        P.drain("vector")
        P.emit("vector", lambda v: v.tensor_scalar(sbm_s[:, 0:1], sbm_s[:, 0:1], -INV_D, None, ALU.mult), inc="SV")
        bank_gate["R0"].append(("SV", P.cnt["SV"]))

        pva = p2[:].rearrange("p a b -> p (a b)")
        pvb = acc[:].rearrange("p a b -> p (a b)")
        P.emit("sync", lambda s: s.dma_start(out=pva[:, 0:4096], in_=wk_blk[:, 0:4096]), inc="SW", n=16)
        P.emit("sync", lambda s: s.dma_start(out=pvb[:, 0:4096], in_=wk_blk[:, 4096:8192]), inc="SW", n=16)
        wkin0_tok = P.cnt["SW"]
        P.wait("vector", "SW", wkin0_tok)
        P.emit("vector", lambda v: v.reduce_max(am_col[:, 0:1], pva[:, 0:4096], axis=AX.X,
                                                apply_absolute_value=True))
        P.drain("vector")
        P.emit("vector", lambda v: v.reduce_max(scale_col[:, 0:1], pvb[:, 0:4096], axis=AX.X,
                                                apply_absolute_value=True))
        P.drain("vector")
        P.emit("vector", lambda v: v.tensor_max(am_col[:, 0:1], am_col[:, 0:1], scale_col[:, 0:1]), inc="SV")
        amax_tok = P.cnt["SV"]
        P.wait("sync", "SV", amax_tok)
        P.wait("sync", "ST", sb_mm_tok)
        P.emit("sync", lambda s: s.dma_start(out=srow[0:1, :], in_=am_col[:, 0:1]), inc="SX", n=16)
        row_tok = P.cnt["SX"]
        bank_wait("R0")
        P.wait("tensor", "SX", row_tok)
        P.emit("tensor", lambda t: t.matmul(psR0[:, 0:128], ones_row[0:1, :], srow[0:1, :],
                                            start=True, stop=True), inc="ST")
        amb_tok = P.cnt["ST"]
        P.wait("vector", "ST", amb_tok)
        P.emit("vector", lambda v: v.reduce_max(am_col[:, 0:1], psR0[:, 0:128], axis=AX.X))
        P.drain("vector")
        P.emit("vector", lambda v: v.tensor_scalar(scale_col[:, 0:1], am_col[:, 0:1], INV7, None, ALU.mult))
        P.drain("vector")
        P.emit("vector", lambda v: v.reciprocal(inv_col[:, 0:1], scale_col[:, 0:1]), inc="SV")
        scaleinv_tok = P.cnt["SV"]
        bank_gate["R0"].append(("SV", scaleinv_tok))

        split_in_tok, split_sg_tok, split_out_tok_l = {}, {}, {}
        NV_SPLIT = 10

        def vsplit_stage_in(u):
            stg = (p2 if u % 2 == 0 else acc)[:].rearrange("p a b -> p (a b)")
            P.wait("scalar", "SV", amax_tok)
            P.emit("scalar", lambda s, u=u, stg=stg: s.dma_start(out=stg[:, 0:4096], in_=hstream[u]),
                   inc="SD", n=16)
            split_in_tok[u] = P.cnt["SD"]

        vsplit_stage_in(0)
        vsplit_stage_in(1)
        hi_v = q1h[:].rearrange("p a b -> p (a b)")
        lo_v = q1l[:].rearrange("p a b -> p (a b)")
        for u in range(NV_SPLIT):
            stg = (p2 if u % 2 == 0 else acc)[:].rearrange("p a b -> p (a b)")
            P.wait("vector", "SD", split_in_tok[u])
            if u >= 1:
                P.wait("vector", "SD", split_out_tok_l[u - 1])
            P.emit("vector", lambda v, stg=stg: v.tensor_copy(hi_v[:, 0:4096], stg[:, 0:4096]))
            P.emit("vector", lambda v, stg=stg: v.tensor_sub(lo_v[:, 0:4096], stg[:, 0:4096], hi_v[:, 0:4096]), inc="SV")
            vtok = P.cnt["SV"]
            P.wait("scalar", "SV", vtok)
            P.emit("scalar", lambda s, u=u: s.dma_start(out=wsp_hi[u], in_=hi_v[:, 0:4096]), inc="SD", n=16)
            P.emit("scalar", lambda s, u=u: s.dma_start(out=wsp_lo[u], in_=lo_v[:, 0:4096]), inc="SD", n=16)
            split_out_tok[u] = P.cnt["SD"]
            split_out_tok_l[u] = P.cnt["SD"]
            if u + 2 < NV_SPLIT:
                vsplit_stage_in(u + 2)
        vsplit_done_sv = P.cnt["SV"]
        q1_free_tok = split_out_tok[NV_SPLIT - 1]

        hi_g = mdh[:].rearrange("p a b -> p (a b)")
        lo_g = mdl[:].rearrange("p a b -> p (a b)")
        gprev_out = 0
        for u in range(NV_SPLIT, NW):
            stg_tok = {}

            def g_stage(q, u=u, stg_tok=stg_tok):
                P.emit("gpsimd", lambda g, u=u, q=q: g.dma_start(
                    out=gstg[:, (q % 2) * 1024:(q % 2) * 1024 + 1024],
                    in_=hstream[u][:, q * 1024:(q + 1) * 1024]), inc="SO", n=16)
                stg_tok[q] = P.cnt["SO"]

            def g_copy(q, u=u):
                P.emit("gpsimd", lambda g, q=q: g.tensor_copy(
                    hi_g[:, q * 1024:(q + 1) * 1024], gstg[:, (q % 2) * 1024:(q % 2) * 1024 + 1024]))

            def g_sub(q, u=u):
                P.emit("gpsimd", lambda g, q=q: g.tensor_sub(
                    lo_g[:, q * 1024:(q + 1) * 1024], gstg[:, (q % 2) * 1024:(q % 2) * 1024 + 1024],
                    hi_g[:, q * 1024:(q + 1) * 1024]))

            g_stage(0)
            g_stage(1)
            P.wait("gpsimd", "SO", gprev_out)
            P.wait("gpsimd", "SO", stg_tok[0])
            g_copy(0)
            P.wait("gpsimd", "SO", stg_tok[1])
            g_copy(1)
            P.drain("gpsimd")
            g_sub(0)
            g_stage(2)
            g_sub(1)
            g_stage(3)
            P.wait("gpsimd", "SO", stg_tok[2])
            g_copy(2)
            P.wait("gpsimd", "SO", stg_tok[3])
            g_copy(3)
            P.drain("gpsimd")
            g_sub(2)
            g_sub(3)
            P.drain("gpsimd")
            P.emit("gpsimd", lambda g, u=u: g.dma_start(out=wsp_hi[u], in_=hi_g[:, 0:4096]), inc="SO", n=16)
            P.emit("gpsimd", lambda g, u=u: g.dma_start(out=wsp_lo[u], in_=lo_g[:, 0:4096]), inc="SO", n=16)
            gprev_out = P.cnt["SO"]
            split_out_so_tok[u] = P.cnt["SO"]
        P.wait("gpsimd", "SO", gprev_out)
        gsplit_out_tok_box[0] = P.cnt["SO"]


        load_unit(0)
        load_unit(1)
        load_unit(2)

        def ln_stats(src, s1_extra=(), sq_extra=(), scr_fn=None):
            bank_wait("R0")
            for sem, tok in s1_extra:
                P.wait("tensor", sem, tok)
            for kc in range(KC):
                P.emit("tensor", lambda t, kc=kc, src=src, s=(kc == 0), p=(kc == KC - 1):
                       t.matmul(psR0[0:1, :], ones_col[:], src[:, kc, :], start=s, stop=p),
                       inc="ST" if kc == KC - 1 else None)
            tokS1 = P.cnt["ST"]
            bank_wait("R1")
            sq_mm = []
            for kc in range(KC):
                scr = scr_fn(kc) if scr_fn else (b0 if kc % 2 == 0 else b1)[:]
                if scr_fn is None and kc >= 2:
                    P.wait("scalar", "ST", sq_mm[kc - 2])
                for sem, tok in sq_extra:
                    P.wait("scalar", sem, tok)
                stok = P.emit("scalar", lambda s, kc=kc, src=src, scr=scr:
                              s.activation(scr, src[:, kc, :], AF.Square), inc="SS")
                P.wait("tensor", "SS", stok)
                P.emit("tensor", lambda t, scr=scr, s=(kc == 0), p=(kc == KC - 1):
                       t.matmul(psR1[0:1, :], ones_col[:], scr, start=s, stop=p), inc="ST")
                sq_mm.append(P.cnt["ST"])
            return tokS1, P.cnt["ST"]

        def row_musig(tokS1, tokSQ):
            P.wait("vector", "ST", tokS1)
            P.emit("vector", lambda v: v.tensor_scalar(r0[0:1, :], psR0[0:1, :], INV_D, None, ALU.mult), inc="SV")
            bank_gate["R0"].append(("SV", P.cnt["SV"]))
            P.wait("vector", "ST", tokSQ)
            P.emit("vector", lambda v: v.tensor_scalar(r1[0:1, :], psR1[0:1, :], INV_D, None, ALU.mult), inc="SV")
            bank_gate["R1"].append(("SV", P.cnt["SV"]))
            P.emit("vector", lambda v: v.tensor_mul(r3[0:1, :], r0[0:1, :], r0[0:1, :]))
            P.emit("vector", lambda v: v.tensor_sub(r1[0:1, :], r1[0:1, :], r3[0:1, :]), inc="SV")
            var_tok = P.cnt["SV"]
            P.wait("scalar", "SV", var_tok)
            sq_tok = P.emit("scalar", lambda s: s.activation(
                r2[0:1, :], r1[0:1, :], AF.Sqrt, bias=eps_s[0:1, 0:1], scale=1.0), inc="SS")
            P.wait("vector", "SS", sq_tok)
            P.emit("vector", lambda v: v.reciprocal(r2[0:1, :], r2[0:1, :]), inc="SV")
            return P.cnt["SV"]

        def bcast(row_ap, bk, gates):
            bank_wait(bk)
            for sem, tok in gates:
                P.wait("tensor", sem, tok)
            P.emit("tensor", lambda t, row_ap=row_ap, bk=bk:
                   t.matmul(banks[bk][:], ones_row[0:1, :], row_ap, start=True, stop=True), inc="ST")
            return P.cnt["ST"]

        def split_pair(src_chunk_fn, dsth, dstl, idxs):
            for i2 in idxs:
                src = src_chunk_fn(i2)
                P.emit("vector", lambda v, src=src, dsth=dsth, i2=i2: v.tensor_copy(dsth[:, i2, :], src))
                P.emit("vector", lambda v, src=src, dsth=dsth, dstl=dstl, i2=i2:
                       v.tensor_sub(dstl[:, i2, :], src, dsth[:, i2, :]),
                       inc="SV" if i2 == idxs[-1] else None)
            return P.cnt["SV"]

        qz = {}
        pv0 = p0[:].rearrange("p a b -> p (a b)")
        qo_flat = q0l[:].rearrange("p a b -> p (a b)")

        def q_stage(pc):
            half, sub = divmod(pc, 4)
            c0 = half * 4096 + sub * 1024
            P.wait("sync", "SV", qz["gate"])
            P.emit("sync", lambda s, c0=c0: s.dma_start(
                out=pv0[:, 0:1024], in_=wk_blk[:, c0:c0 + 1024]), inc="SW", n=16)
            qz["ld"] = P.cnt["SW"]

        def q_piece(pc):
            half, sub = divmod(pc, 4)
            c0 = sub * 1024
            if pc == 4:
                P.wait("vector", "SD", qz["out0"])
            P.wait("vector", "SW", qz["ld"])
            P.emit("vector", lambda v: v.tensor_scalar(
                pv0[:, 0:1024], pv0[:, 0:1024], inv_col[:, 0:1], None, ALU.mult))
            P.emit("vector", lambda v: v.tensor_scalar(
                pv0[:, 1024:2048], pv0[:, 0:1024], -7.5, None, ALU.is_ge))
            for k in range(-6, 8):
                P.emit("vector", lambda v, thr=k - 0.5: v.scalar_tensor_tensor(
                    pv0[:, 1024:2048], pv0[:, 0:1024], thr, pv0[:, 1024:2048], ALU.is_ge, ALU.add))
            P.emit("vector", lambda v, c0=c0: v.tensor_scalar(
                qo_flat[:, c0:c0 + 1024], pv0[:, 1024:2048], -8.0, None, ALU.add), inc="SV")
            qz["gate"] = P.cnt["SV"]
            if pc == 3:
                P.wait("scalar", "SV", qz["gate"])
                P.emit("scalar", lambda s: s.dma_start(
                    out=wq_dram[:, 0:4096], in_=qo_flat[:, 0:4096]), inc="SD", n=16)
                qz["out0"] = P.cnt["SD"]
            if pc + 1 < 8:
                q_stage(pc + 1)

        def q_out_and_xns():
            P.wait("scalar", "SV", qz["gate"])
            P.emit("scalar", lambda s: s.dma_start(
                out=wq_dram[:, 4096:8192], in_=qo_flat[:, 0:4096]), inc="SD", n=16)
            wq_out_tok_box[0] = P.cnt["SD"]
            P.wait("vector", "SD", wq_out_tok_box[0])
            for kc in range(KC):
                P.emit("vector", lambda v, kc=kc: v.tensor_scalar(
                    q0h[:, kc, :], p1[:, kc, :], scale_col[:, 0:1], None, ALU.mult))
                P.emit("vector", lambda v, kc=kc: v.scalar_tensor_tensor(
                    q0l[:, kc, :], p1[:, kc, :], scale_col[:, 0:1], q0h[:, kc, :],
                    ALU.mult, ALU.subtract), inc="SV" if kc == KC - 1 else None)
            qz["xns"] = P.cnt["SV"]
            return qz["xns"]

        x_tok = {0: consts_tok}
        prev_out_so = 0
        prev_ffn_tok = 0
        prev_mlp1_tok = 0
        prev_xns_tok = 0
        prev_tokr = 0
        prev_hp_sv = 0
        pend_sq, pend_stats, pend_xn2p = {}, {}, {}

        def pairs_of(ti):
            return ((q0h, q0l, q1h, q1l) if ti % 2 == 0
                    else (q1h, q1l, q0h, q0l))

        def emit_p0_squares(ti):
            P.wait("scalar", "SX", x_tok[ti])
            P.wait("scalar", "SV", prev_hp_sv)
            if ti == 1:
                P.wait("scalar", "SD", wq_out_tok_box[0])
            toks = []
            for kc in range(KC):
                P.emit("scalar", lambda s, kc=kc: s.activation(
                    p0[:, kc, :], p3[:, kc, :], AF.Square), inc="SS")
                toks.append(P.cnt["SS"])
            pend_sq[ti] = toks

        def emit_p0_stats(ti):
            P.wait("tensor", "SX", x_tok[ti])
            bank_wait("R0")
            for kc in range(KC):
                P.emit("tensor", lambda t, kc=kc, s=(kc == 0), p=(kc == KC - 1):
                       t.matmul(psR0[0:1, :], ones_col[:], p3[:, kc, :], start=s, stop=p),
                       inc="ST" if kc == KC - 1 else None)
            tokS1 = P.cnt["ST"]
            bank_wait("R1")
            for kc in range(KC):
                P.wait("tensor", "SS", pend_sq[ti][kc])
                P.emit("tensor", lambda t, kc=kc, s=(kc == 0), p=(kc == KC - 1):
                       t.matmul(psR1[0:1, :], ones_col[:], p0[:, kc, :], start=s, stop=p),
                       inc="ST" if kc == KC - 1 else None)
            tokSQ = P.cnt["ST"]
            rstd_tok = row_musig(tokS1, tokSQ)
            P.emit("vector", lambda v: v.tensor_mul(r1[0:1, :], r1[0:1, :], r2[0:1, :]))
            P.emit("vector", lambda v: v.tensor_mul(r1[0:1, :], r1[0:1, :], r2[0:1, :]), inc="SV")
            v1_tok = P.cnt["SV"]
            P.wait("scalar", "SV", v1_tok)
            sq1_tok = P.emit("scalar", lambda s: s.activation(
                r1[0:1, :], r1[0:1, :], AF.Sqrt, bias=eps_s[0:1, 0:1], scale=1.0), inc="SS")
            P.wait("vector", "SS", sq1_tok)
            P.emit("vector", lambda v: v.reciprocal(r1[0:1, :], r1[0:1, :]))
            P.drain("vector")
            P.emit("vector", lambda v: v.tensor_mul(r3[0:1, :], r2[0:1, :], r1[0:1, :]), inc="SV")
            pend_stats[ti] = (rstd_tok, P.cnt["SV"])

        pend_bc = {}

        def emit_p1_bcast(ti):
            rstd_tok, r2c_tok = pend_stats[ti]
            pend_bc[ti] = {"mu": bcast(r0[0:1, :], "R0", [("SV", rstd_tok)]),
                           "r2c": bcast(r3[0:1, :], "R1", [("SV", r2c_tok)]),
                           "rstd_tok": rstd_tok}

        def p1_weave(ti, fc):
            qAh, qAl, _, _ = pairs_of(ti)
            bc = pend_bc[ti]
            if fc == 0:
                P.wait("vector", "ST", bc["mu"])
                P.wait("vector", "ST", prev_tokr)
                for kc in range(KC):
                    P.emit("vector", lambda v, kc=kc: v.tensor_sub(p1[:, kc, :], p3[:, kc, :], psR0[:, :]),
                           inc="SV" if kc == KC - 1 else None)
                bc["s"] = P.cnt["SV"]
                bank_gate["R0"].append(("SV", bc["s"]))
            else:
                if fc == 1:
                    P.wait("vector", "ST", bc["r2c"])
                    P.wait("vector", "ST", prev_mlp1_tok)
                for kc in ([fc - 1] if fc < 7 else [6, 7]):
                    P.emit("vector", lambda v, kc=kc: v.tensor_mul(p0[:, kc, :], p1[:, kc, :], psR1[:, :]))
                    P.emit("vector", lambda v, kc=kc: v.tensor_scalar(
                        p0[:, kc, :], p0[:, kc, :], n1w_s[:, kc:kc + 1], n1b_s[:, kc:kc + 1], ALU.mult, ALU.add))
                    P.emit("vector", lambda v, kc=kc, qAh=qAh: v.tensor_copy(qAh[:, kc, :], p0[:, kc, :]))
                    P.emit("vector", lambda v, kc=kc, qAh=qAh, qAl=qAl:
                           v.tensor_sub(qAl[:, kc, :], p0[:, kc, :], qAh[:, kc, :]),
                           inc="SV" if (fc == 7 and kc == 7) else None)
                if fc == 7:
                    pend_xn2p[ti] = P.cnt["SV"]
                    bank_gate["R1"].append(("SV", pend_xn2p[ti]))

        def emit_p1_tail(ti):
            rstd_b = bcast(r2[0:1, :], "R0", [("SV", pend_bc[ti]["rstd_tok"])])
            P.wait("vector", "ST", rstd_b)
            for kc in range(KC):
                P.emit("vector", lambda v, kc=kc: v.tensor_mul(p1[:, kc, :], p1[:, kc, :], psR0[:, :]),
                       inc="SV" if kc == KC - 1 else None)
            bank_gate["R0"].append(("SV", P.cnt["SV"]))
            if dbg and ti == 0:
                dump("xn", p1[:].rearrange("p a b -> p (a b)"), [("SV", P.cnt["SV"])])
                dump("xn2", p0[:].rearrange("p a b -> p (a b)"), [("SV", pend_xn2p[ti])], blockers=("vector",))

        P.barrier()
        emit_p0_squares(0)
        emit_p0_stats(0)
        emit_p1_bcast(0)
        for fc0 in range(8):
            p1_weave(0, fc0)
        emit_p1_tail(0)

        for i in range(NT):
            first_in_batch = (i % 4 == 0)
            qAh, qAl, qBh, qBl = pairs_of(i)
            xn2p_tok = pend_xn2p[i]

            scr_lo_tok = {}
            gg_tok = 0
            for q in range(4):
                use_unit(i, q)
                uvh, uvl = unit_views(i, q)
                for fo in range(4):
                    fcg = q * 4 + fo
                    bk = "A" if fcg % 2 == 0 else "B"
                    tok = mm_group(bk, mm3(uvh, uvl, (fo * 128, (fo + 1) * 128), qAh, qAl),
                                   extra_gate=[("SV", xn2p_tok)] if fcg == 0 else None)
                    if fcg < 8:
                        evac("scalar", bk, lambda s, bk=bk, tc=fcg, fcg=fcg:
                             s.activation(p3[:, tc, :], banks[bk][:], AF.Silu,
                                          bias=gate_b_s[:, fcg:fcg + 1], scale=1.0), tok)
                    else:
                        tc = fcg - 8
                        scr = b0 if tc % 2 == 0 else b1
                        if i == 0 and tc == 0:
                            P.wait("scalar", "SD", wq_out_tok_box[0])
                            P.wait("vector", "SD", wq_out_tok_box[0])
                        if tc == 0:
                            P.wait("scalar", "ST", prev_ffn_tok)
                        P.wait("scalar", "ST", tok)
                        if tc >= 2:
                            P.wait("scalar", "SV", scr_lo_tok[tc - 2])
                        st1 = P.emit("scalar", lambda s, bk=bk, scr=scr, fcg=fcg:
                                     s.activation(scr[:], banks[bk][:], AF.Silu,
                                                  bias=gate_b_s[:, fcg:fcg + 1], scale=1.0), inc="SS")
                        bank_gate[bk].append(("SS", st1))
                        P.emit("scalar", lambda s, scr=scr, tc=tc, qBh=qBh: s.copy(qBh[:, tc, :], scr[:]), inc="SS")
                        st2 = P.cnt["SS"]
                        P.wait("vector", "SS", st2)
                        gg_tok = P.emit("vector", lambda v, scr=scr, tc=tc, qBh=qBh, qBl=qBl:
                                        v.tensor_sub(qBl[:, tc, :], scr[:], qBh[:, tc, :]), inc="SV")
                        scr_lo_tok[tc] = gg_tok
                done_unit(i, q)
            silu_all_s = P.cnt["SS"]

            use_unit(i, 4)
            use_unit(i, 5)
            if i == 0:
                P.wait("vector", "SW", unit_dma_cum[5])
                u4h0, u4l0 = unit_views(0, 4)
                u5h0, u5l0 = unit_views(0, 5)
                for kc in range(KC):
                    P.emit("vector", lambda v, kc=kc, u4h0=u4h0: v.reduce_sum(swp_s[:, kc:kc + 1], u4h0[:, kc, :], axis=AX.X))
                    P.emit("vector", lambda v, kc=kc, u5h0=u5h0: v.reduce_sum(swp2_s[:, kc:kc + 1], u5h0[:, kc, :], axis=AX.X))
                P.drain("vector")
                P.emit("vector", lambda v: v.tensor_add(swm_s[:], swp_s[:], swp2_s[:]))
                for kc in range(KC):
                    P.emit("vector", lambda v, kc=kc, u4l0=u4l0: v.reduce_sum(swp_s[:, kc:kc + 1], u4l0[:, kc, :], axis=AX.X))
                    P.emit("vector", lambda v, kc=kc, u5l0=u5l0: v.reduce_sum(swp2_s[:, kc:kc + 1], u5l0[:, kc, :], axis=AX.X))
                P.drain("vector")
                P.emit("vector", lambda v: v.tensor_add(swp_s[:], swp_s[:], swp2_s[:]))
                P.drain("vector")
                P.emit("vector", lambda v: v.tensor_add(swm_s[:], swm_s[:], swp_s[:]))
                P.drain("vector")
                P.emit("vector", lambda v: v.tensor_scalar(swm_s[:], swm_s[:], INV_D, None, ALU.mult), inc="SV")
                P.wait("tensor", "SV", P.cnt["SV"])
            bank_wait("R0")
            for kc in range(KC):
                P.emit("tensor", lambda t, kc=kc, s=(kc == 0), p=(kc == KC - 1):
                       t.matmul(psR0[0:1, :], swm_s[:, kc:kc + 1], p0[:, kc, :], start=s, stop=p),
                       inc="ST" if kc == KC - 1 else None)
            dmv_tok = P.cnt["ST"]
            P.wait("vector", "ST", dmv_tok)
            P.wait("vector", "SS", silu_all_s)
            for kc in range(KC):
                scr = b0 if kc % 2 == 0 else b1
                P.emit("vector", lambda v, kc=kc, scr=scr: v.tensor_mul(scr[:], p0[:, kc, :], p3[:, kc, :]))
                P.emit("vector", lambda v, kc=kc, scr=scr, qAh=qAh: v.tensor_copy(qAh[:, kc, :], scr[:]))
                P.emit("vector", lambda v, kc=kc, scr=scr, qAh=qAh, qAl=qAl: v.tensor_sub(qAl[:, kc, :], scr[:], qAh[:, kc, :]),
                       inc="SV" if kc == KC - 1 else None)
            xgp_tok = P.cnt["SV"]
            P.wait("scalar", "ST", dmv_tok)
            da_tok = P.emit("scalar", lambda s: s.activation(
                r0[0:1, :], psR0[0:1, :], AF.Sigmoid, bias=sbm_s[0:1, 0:1], scale=-1.0), inc="SS")
            bank_gate["R0"].append(("SS", da_tok))
            ln_tok = P.emit("scalar", lambda s: s.activation(r1[0:1, :], r0[0:1, :], AF.Ln), inc="SS")
            P.wait("vector", "SS", ln_tok)
            P.emit("vector", lambda v: v.tensor_scalar(r1[0:1, :], r1[0:1, :], -1.0, None, ALU.mult), inc="SV")
            dl_tok = P.cnt["SV"]

            u4h, u4l = unit_views(i, 4)
            u5h, u5l = unit_views(i, 5)
            for j in range(KC):
                uvh, uvl = (u4h, u4l) if j < 4 else (u5h, u5l)
                bk = "A" if j % 2 == 0 else "B"
                tok = mm_group(bk, mm3(uvh, uvl, ((j % 4) * 128, (j % 4 + 1) * 128), qBh, qBl),
                               extra_gate=[("SV", gg_tok), ("SS", silu_all_s)] if j == 0 else None)
                evac("vector", bk, lambda v, bk=bk, j=j: v.tensor_scalar(
                    p3[:, j, :], banks[bk][:], state_b_s[:, j:j + 1], None, ALU.add), tok,
                    extra=[("SV", xgp_tok)] if j == 0 else ())

            dlb = bcast(r1[0:1, :], "R0", [("SV", dl_tok)])
            dab = bcast(r0[0:1, :], "R1", [("SS", da_tok)])

            P.wait("vector", "ST", dlb)
            P.wait("vector", "ST", dab)

            def chain_step(j):
                if 1 <= j <= 8:
                    c = j - 1
                    P.emit("vector", lambda v, c=c: v.tensor_mul(p0[:, c, :], p0[:, c, :], p3[:, c, :]))
                    P.emit("vector", lambda v, c=c: v.tensor_mul(p0[:, c, :], p0[:, c, :], psR0[:, :]))
                if 2 <= j <= 9:
                    c = j - 2
                    init = 0.0 if first_in_batch else carry[:, c:c + 1]
                    P.emit("vector", lambda v, c=c, init=init: v.tensor_tensor_scan(
                        p3[:, c, :], psR1[:, :], p0[:, c, :], init, ALU.mult, ALU.add))
                if 3 <= j <= 10:
                    c = j - 3
                    P.emit("vector", lambda v, c=c, qBh=qBh: v.tensor_copy(qBh[:, c, :], p3[:, c, :]))
                    P.emit("vector", lambda v, c=c, qBh=qBh, qBl=qBl: v.tensor_sub(qBl[:, c, :], p3[:, c, :], qBh[:, c, :]))
                if 4 <= j <= 11:
                    c = j - 4
                    P.emit("vector", lambda v, c=c: v.tensor_copy(carry[:, c:c + 1], p3[:, c, T - 1:T]),
                           inc="SV" if j == 11 else None)

            for j in range(KC):
                uvh, uvl = (u4h, u4l) if j < 4 else (u5h, u5l)
                bk = "A" if j % 2 == 0 else "B"
                tok = mm_group(bk, mm3(uvh, uvl, ((j % 4) * 128, (j % 4 + 1) * 128), qAh, qAl),
                               extra_gate=[("SV", xgp_tok)] if j == 0 else None)
                chain_step(j)
                evac("vector", bk, lambda v, bk=bk, j=j: v.tensor_scalar(
                    p0[:, j, :], banks[bk][:], state_b_s[:, j:j + 1], None, ALU.add), tok)
                if j == 3:
                    done_unit(i, 4)
            done_unit(i, 5)
            for j in range(KC, 12):
                chain_step(j)
            hp_tok = P.cnt["SV"]
            carry_tok = hp_tok
            prev_hp_sv = hp_tok
            bank_gate["R0"].append(("SV", hp_tok))
            bank_gate["R1"].append(("SV", hp_tok))
            if dbg and i == 0:
                dump("dA_row", r0[0:1, :], [("SS", da_tok)], blockers=("vector",))
                dump("delta_row", r1[0:1, :], [("SV", dl_tok)], blockers=("vector",))
                dump("swm", swm_s[:], [("SV", dl_tok)])
            if i == 0:
                qz["gate"] = hp_tok
                q_stage(0)

            use_unit(i, 6)
            use_unit(i, 7)
            u6h, u6l = unit_views(i, 6)
            u7h, u7l = unit_views(i, 7)
            for fc in range(KC):
                uvh, uvl = (u6h, u6l) if fc < 4 else (u7h, u7l)
                bk = "A" if fc % 2 == 0 else "B"
                tok = mm_group(bk, mm3(uvh, uvl, ((fc % 4) * 128, (fc % 4 + 1) * 128), qBh, qBl),
                               extra_gate=[("SV", hp_tok)] if fc == 0 else None)
                evac("vector", bk, lambda v, bk=bk, fc=fc: v.scalar_tensor_tensor(
                    p2[:, fc, :], banks[bk][:], out_b_s[:, fc:fc + 1], p1[:, fc, :], ALU.add, ALU.add), tok,
                    extra=[("SO", prev_out_so)] if fc == 0 else ())
                if fc == 3:
                    done_unit(i, 6)
            done_unit(i, 7)
            ssm_tok = P.cnt["ST"]
            xmid_tok = P.cnt["SV"]

            P.wait("gpsimd", "ST", ssm_tok)
            P.wait("gpsimd", "SV", carry_tok)
            P.emit("gpsimd", lambda g, i=i: g.dma_start(
                out=p3[:], in_=x_tl[i].rearrange("p (a b) -> p a b", b=T)), inc="SX", n=16)
            xrl_tok = P.cnt["SX"]

            tokS1b, tokSQb = ln_stats(p2, s1_extra=[("SV", xmid_tok)], sq_extra=[("SV", xmid_tok)])
            rstd2_tok = row_musig(tokS1b, tokSQb)
            mu2_b = bcast(r0[0:1, :], "R0", [("SV", rstd2_tok)])
            rstd2_b = bcast(r2[0:1, :], "R1", [("SV", rstd2_tok)])
            bank_wait("B")
            for kc in range(KC):
                P.emit("tensor", lambda t, kc=kc, s=(kc == 0), p=(kc == KC - 1):
                       t.matmul(psB[0:16, :], a_s[:, kc, :], p1[:, kc, :], start=s, stop=p),
                       inc="ST" if kc == KC - 1 else None)
            tokr = P.cnt["ST"]
            P.wait("vector", "ST", rstd2_b)
            for kc in range(KC):
                scr = b0 if kc % 2 == 0 else b1
                P.emit("vector", lambda v, kc=kc, scr=scr: v.tensor_sub(scr[:], p2[:, kc, :], psR0[:, :]))
                P.emit("vector", lambda v, kc=kc, scr=scr: v.tensor_mul(scr[:], scr[:], psR1[:, :]))
                P.emit("vector", lambda v, kc=kc, scr=scr: v.tensor_scalar(
                    scr[:], scr[:], n2w_s[:, kc:kc + 1], n2b_s[:, kc:kc + 1], ALU.mult, ALU.add))
                P.emit("vector", lambda v, kc=kc, scr=scr, qBh=qBh: v.tensor_copy(qBh[:, kc, :], scr[:]))
                P.emit("vector", lambda v, kc=kc, scr=scr, qBh=qBh, qBl=qBl: v.tensor_sub(qBl[:, kc, :], scr[:], qBh[:, kc, :]),
                       inc="SV" if kc == KC - 1 else None)
            x2p_tok = P.cnt["SV"]
            bank_gate["R0"].append(("SV", x2p_tok))
            bank_gate["R1"].append(("SV", x2p_tok))
            if dbg and i == 0:
                dump("x2h", qBh[:].rearrange("p a b -> p (a b)"), [("SV", x2p_tok)], blockers=("vector",))

            P.wait("vector", "ST", tokr)
            rs_tok = P.emit("vector", lambda v: v.tensor_copy(r_s[:], psB[0:16, :]), inc="SV")
            bank_gate["B"].append(("SV", rs_tok))
            prev_tokr = tokr

            if i > 0:
                for kc in range(KC):
                    P.emit("vector", lambda v, kc=kc, qAh=qAh: v.tensor_scalar(
                        qAh[:, kc, :], p1[:, kc, :], scale_col[:, 0:1], None, ALU.mult))
                    P.emit("vector", lambda v, kc=kc, qAh=qAh, qAl=qAl: v.scalar_tensor_tensor(
                        qAl[:, kc, :], p1[:, kc, :], scale_col[:, 0:1], qAh[:, kc, :],
                        ALU.mult, ALU.subtract), inc="SV" if kc == KC - 1 else None)
                xnp_tok = P.cnt["SV"]
            else:
                xnp_tok = None

            P.wait("vector", "SX", xrl_tok)
            for kc in range(KC):
                P.emit("vector", lambda v, kc=kc: v.tensor_add(p2[:, kc, :], p3[:, kc, :], p2[:, kc, :]),
                       inc="SV" if kc == KC - 1 else None)
            xmm_tok = P.cnt["SV"]
            if i == 0:
                q_piece(0)
            if dbg and i == 0:
                dump("h_hi", qBh[:].rearrange("p a b -> p (a b)"), [("SV", hp_tok)])
                dump("x_mid", p2[:].rearrange("p a b -> p (a b)"), [("SV", xmid_tok)], blockers=("vector",))
                dump("r_s", r_s[:], [("SV", rs_tok)])

            if i + 1 < NT:
                P.wait("gpsimd", "SV", xmm_tok)
                P.emit("gpsimd", lambda g, i=i: g.dma_start(
                    out=p3[:], in_=x_tl[i + 1].rearrange("p (a b) -> p a b", b=T)), inc="SX", n=16)
                x_tok[i + 1] = P.cnt["SX"]

            wqv, _ = unit_views(i, 24)
            mid_lo_tok = {}
            for hg in range(4):
                ua, ub = 8 + hg * 4, 9 + hg * 4
                use_unit(i, ua)
                use_unit(i, ub)
                w1ah, w1al = unit_views(i, ua)
                w1bh, w1bl = unit_views(i, ub)
                for hc in range(8):
                    hcg = hg * 8 + hc
                    uvh, uvl = (w1ah, w1al) if hc < 4 else (w1bh, w1bl)
                    bk = "A" if hc % 2 == 0 else "B"
                    tok = mm_group(bk, mm3(uvh, uvl, ((hc % 4) * 128, (hc % 4 + 1) * 128), qBh, qBl),
                                   extra_gate=[("SV", x2p_tok)] if (hg == 0 and hc == 0) else None)
                    scr = b0 if hc % 2 == 0 else b1
                    if i == 0 and hcg == 0:
                        P.wait("scalar", "SO", gsplit_out_tok_box[0])
                        P.wait("vector", "SO", gsplit_out_tok_box[0])
                    if hcg < 2:
                        P.wait("scalar", "SV", xnp_tok)
                    P.wait("scalar", "ST", tok)
                    if hcg >= 2:
                        P.wait("scalar", "SV", mid_lo_tok[hcg - 2])
                    gt = P.emit("scalar", lambda s, bk=bk, scr=scr, hcg=hcg:
                                s.activation(scr[:], banks[bk][:], AF.Gelu,
                                             bias=mlp_b1_s[:, hcg:hcg + 1], scale=1.0), inc="SS")
                    bank_gate[bk].append(("SS", gt))
                    P.emit("scalar", lambda s, scr=scr, hc=hc: s.copy(mdh[:, hc, :], scr[:]), inc="SS")
                    st2 = P.cnt["SS"]
                    P.wait("vector", "SS", st2)
                    mid_lo_tok[hcg] = P.emit("vector", lambda v, scr=scr, hc=hc:
                                             v.tensor_sub(mdl[:, hc, :], scr[:], mdh[:, hc, :]), inc="SV")
                    if hc == 3:
                        done_unit(i, ua)
                mid_tok = P.cnt["SV"]
                if i == 0 and hg < 3:
                    q_piece(1 + hg * 2)
                if dbg and i == 0 and hg == 0:
                    dump("mid0h", mdh[:].rearrange("p a b -> p (a b)"), [("SV", mid_tok)], blockers=("scalar", "vector"))
                    dump("mid0l", mdl[:].rearrange("p a b -> p (a b)"), [("SV", mid_tok)], blockers=("scalar", "vector"))
                if hg == 3:
                    prev_mlp1_tok = P.cnt["ST"]
                    if i + 1 < NT:
                        emit_p0_stats(i + 1)
                        emit_p1_bcast(i + 1)
                done_unit(i, ub)
                uc, ud = 10 + hg * 4, 11 + hg * 4
                use_unit(i, uc)
                use_unit(i, ud)
                w2ah, w2al = unit_views(i, uc)
                w2bh, w2bl = unit_views(i, ud)
                if hg == 3:
                    use_unit(i, 24)
                for fc in range(8):
                    uvh, uvl = (w2ah, w2al) if fc < 4 else (w2bh, w2bl)
                    bk = "A" if fc % 2 == 0 else "B"
                    mms = mm3(uvh, uvl, ((fc % 4) * 128, (fc % 4 + 1) * 128), mdh, mdl)
                    if hg == 3:
                        for kc in range(KC):
                            mms.append(((lambda wqv=wqv, kc=kc, fc=fc: wqv[:, kc, fc * 128:(fc + 1) * 128]),
                                        (lambda kc=kc, qAh=qAh: qAh[:, kc, :])))
                            mms.append(((lambda wqv=wqv, kc=kc, fc=fc: wqv[:, kc, fc * 128:(fc + 1) * 128]),
                                        (lambda kc=kc, qAl=qAl: qAl[:, kc, :])))
                        mms.append(((lambda fc=fc: bt_s[:, fc, :]), (lambda: r_s[:])))
                    eg = [("SV", mid_tok)] if fc == 0 else []
                    if hg == 3 and fc == 0:
                        eg += [("SV", xnp_tok), ("SV", rs_tok)]
                    tok = mm_group(bk, mms, extra_gate=eg or None)
                    if hg == 0:
                        evac("vector", bk, lambda v, bk=bk, fc=fc: v.tensor_copy(
                            acc[:, fc, :], banks[bk][:]), tok)
                    else:
                        evac("vector", bk, lambda v, bk=bk, fc=fc: v.tensor_add(
                            acc[:, fc, :], acc[:, fc, :], banks[bk][:]), tok)
                    if fc == 3:
                        done_unit(i, uc)
                    if hg == 3 and i + 1 < NT:
                        p1_weave(i + 1, fc)
                done_unit(i, ud)
                if i == 0:
                    if hg < 2:
                        q_piece(2 + hg * 2)
                    elif hg == 2:
                        q_piece(6)
                        q_piece(7)
                        xnp_tok = q_out_and_xns()
                if hg == 2 and i + 1 < NT:
                    emit_p0_squares(i + 1)
                if hg == 3:
                    prev_ffn_tok = P.cnt["ST"]
            done_unit(i, 24)
            if i + 1 < NT:
                emit_p1_tail(i + 1)

            if dbg and i == 0:
                dump("acc", acc[:].rearrange("p a b -> p (a b)"), [("SV", P.cnt["SV"])], blockers=("vector",))
                dump("w10h", wsp_hi[10], [("SV", P.cnt["SV"])])
                dump("w10l", wsp_lo[10], [("SV", P.cnt["SV"])])
                dump("w11h", wsp_hi[11], [("SV", P.cnt["SV"])])
            out_gate = []
            for fc in range(KC):
                P.emit("vector", lambda v, fc=fc: v.scalar_tensor_tensor(
                    p2[:, fc, :], acc[:, fc, :], cb_s[:, fc:fc + 1], p2[:, fc, :], ALU.add, ALU.add),
                       inc="SV")
                out_gate.append(P.cnt["SV"])
            if dbg and i == 0:
                dump("final", p2[:].rearrange("p a b -> p (a b)"), [("SV", out_gate[-1])], blockers=("vector",))
            for half in range(2):
                P.wait("gpsimd", "SV", out_gate[4 * half + 3])
                P.emit("gpsimd", lambda g, i=i, half=half: g.dma_start(
                    out=out_tl[i][:, half * 2048:(half + 1) * 2048],
                    in_=p2[:, 4 * half:4 * half + 4, :].rearrange("p a b -> p (a b)")), inc="SO", n=16)
            prev_out_so = P.cnt["SO"]

        P.wait("gpsimd", "SO", P.cnt["SO"])

        def runner(eng_name):
            drain_incs = eng_name in ("tensor", "vector", "scalar", "gpsimd")

            def body(eng):
                for op in P.ops[eng_name]:
                    if op[0] == "w":
                        eng.wait_ge(sems[op[1]], op[2])
                    elif op[0] == "d":
                        eng.drain()
                    else:
                        _, fn, inc, n = op
                        instr = fn(eng)
                        if inc is not None and drain_incs and n != 16:
                            instr = eng.drain()
                        if inc is not None:
                            instr.then_inc(sems[inc], n)
            return body

        with nc.Block() as block:
            block.sync(runner("sync"))
            block.gpsimd(runner("gpsimd"))
            block.tensor(runner("tensor"))
            block.vector(runner("vector"))
            block.scalar(runner("scalar"))

    if dbg:
        return nc, dbg_names
    return nc


def _block_unit(w, k0, k1, f0, f1):
    v = w[k0:k1, f0:f1]
    return np.ascontiguousarray(v.reshape((k1 - k0) // 128, 128, f1 - f0).transpose(1, 0, 2)).reshape(128, 4096)


def _col_block(v):
    return np.ascontiguousarray(np.asarray(v, np.float32).reshape(-1, 128).T)


def make_hstream(gw, sw, ow, w1, w2):
    units = []
    for q in range(4):
        units.append(_block_unit(gw, 0, 1024, q * 512, (q + 1) * 512))
    for h in range(2):
        units.append(_block_unit(sw, 0, 1024, h * 512, (h + 1) * 512))
    for h in range(2):
        units.append(_block_unit(ow, 0, 1024, h * 512, (h + 1) * 512))
    for hg in range(4):
        units.append(_block_unit(w1, 0, 1024, hg * 1024, hg * 1024 + 512))
        units.append(_block_unit(w1, 0, 1024, hg * 1024 + 512, hg * 1024 + 1024))
        units.append(_block_unit(w2, hg * 1024, (hg + 1) * 1024, 0, 512))
        units.append(_block_unit(w2, hg * 1024, (hg + 1) * 1024, 512, 1024))
    return np.ascontiguousarray(np.stack(units))


_NC_CACHE = [None]


def kernel(L0_attn_x, L0_attn_norm1_w, L0_attn_norm1_b, L0_attn_gate_w, L0_attn_gate_b,
           L0_attn_state_w, L0_attn_state_b, L0_attn_out_w, L0_attn_out_b,
           L0_attn_norm2_w, L0_attn_norm2_b, L0_attn_mlp_w1, L0_attn_mlp_b1,
           L0_attn_mlp_w2, L0_attn_mlp_b2, L0_ffn_kernel, L0_ffn_bias, L0_ffn_a, L0_ffn_b):
    f = np.float32
    x = np.ascontiguousarray(L0_attn_x, dtype=f)
    hstream = make_hstream(np.asarray(L0_attn_gate_w, f), np.asarray(L0_attn_state_w, f),
                           np.asarray(L0_attn_out_w, f), np.asarray(L0_attn_mlp_w1, f),
                           np.asarray(L0_attn_mlp_w2, f))
    wk = np.asarray(L0_ffn_kernel, f)
    consts = {
        "hstream": hstream,
        "wk_blk": np.ascontiguousarray(
            wk.reshape(8, 128, 1024).transpose(1, 0, 2).reshape(128, 8192)),
        "gate_b_t": _col_block(L0_attn_gate_b),
        "state_b_t": _col_block(L0_attn_state_b),
        "out_b_t": _col_block(L0_attn_out_b),
        "mlp_b1_t": _col_block(L0_attn_mlp_b1),
        "mlp_b2_t": _col_block(L0_attn_mlp_b2),
        "ffn_b_t": _col_block(L0_ffn_bias),
        "n1w_t": _col_block(L0_attn_norm1_w),
        "n1b_t": _col_block(L0_attn_norm1_b),
        "n2w_t": _col_block(L0_attn_norm2_w),
        "n2b_t": _col_block(L0_attn_norm2_b),
        "a_t": np.ascontiguousarray(
            np.asarray(L0_ffn_a, f).reshape(8, 128, 16).transpose(1, 0, 2).reshape(128, 128)),
        "b_t": np.ascontiguousarray(np.asarray(L0_ffn_b, f)),
    }

    in_maps = []
    for c in range(N_CORES):
        xc = x[2 * c:2 * c + 2].reshape(4096, 1024)
        x_tl = np.ascontiguousarray(
            xc.reshape(8, 512, 8, 128).transpose(0, 3, 2, 1).reshape(NT, 128, KC * T))
        m = dict(consts)
        m["x_tl"] = x_tl
        in_maps.append(m)

    if _NC_CACHE[0] is None:
        _NC_CACHE[0] = build_nc()
    res = run_bass_kernel_spmd(_NC_CACHE[0], in_maps, core_ids=list(range(N_CORES)))

    outs = []
    for c in range(N_CORES):
        ot = np.asarray(res.results[c]["out_tl"])
        oc = ot.reshape(8, 128, 8, 512).transpose(0, 3, 2, 1).reshape(4096, 1024)
        outs.append(oc.reshape(2, 2048, 1024))
    return np.ascontiguousarray(np.concatenate(outs, axis=0))
